# version 03
# baseline (speedup 1.0000x reference)
import numpy as np
import concourse.bass as bass
import concourse.mybir as mybir
from concourse.bass_utils import run_bass_kernel_spmd

F32 = mybir.dt.float32
AF = mybir.ActivationFunctionType
ALU = mybir.AluOpType
AX = mybir.AxisListType.X

D = 2048
H = 8
DH = 128
KEY = 1024
NE = 6
R = 16
HALO = 3
C = 128
CT = C // 2
NT = D // 128
NQ = KEY // 128
SM = R + NE + NE * R
EPS_LN = float(np.float32(1e-5))
EPS_RMS = float(np.float32(1e-5))
INV_D = float(np.float32(1.0) / np.float32(D))
INV_DH = float(np.float32(1.0) / np.float32(DH))
INV_SQRT_DH = float(np.float32(1.0) / np.float32(np.sqrt(np.float32(128.0))))


def spans(width, maxw=512):
    out, o = [], 0
    while o < width:
        w = min(maxw, width - o)
        out.append((o, w))
        o += w
    return out


def build_program(n_cores, tpc, dbg=0):
    TW = tpc + HALO
    NCH = tpc // CT
    NTK = (TW + 127) // 128
    nc = bass.Bass("TRN2", target_bir_lowering=False)

    x_fm = nc.dram_tensor("x_fm", [D, TW], F32, kind="ExternalInput").ap()
    wq_t = nc.dram_tensor("wq_t", [D, KEY], F32, kind="ExternalInput").ap()
    wk_t = nc.dram_tensor("wk_t", [D, D], F32, kind="ExternalInput").ap()
    wv_t = nc.dram_tensor("wv_t", [D, D], F32, kind="ExternalInput").ap()
    wbwa_t = nc.dram_tensor("wbwa_t", [D, 3 * H], F32, kind="ExternalInput").ap()
    wg_t = nc.dram_tensor("wg_t", [D, KEY], F32, kind="ExternalInput").ap()
    wo_t = nc.dram_tensor("wo_t", [KEY, D], F32, kind="ExternalInput").ap()
    a_log = nc.dram_tensor("a_log", [1, H], F32, kind="ExternalInput").ap()
    dt_b = nc.dram_tensor("dt_b", [1, H], F32, kind="ExternalInput").ap()
    conv_q = nc.dram_tensor("conv_q", [KEY, 4], F32, kind="ExternalInput").ap()
    conv_k = nc.dram_tensor("conv_k", [D, 4], F32, kind="ExternalInput").ap()
    conv_v = nc.dram_tensor("conv_v", [D, 4], F32, kind="ExternalInput").ap()
    onw = nc.dram_tensor("onw", [1, DH], F32, kind="ExternalInput").ap()
    w_base_t = nc.dram_tensor("w_base_t", [D, D], F32, kind="ExternalInput").ap()
    b_base_c = nc.dram_tensor("b_base_c", [128, NT], F32, kind="ExternalInput").ap()
    w_small_t = nc.dram_tensor("w_small_t", [D, SM], F32, kind="ExternalInput").ap()
    b_rout = nc.dram_tensor("b_rout", [1, NE], F32, kind="ExternalInput").ap()
    b_s_w = nc.dram_tensor("b_s_w", [R, D], F32, kind="ExternalInput").ap()
    b_r_w = nc.dram_tensor("b_r_w", [NE * R, D], F32, kind="ExternalInput").ap()
    csel = nc.dram_tensor("csel", [1, 8], F32, kind="ExternalInput").ap()
    hmask = nc.dram_tensor("hmask", [1, 1], F32, kind="ExternalInput").ap()
    out_fm = nc.dram_tensor("out_fm", [D, tpc], F32, kind="ExternalOutput").ap()

    x1_sp = nc.dram_tensor("x1_sp", [D, TW], F32, kind="Internal").ap()
    k_sp = nc.dram_tensor("k_sp", [D, TW], F32, kind="Internal").ap()
    v_sp = nc.dram_tensor("v_sp", [D, TW], F32, kind="Internal").ap()
    q_sp = nc.dram_tensor("q_sp", [KEY, TW], F32, kind="Internal").ap()
    gg_sp = nc.dram_tensor("gg_sp", [KEY, tpc], F32, kind="Internal").ap()
    cc_in = [nc.dram_tensor(f"cc_in{h}", [128, 2 * DH], F32, kind="Internal").ap() for h in range(H)]
    cc_out = [nc.dram_tensor(f"cc_out{h}", [n_cores, 128, 2 * DH], F32, kind="Internal",
                             addr_space="Local").ap() for h in range(H)]

    DBG = {}
    if dbg:
        def dout(name, shape):
            DBG[name] = nc.dram_tensor(name, shape, F32, kind="ExternalOutput").ap()
        dout("d_x1", [D, TW]); dout("d_xln", [D, TW])
        dout("d_k", [D, TW]); dout("d_v", [D, TW]); dout("d_q", [KEY, TW])
        dout("d_b", [2 * H, TW]); dout("d_g", [H, TW]); dout("d_cum", [H, TW])
        dout("d_kint", [DH, C]); dout("d_ktm", [C, DH]); dout("d_rhs", [C, 2 * DH])
        dout("d_gst", [C, C]); dout("d_art", [128, 4 * DH]); dout("d_bup", [C, C]); dout("d_alo", [C, C]); dout("d_kbf", [DH, C]); dout("d_kbt", [C, DH])
        dout("d_sin", [DH, DH]); dout("d_och", [C, DH]); dout("d_ofm", [KEY, tpc])
        dout("d_x2", [D, tpc])

    guards = []

    def sb(name, shape, dtype=F32):
        g = nc.sbuf_tensor(name, shape, dtype); t = g.__enter__(); guards.append(g); return t

    def psu(name, shape, dtype=F32):
        g = nc.psum_tensor(name, shape, dtype); t = g.__enter__(); guards.append(g); return t

    big0 = sb("big0", [128, NT * TW])
    big1 = sb("big1", [128, NT * TW])
    work = sb("work", [128, 2 * 128])
    wst = sb("wst", [128, 2 * 1280])
    bbase = sb("bbase", [128, NT])
    consts = sb("consts", [128, 1280])
    crow1 = sb("crow1", [1, 280])
    SELN = max(NTK, H)
    selb = sb("selb", [SELN, SELN * 128])
    sel127 = sb("sel127", [128, 128])
    stepcs = sb("stepcs", [128, NCH * 40])
    rowcs = sb("rowcs", [H, NCH * 128])
    accb = sb("accb", [128, NCH * 8])
    nrmc = sb("nrmc", [128, 8])
    selbc = sb("selbc", [128, 8])
    onrm = sb("onrm", [128, DH])
    poolh = sb("poolh", [128, 7400])
    pool = poolh[:, :]
    smw = pool[:, 0:NT * SM]
    sfm = pool[0:R, 1888:1888 + TW]
    rfm = pool[0:NE, 2915:2915 + TW]
    afm = pool[0:NE * R, 3942:3942 + TW]
    hwb = pool[0:NE * R, 4969:4969 + TW]
    wfm = pool[0:NE, 5996:5996 + TW]
    statc = pool[:, 7023:7023 + 5 * NTK]
    statf = pool[0:NTK, 7078:7078 + 256]
    gtm = pool[:, 7338:7338 + 32]
    bgw = pool[:, 0:NT * 24]
    cvw = pool[:, 384:384 + (2 * NT + NQ) * 4]
    bgb = pool[0:2 * H, 544:544 + TW]
    gb = pool[0:H, 1571:1571 + TW]
    cumb = pool[0:H, 2598:2598 + TW]
    kint = pool[:, 0:2 * C]
    qint = pool[:, 256:256 + 2 * C]
    khf = pool[:, 512:512 + 2 * C]
    qhf = pool[:, 768:768 + 2 * C]
    ktm = pool[:, 1024:1024 + 2 * DH]
    kbtm = pool[:, 1280:1280 + 2 * DH]
    rhsb = pool[:, 1536:1536 + 2 * 256]
    apw = pool[:, 2048:2048 + 2 * C]
    bpw = pool[:, 2304:2304 + 7 * C]
    dxp = pool[:, 3200:3200 + C]
    gst = pool[:, 3328:3328 + C]
    kbar = pool[:, 3456:3456 + DH]
    qtl = pool[:, 3584:3584 + C]
    scr = pool[:, 3712:3712 + 6 * C]
    comp = pool[:, 4480:4480 + 3 * DH]
    ccsb = pool[:, 4864:4864 + 2 * DH]
    gsb = pool[:, 5120:5120 + 2 * 256]
    bscr = pool[:, 5632:5632 + 4 * DH]
    prefx = pool[:, 6144:6144 + DH]
    stt = pool[:, 6272:6272 + DH]
    ggch = pool[:, 6400:6400 + 2 * CT]
    och = pool[:, 6528:6528 + 2 * DH]
    qatm = pool[:, 6784:6784 + DH]
    kbfm = pool[:, 6912:6912 + C]
    cwork = big0

    ARTS = big0

    def art(h, c, slot):
        return ARTS[:, (h % 2) * (NCH * 4 * DH) + c * 4 * DH + slot * DH:
                    (h % 2) * (NCH * 4 * DH) + c * 4 * DH + (slot + 1) * DH]

    def khv(h, part):
        b = ((h % 2) * 2 + part) * TW
        return big1[:, b:b + TW]

    def qhv(h):
        b = 4 * TW + (h % 2) * TW
        return big1[:, b:b + TW]

    def vchv(c, part):
        b = 6 * TW + ((c % 2) * 2 + part) * CT
        return big1[:, b:b + CT]

    OFM_OFF = 8 * TW

    def ofmv(h):
        return big1[:, OFM_OFF + h * tpc:OFM_OFF + (h + 1) * tpc]

    pLN = psu("pLN", [128, 512])
    pBC = psu("pBC", [128, 512])
    pFF = psu("pFF", [128, 512])
    pFF2 = psu("pFF2", [128, 512])
    pRW = psu("pRW", [128, 512])
    pSQ = psu("pSQ", [128, 512])
    pAP = psu("pAP", [128, 512])
    pMP = psu("pMP", [128, 512])

    sems, cnt, ev = {}, {}, {}
    for s in ["qS", "qA", "qG", "tT", "tV", "tS", "tG", "cc"]:
        g = nc.semaphore(s); sems[s] = g.__enter__(); guards.append(g); cnt[s] = 0

    def inc(s, by=1):
        cnt[s] += by
        return cnt[s]

    def mark(name, s):
        ev[name] = (s, cnt[s])

    def wt(eng, name):
        s, v = ev[name]
        if v > 0:
            eng.wait_ge(sems[s], v)

    def barrier(tagb):
        vals = [(s, cnt[s]) for s in ["tT", "tV", "tS", "tG", "qS", "qA", "qG", "cc"] if cnt[s] > 0]

        @blk.tensor
        def _(e):
            for s, v in vals:
                e.wait_ge(sems[s], v)
            e.matmul(pLN[0:1, 40:41], ONESC[0:1, 0:1], ONESC[0:1, 0:1], start=True, stop=True).then_inc(sems["tT"], 1)
        inc("tT")
        mark(f"bar_{tagb}_t", "tT")

        @blk.vector
        def _(e):
            for s, v in vals:
                e.wait_ge(sems[s], v)
            e.sem_inc(sems["tV"], 1)
        inc("tV")

        @blk.scalar
        def _(e):
            for s, v in vals:
                e.wait_ge(sems[s], v)
            e.sem_inc(sems["tS"], 1)
        inc("tS")

        @blk.sync
        def _(e):
            for s, v in vals:
                e.wait_ge(sems[s], v)

    def gate(eng, qn):
        if cnt[qn] > 0:
            eng.wait_ge(sems[qn], cnt[qn])

    blk = nc.Block().__enter__()
    guards.append(blk)

    I128 = consts[:, 0:128]
    E_evT = consts[0:64, 128:256]
    E_odT = consts[0:64, 720:848]
    E_boT = consts[0:64, 256:384]
    E6 = consts[0:NE, 448:544]
    NEXPA = consts[0:H, 560:561]
    DTC = consts[0:H, 561:562]
    BRT = consts[0:128, 576:582]
    ONESC = consts[:, 640:641]
    EPSC = consts[:, 648:649]
    INVDHC = consts[:, 649:650]
    HMC = consts[:, 652:653]
    ONES_T = consts[0:H, 656:656 + CT]
    MSU = consts[:, 896:1024]
    MSL = consts[:, 1024:1152]
    MGU = consts[:, 1152:1280]
    ONESR = crow1[0:1, 0:128]

    @blk.sync
    def _(e):
        gate(e, "qS")
        for f in range(NT):
            e.dma_start(out=big0[:, f * TW:(f + 1) * TW], in_=x_fm[f * 128:(f + 1) * 128, :]).then_inc(sems["qS"], 16)
        inc("qS", 16 * NT)
        mark("x_loaded", "qS")

    @blk.scalar
    def _(e):
        n = 0
        e.wait_ge(sems["tG"], 1)
        gate(e, "qA")
        for f in range(NT):
            e.dma_start(out=smw[:, f * SM:(f + 1) * SM], in_=w_small_t[f * 128:(f + 1) * 128, :]).then_inc(sems["qA"], 16); n += 1

        e.dma_start(out=bbase[:, :], in_=b_base_c).then_inc(sems["qA"], 16); n += 1
        e.dma_start(out=crow1[0:1, 256:256 + NE], in_=b_rout).then_inc(sems["qA"], 16); n += 1
        e.dma_start(out=consts[0:1, 544:544 + H], in_=a_log).then_inc(sems["qA"], 16); n += 1
        e.dma_start(out=consts[0:1, 552:552 + H], in_=dt_b).then_inc(sems["qA"], 16); n += 1
        e.dma_start(out=crow1[0:1, 128:128 + DH], in_=onw).then_inc(sems["qA"], 16); n += 1
        e.dma_start(out=crow1[0:1, 262:270], in_=csel).then_inc(sems["qA"], 16); n += 1
        e.dma_start(out=crow1[0:1, 270:271], in_=hmask).then_inc(sems["qA"], 16); n += 1

        inc("qA", 16 * n)
        mark("smalls_loaded", "qA")

    @blk.gpsimd
    def _(e):
        e.memset(consts[:, 0:544], 1.0)
        e.memset(consts[:, 544:640], 0.0)
        e.memset(consts[:, 640:1280], 1.0)
        e.memset(crow1[0:1, 0:128], 1.0)
        e.memset(selb[:, :], 1.0)
        e.memset(sel127[:, :], 1.0)
        e.memset(poolh[:, :], 0.0)
        e.memset(work[:, :], 0.0)
        e.drain()
        e.memset(EPSC, EPS_LN)
        e.memset(INVDHC, INV_DH)
        e.drain()
        e.affine_select(I128, I128, pattern=[[1, 128]], compare_op=ALU.is_equal, fill=0.0, base=0, channel_multiplier=-1)
        e.memset(E_odT, 1.0)
        e.drain()
        e.affine_select(E_evT, E_evT, pattern=[[1, 128]], compare_op=ALU.is_equal, fill=0.0, base=0, channel_multiplier=-2)
        e.affine_select(E_odT, E_odT, pattern=[[1, 128]], compare_op=ALU.is_equal, fill=0.0, base=-1, channel_multiplier=-2)
        e.affine_select(E_boT, E_boT, pattern=[[1, 128]], compare_op=ALU.is_ge, fill=0.0, base=0, channel_multiplier=-2)
        e.affine_select(E6, E6, pattern=[[1, 96]], compare_op=ALU.is_ge, fill=0.0, base=0, channel_multiplier=-16)
        e.affine_select(MSU, MSU, pattern=[[1, 128]], compare_op=ALU.is_ge, fill=0.0, base=-1, channel_multiplier=-1)
        e.affine_select(MSL, MSL, pattern=[[-1, 128]], compare_op=ALU.is_ge, fill=0.0, base=-1, channel_multiplier=1)
        e.affine_select(MGU, MGU, pattern=[[1, 128]], compare_op=ALU.is_ge, fill=0.0, base=0, channel_multiplier=-1)
        e.affine_select(selb[:, :], selb[:, :], pattern=[[1, SELN * 128]], compare_op=ALU.is_ge, fill=0.0, base=0, channel_multiplier=-128)
        e.drain()
        e.affine_select(selb[:, :], selb[:, :], pattern=[[-1, SELN * 128]], compare_op=ALU.is_ge, fill=0.0, base=127, channel_multiplier=128)
        e.affine_select(sel127[:, :], sel127[:, :], pattern=[[0, 128]], compare_op=ALU.is_equal, fill=0.0, base=-127, channel_multiplier=1)
        e.drain()
        e.affine_select(E_boT, E_boT, pattern=[[-1, 128]], compare_op=ALU.is_ge, fill=0.0, base=1, channel_multiplier=2)
        e.affine_select(E6, E6, pattern=[[-1, 96]], compare_op=ALU.is_ge, fill=0.0, base=15, channel_multiplier=16)
        e.drain()
        e.sem_inc(sems["tG"], 1)
        inc("tG")
        mark("consts_ready", "tG")

    @blk.tensor
    def _(e):
        wt(e, "smalls_loaded")
        wt(e, "consts_ready")
        e.matmul(pBC[:, 0:NE], ONESR, crow1[0:1, 256:256 + NE], start=True, stop=True)
        e.matmul(pBC[:, 384:392], ONESR, crow1[0:1, 262:270], start=True, stop=True)
        e.matmul(pBC[:, 392:393], ONESR, crow1[0:1, 270:271], start=True, stop=True)
        e.transpose(pBC[0:DH, 264:265], crow1[0:1, 128:128 + DH], I128[0:1, 0:1])
        e.transpose(pBC[0:H, 256:257], consts[0:1, 544:544 + H], I128[0:1, 0:1])
        e.transpose(pBC[0:H, 260:261], consts[0:1, 552:552 + H], I128[0:1, 0:1]).then_inc(sems["tT"], 1)
        inc("tT")
        mark("tiny_tr", "tT")

    @blk.vector
    def _(e):
        wt(e, "tiny_tr")
        e.tensor_copy(BRT, pBC[:, 0:NE])
        e.tensor_copy(onrm[:, 0:1], pBC[0:DH, 264:265])
        e.tensor_copy(selbc[:, :], pBC[:, 384:392])
        e.tensor_copy(HMC, pBC[:, 392:393])
        e.tensor_copy(DTC, pBC[0:H, 260:261])
        e.drain()
        e.sem_inc(sems["tV"], 1)
        inc("tV")
        mark("tiny_v", "tV")

    @blk.scalar
    def _(e):
        wt(e, "tiny_v")
        e.activation(NEXPA, pBC[0:H, 256:257], AF.Exp)
        e.drain()
        e.sem_inc(sems["tS"], 1)
        inc("tS")
        mark("tiny_s", "tS")

    @blk.vector
    def _(e):
        wt(e, "tiny_s")
        e.tensor_scalar(NEXPA, NEXPA, -1.0, None, ALU.mult)
        e.drain()
        e.sem_inc(sems["tV"], 1)
        inc("tV")
        mark("consts2", "tV")

    SEQ = {"ln": 0}

    def emit_ln(src, dst, width, ready_evs):
        tch = spans(width, 128)
        tag = f"ln{SEQ['ln']}"; SEQ["ln"] += 1
        ncol = len(tch)

        for ci, (o, w) in enumerate(tch):
            for f in range(NT):
                @blk.scalar
                def _(e, ci=ci, o=o, w=w, f=f):
                    for evn in ready_evs:
                        wt(e, evn)
                    k = ci * NT + f
                    if k >= 2:
                        pci, pf = divmod(k - 2, NT)
                        wt(e, f"{tag}_sq{pci}_{pf}")
                    e.activation(work[:, (f % 2) * 128:(f % 2) * 128 + w],
                                 src[:, f * width + o:f * width + o + w], AF.Square)
                    e.drain()
                    e.sem_inc(sems["tS"], 1)
                    inc("tS")
                    mark(f"{tag}_sqs{ci}_{f}", "tS")

                @blk.tensor
                def _(e, ci=ci, o=o, w=w, f=f):
                    wt(e, f"{tag}_sqs{ci}_{f}")
                    if ci == 0 and f == 0:
                        wt(e, "consts2")
                    e.matmul(pLN[0:w, ci:ci + 1], src[:, f * width + o:f * width + o + w], ONESC,
                             start=(f == 0), stop=(f == NT - 1))
                    e.matmul(pBC[0:w, ci:ci + 1], work[:, (f % 2) * 128:(f % 2) * 128 + w], ONESC,
                             start=(f == 0), stop=(f == NT - 1)).then_inc(sems["tT"], 1)
                    inc("tT")
                    mark(f"{tag}_sq{ci}_{f}", "tT")

        @blk.vector
        def _(e):
            wt(e, f"{tag}_sq{len(tch) - 1}_{NT - 1}")
            for ci, (o, w) in enumerate(tch):
                e.tensor_scalar(statc[0:w, ci:ci + 1], pLN[0:w, ci:ci + 1], INV_D, None, ALU.mult)
                e.tensor_scalar(statc[0:w, NTK + ci:NTK + ci + 1], pBC[0:w, ci:ci + 1], INV_D, None, ALU.mult)
            e.drain()
            e.tensor_tensor(statc[:, 2 * NTK:2 * NTK + ncol], statc[:, 0:ncol], statc[:, 0:ncol], ALU.mult)
            e.drain()
            e.tensor_sub(statc[:, NTK:NTK + ncol], statc[:, NTK:NTK + ncol], statc[:, 2 * NTK:2 * NTK + ncol])
            e.drain()
            e.sem_inc(sems["tV"], 1)
            inc("tV")
            mark(f"{tag}_var", "tV")

        @blk.scalar
        def _(e):
            wt(e, f"{tag}_var")
            e.activation(statc[:, 2 * NTK:2 * NTK + ncol], statc[:, NTK:NTK + ncol], AF.Sqrt, bias=EPSC)
            e.drain()
            e.sem_inc(sems["tS"], 1)
            inc("tS")
            mark(f"{tag}_inv", "tS")

        @blk.vector
        def _(e):
            wt(e, f"{tag}_inv")
            e.reciprocal(statc[:, 2 * NTK:2 * NTK + ncol], statc[:, 2 * NTK:2 * NTK + ncol])
            e.drain()
            e.tensor_tensor(statc[:, 3 * NTK:3 * NTK + ncol], statc[:, 0:ncol],
                            statc[:, 2 * NTK:2 * NTK + ncol], ALU.mult)
            e.drain()
            e.sem_inc(sems["tV"], 1)
            inc("tV")
            mark(f"{tag}_minv", "tV")

        @blk.tensor
        def _(e):
            wt(e, f"{tag}_minv")
            e.transpose(pLN[0:ncol, 128:256], statc[:, 2 * NTK:2 * NTK + ncol], I128)
            e.transpose(pLN[0:ncol, 256:384], statc[:, 3 * NTK:3 * NTK + ncol], I128).then_inc(sems["tT"], 1)
            inc("tT")
            mark(f"{tag}_strows", "tT")

        @blk.vector
        def _(e):
            wt(e, f"{tag}_strows")
            e.tensor_copy(statf[0:ncol, 0:128], pLN[0:ncol, 128:256])
            e.tensor_copy(statf[0:ncol, 128:256], pLN[0:ncol, 256:384])
            e.drain()
            e.sem_inc(sems["tV"], 1)
            inc("tV")
            mark(f"{tag}_stf", "tV")

        for ci, (o, w) in enumerate(tch):
            @blk.tensor
            def _(e, ci=ci, o=o, w=w):
                wt(e, f"{tag}_stf")
                if ci >= 2:
                    wt(e, f"{tag}_ap{ci - 2}")
                pbk = pBC if ci % 2 == 0 else pLN
                e.matmul(pbk[:, 0:w], selb[0:NTK, ci * 128:ci * 128 + 128], statf[0:NTK, 0:w], start=True, stop=True)
                e.matmul(pbk[:, 128:128 + w], selb[0:NTK, ci * 128:ci * 128 + 128], statf[0:NTK, 128:128 + w],
                         start=True, stop=True).then_inc(sems["tT"], 1)
                inc("tT")
                mark(f"{tag}_bc{ci}", "tT")

            @blk.vector
            def _(e, ci=ci, o=o, w=w):
                wt(e, f"{tag}_bc{ci}")
                pbk = pBC if ci % 2 == 0 else pLN
                for f in range(NT):
                    e.tensor_tensor(dst[:, f * width + o:f * width + o + w],
                                    src[:, f * width + o:f * width + o + w],
                                    pbk[:, 0:w], ALU.mult)
                e.drain()
                for f in range(NT):
                    e.tensor_sub(dst[:, f * width + o:f * width + o + w],
                                 dst[:, f * width + o:f * width + o + w],
                                 pbk[:, 128:128 + w])
                e.drain()
                e.sem_inc(sems["tV"], 1)
                inc("tV")
                mark(f"{tag}_ap{ci}", "tV")

        return f"{tag}_ap{len(tch) - 1}"

    def emit_ffn(xarena, starena, width, ready_evs, tag):
        tch = spans(width, 128)
        sp512 = spans(width, 512)

        for si, (o, w) in enumerate(sp512):
            @blk.tensor
            def _(e, si=si, o=o, w=w):
                for evn in ready_evs:
                    wt(e, evn)
                wt(e, "consts2")
                if si > 0:
                    wt(e, f"{tag}_spe{si - 1}")
                for f in range(NT):
                    e.matmul(pFF[0:R, 0:w], smw[:, f * SM:f * SM + R],
                             xarena[:, f * width + o:f * width + o + w],
                             start=(f == 0), stop=(f == NT - 1))
                for f in range(NT):
                    e.matmul(pLN[0:NE, 0:w], smw[:, f * SM + R:f * SM + R + NE],
                             xarena[:, f * width + o:f * width + o + w],
                             start=(f == 0), stop=(f == NT - 1))
                for f in range(NT):
                    mm = e.matmul(pFF2[0:NE * R, 0:w], smw[:, f * SM + R + NE:(f + 1) * SM],
                                  xarena[:, f * width + o:f * width + o + w],
                                  start=(f == 0), stop=(f == NT - 1))
                mm.then_inc(sems["tT"], 1)
                inc("tT")
                mark(f"{tag}_sp{si}", "tT")

            @blk.vector
            def _(e, si=si, o=o, w=w):
                wt(e, f"{tag}_sp{si}")
                e.tensor_copy(sfm[:, o:o + w], pFF[0:R, 0:w])
                e.tensor_copy(rfm[:, o:o + w], pLN[0:NE, 0:w])
                e.tensor_copy(afm[:, o:o + w], pFF2[0:NE * R, 0:w])
                e.drain()
                e.sem_inc(sems["tV"], 1)
                inc("tV")
                mark(f"{tag}_spe{si}", "tV")

        for ci, (o, w) in enumerate(tch):
            @blk.tensor
            def _(e, ci=ci, o=o, w=w):
                wt(e, f"{tag}_spe{len(sp512) - 1}")
                if ci > 0:
                    wt(e, f"{tag}_wf{ci - 1}")
                e.transpose(pBC[0:w, 0:NE], rfm[0:NE, o:o + w], I128[0:NE, 0:NE]).then_inc(sems["tT"], 1)
                inc("tT")
                mark(f"{tag}_rt{ci}", "tT")

            @blk.vector
            def _(e, ci=ci, o=o, w=w):
                wt(e, f"{tag}_rt{ci}")
                e.tensor_tensor(gtm[0:w, 0:NE], pBC[0:w, 0:NE], BRT[0:w, :], ALU.add)
                e.drain()
                e.tensor_reduce(gtm[0:w, 8:9], gtm[0:w, 0:NE], AX, ALU.max, negate=True)
                e.drain()
                e.sem_inc(sems["tV"], 1)
                inc("tV")
                mark(f"{tag}_z{ci}", "tV")

            @blk.scalar
            def _(e, ci=ci, o=o, w=w):
                wt(e, f"{tag}_z{ci}")
                e.activation(gtm[0:w, 16:16 + NE], gtm[0:w, 0:NE], AF.Exp, bias=gtm[0:w, 8:9])
                e.drain()
                e.sem_inc(sems["tS"], 1)
                inc("tS")
                mark(f"{tag}_e{ci}", "tS")

            @blk.vector
            def _(e, ci=ci, o=o, w=w):
                wt(e, f"{tag}_e{ci}")
                eT = gtm[0:w, 16:16 + NE]
                e.tensor_reduce(gtm[0:w, 9:10], eT, AX, ALU.add)
                e.tensor_reduce(gtm[0:w, 10:11], eT, AX, ALU.max)
                e.drain()
                e.reciprocal(gtm[0:w, 11:12], gtm[0:w, 9:10])
                e.tensor_scalar(gtm[0:w, 24:24 + NE], eT, gtm[0:w, 10:11], None, ALU.is_ge)
                e.drain()
                e.scalar_tensor_tensor(gtm[0:w, 24:24 + NE], gtm[0:w, 24:24 + NE], -2.0, eT, ALU.mult, ALU.add)
                e.drain()
                e.tensor_reduce(gtm[0:w, 12:13], gtm[0:w, 24:24 + NE], AX, ALU.max)
                e.drain()
                e.tensor_scalar(gtm[0:w, 24:24 + NE], eT, gtm[0:w, 12:13], None, ALU.is_ge)
                e.drain()
                e.scalar_tensor_tensor(gtm[0:w, 0:NE], eT, gtm[0:w, 11:12], gtm[0:w, 24:24 + NE], ALU.mult, ALU.mult)
                e.drain()
                e.sem_inc(sems["tV"], 1)
                inc("tV")
                mark(f"{tag}_w{ci}", "tV")

            @blk.tensor
            def _(e, ci=ci, o=o, w=w):
                wt(e, f"{tag}_w{ci}")
                e.transpose(pBC[0:NE, 128:128 + w], gtm[0:w, 0:NE], I128[0:w, 0:w]).then_inc(sems["tT"], 1)
                inc("tT")
                mark(f"{tag}_wt{ci}", "tT")

            @blk.vector
            def _(e, ci=ci, o=o, w=w):
                wt(e, f"{tag}_wt{ci}")
                e.tensor_copy(wfm[:, o:o + w], pBC[0:NE, 128:128 + w])
                e.drain()
                e.sem_inc(sems["tV"], 1)
                inc("tV")
                mark(f"{tag}_wf{ci}", "tV")

        for si, (o, w) in enumerate(sp512):
            @blk.tensor
            def _(e, si=si, o=o, w=w):
                wt(e, f"{tag}_wf{len(tch) - 1}")
                if si > 0:
                    wt(e, f"{tag}_hwe{si - 1}")
                e.matmul(pFF[0:NE * R, 0:w], E6, wfm[:, o:o + w], start=True, stop=True).then_inc(sems["tT"], 1)
                inc("tT")
                mark(f"{tag}_hw{si}", "tT")

            @blk.vector
            def _(e, si=si, o=o, w=w):
                wt(e, f"{tag}_hw{si}")
                e.tensor_tensor(hwb[:, o:o + w], afm[:, o:o + w], pFF[0:NE * R, 0:w], ALU.mult)
                e.drain()
                e.sem_inc(sems["tV"], 1)
                inc("tV")
                mark(f"{tag}_hwe{si}", "tV")

        def wld_batch(e, fo, hv):
            qn = "qS" if (fo * 2 + hv) % 2 == 0 else "qG"
            gate(e, qn)
            slot = (fo * 2 + hv) % 2
            nb = 0
            for f8 in range(8):
                f = hv * 8 + f8
                e.dma_start(out=wst[:, slot * 1280 + f8 * 128:slot * 1280 + (f8 + 1) * 128],
                            in_=w_base_t[f * 128:(f + 1) * 128, fo * 128:(fo + 1) * 128]).then_inc(sems[qn], 16)
                nb += 1
            if hv == 1:
                e.dma_start(out=wst[0:R, slot * 1280 + 1024:slot * 1280 + 1152],
                            in_=b_s_w[:, fo * 128:(fo + 1) * 128]).then_inc(sems[qn], 16)
                e.dma_start(out=wst[0:NE * R, slot * 1280 + 1152:slot * 1280 + 1280],
                            in_=b_r_w[:, fo * 128:(fo + 1) * 128]).then_inc(sems[qn], 16)
                nb += 2
            inc(qn, 16 * nb)
            mark(f"{tag}_wld{fo}_{hv}", qn)

        @blk.sync
        def _(e):
            for evn in ready_evs:
                wt(e, evn)
            wld_batch(e, 0, 0)
            wld_batch(e, 0, 1)

        nsp = len(sp512)
        BANKS = [pFF, pFF2, pLN]
        BOFF = [0, 0, 64]

        for fo in range(NT):
            for hv in range(2):
                @blk.tensor
                def _(e, fo=fo, hv=hv):
                    wt(e, f"{tag}_wld{fo}_{hv}")
                    wt(e, f"{tag}_hwe{nsp - 1}")
                    if hv == 0 and fo >= 1:
                        for psi in range(nsp):
                            wt(e, f"{tag}_be{fo - 1}_{psi}")
                    slot = (fo * 2 + hv) % 2
                    last = None
                    for si, (o, w) in enumerate(sp512):
                        pb = BANKS[si]
                        po = BOFF[si]
                        for f8 in range(8):
                            last = e.matmul(pb[:, po:po + w], wst[:, slot * 1280 + f8 * 128:slot * 1280 + (f8 + 1) * 128],
                                            xarena[:, (hv * 8 + f8) * width + o:(hv * 8 + f8) * width + o + w],
                                            start=(hv == 0 and f8 == 0), stop=False)
                        if hv == 1:
                            e.matmul(pb[:, po:po + w], wst[0:R, slot * 1280 + 1024:slot * 1280 + 1152],
                                     sfm[:, o:o + w], start=False, stop=False)
                            last = e.matmul(pb[:, po:po + w], wst[0:NE * R, slot * 1280 + 1152:slot * 1280 + 1280],
                                            hwb[:, o:o + w], start=False, stop=True)
                    last.then_inc(sems["tT"], 1)
                    inc("tT")
                    mark(f"{tag}_b{fo}_{hv}", "tT")

                nxt = fo * 2 + hv + 2
                if nxt < NT * 2:
                    @blk.sync
                    def _(e, fo=fo, hv=hv, nxt=nxt):
                        wt(e, f"{tag}_b{fo}_{hv}")
                        wld_batch(e, nxt // 2, nxt % 2)

            for si, (o, w) in enumerate(sp512):
                @blk.vector
                def _(e, fo=fo, si=si, o=o, w=w):
                    wt(e, f"{tag}_b{fo}_{1}")
                    pb = BANKS[si]
                    po = BOFF[si]
                    e.tensor_scalar(scr[:, 0:w], pb[:, po:po + w], bbase[:, fo:fo + 1], 0.5, ALU.add, ALU.mult)
                    e.drain()
                    e.tensor_tensor(starena[:, fo * width + o:fo * width + o + w],
                                    starena[:, fo * width + o:fo * width + o + w], scr[:, 0:w], ALU.add)
                    e.drain()
                    e.sem_inc(sems["tV"], 1)
                    inc("tV")
                    mark(f"{tag}_be{fo}_{si}", "tV")

        return f"{tag}_be{NT - 1}_{nsp - 1}"

    ln0_done = emit_ln(big0, big1, TW, ["x_loaded"])
    ffn1_done = emit_ffn(big1, big0, TW, [ln0_done], "fA")

    if dbg:
        @blk.sync
        def _(e):
            wt(e, ffn1_done)
            gate(e, "qS")
            for f in range(NT):
                e.dma_start(out=DBG["d_x1"][f * 128:(f + 1) * 128, :], in_=big0[:, f * TW:(f + 1) * TW]).then_inc(sems["qS"], 16)
            inc("qS", 16 * NT)
            mark("dbg_x1", "qS")

    barrier("ffn1")
    ln1_done = emit_ln(big0, big1, TW, [ffn1_done])

    @blk.scalar
    def _(e):
        wt(e, ln1_done)
        gate(e, "qA")
        for f in range(NT):
            e.dma_start(out=x1_sp[f * 128:(f + 1) * 128, :], in_=big0[:, f * TW:(f + 1) * TW]).then_inc(sems["qA"], 16)
        inc("qA", 16 * NT)
        mark("x1_spilled", "qA")

    @blk.scalar
    def _(e):
        wt(e, ln1_done)
        wt(e, "x1_spilled")
        gate(e, "qA")
        nbg = 0
        for f in range(NT):
            e.dma_start(out=bgw[:, f * 24:(f + 1) * 24], in_=wbwa_t[f * 128:(f + 1) * 128, :]).then_inc(sems["qA"], 16); nbg += 1
        for f in range(NT):
            e.dma_start(out=cvw[:, f * 4:(f + 1) * 4], in_=conv_k[f * 128:(f + 1) * 128, :]).then_inc(sems["qA"], 16); nbg += 1
        for f in range(NT):
            e.dma_start(out=cvw[:, (NT + f) * 4:(NT + f + 1) * 4], in_=conv_v[f * 128:(f + 1) * 128, :]).then_inc(sems["qA"], 16); nbg += 1
        for f in range(NQ):
            e.dma_start(out=cvw[:, (2 * NT + f) * 4:(2 * NT + f + 1) * 4], in_=conv_q[f * 128:(f + 1) * 128, :]).then_inc(sems["qA"], 16); nbg += 1
        inc("qA", 16 * nbg)
        mark("bgcv_loaded", "qA")

    if dbg:
        @blk.sync
        def _(e):
            wt(e, ln1_done)
            gate(e, "qS")
            for f in range(NT):
                e.dma_start(out=DBG["d_xln"][f * 128:(f + 1) * 128, :], in_=big1[:, f * TW:(f + 1) * TW]).then_inc(sems["qS"], 16)
            inc("qS", 16 * NT)
            mark("dbg_xln", "qS")

    def emit_proj(name, w_dram, n_fo, conv_base, dst_spill, has_conv, width, tag, prev_sp=None):
        sp512 = spans(width, 512)
        nsp = len(sp512)
        for fo in range(n_fo):
            def pwld_batch(e, fo, hv):
                qn = "qS" if (fo * 2 + hv) % 2 == 0 else "qG"
                gate(e, qn)
                slot = (fo * 2 + hv) % 2
                for f8 in range(8):
                    f = hv * 8 + f8
                    e.dma_start(out=wst[:, slot * 1280 + f8 * 128:slot * 1280 + (f8 + 1) * 128],
                                in_=w_dram[f * 128:(f + 1) * 128, fo * 128:(fo + 1) * 128]).then_inc(sems[qn], 16)
                inc(qn, 16 * 8)
                mark(f"{tag}_wld{fo}_{hv}", qn)

            if fo == 0:
                @blk.sync
                def _(e, fo=fo):
                    wt(e, "x1_spilled")
                    pwld_batch(e, 0, 0)
                    pwld_batch(e, 0, 1)

            for hv in range(2):
                @blk.tensor
                def _(e, fo=fo, hv=hv):
                    wt(e, f"{tag}_wld{fo}_{hv}")
                    wt(e, ln1_done)
                    if hv == 0 and fo >= 1:
                        for psi in range(nsp):
                            wt(e, f"{tag}_pes{fo - 1}_{psi}")
                    slot = (fo * 2 + hv) % 2
                    last = None
                    for si, (o, w) in enumerate(sp512):
                        pb = [pFF, pFF2, pLN][si]
                        po = [0, 0, 64][si]
                        for f8 in range(8):
                            last = e.matmul(pb[:, po:po + w], wst[:, slot * 1280 + f8 * 128:slot * 1280 + (f8 + 1) * 128],
                                            big1[:, (hv * 8 + f8) * width + o:(hv * 8 + f8) * width + o + w],
                                            start=(hv == 0 and f8 == 0), stop=(hv == 1 and f8 == 7))
                    last.then_inc(sems["tT"], 1)
                    inc("tT")
                    mark(f"{tag}_b{fo}_{hv}", "tT")
                    if hv == 1:
                        mark(f"{tag}_mm{fo}", "tT")

                nxt = fo * 2 + hv + 2
                if nxt < n_fo * 2:
                    @blk.sync
                    def _(e, fo=fo, hv=hv, nxt=nxt):
                        wt(e, f"{tag}_b{fo}_{hv}")
                        pwld_batch(e, nxt // 2, nxt % 2)

            for si, (o, w) in enumerate(sp512):
                @blk.vector
                def _(e, fo=fo, si=si, o=o, w=w):
                    wt(e, f"{tag}_b{fo}_{1}")
                    if fo >= 2 and si == 0:
                        wt(e, f"{tag}_cv{fo - 2}")
                    pb = [pFF, pFF2, pLN][si]
                    po = [0, 0, 64][si]
                    e.tensor_copy(cwork[:, (fo % 2) * width + o:(fo % 2) * width + o + w], pb[:, po:po + w])
                    e.drain()
                    e.sem_inc(sems["tV"], 1)
                    inc("tV")
                    mark(f"{tag}_pes{fo}_{si}", "tV")

            @blk.vector
            def _(e, fo=fo):
                wt(e, f"{tag}_pes{fo}_{nsp - 1}")
                if fo >= 2:
                    wt(e, f"{tag}_sp{fo - 2}")
                elif prev_sp is not None:
                    wt(e, prev_sp)
                wt(e, "bgcv_loaded")
                wtile = cwork[:, (fo % 2) * width:(fo % 2) * width + width]
                dtile = cwork[:, (2 + fo % 2) * width:(2 + fo % 2) * width + width]
                if has_conv:
                    cw = cvw[:, (conv_base + fo) * 4:(conv_base + fo) * 4 + 4]
                    e.tensor_scalar(wtile[:, 0:HALO], wtile[:, 0:HALO], HMC[:, 0:1], None, ALU.mult)
                    e.drain()
                    e.tensor_scalar(dtile[:, HALO:width], wtile[:, 0:width - HALO], cw[:, 0:1], None, ALU.mult)
                    e.drain()
                    for j in range(1, 4):
                        e.scalar_tensor_tensor(dtile[:, HALO:width], wtile[:, j:j + width - HALO], cw[:, j:j + 1],
                                               dtile[:, HALO:width], ALU.mult, ALU.add)
                        e.drain()
                else:
                    e.tensor_copy(dtile[:, HALO:width], wtile[:, HALO:width])
                    e.drain()
                e.sem_inc(sems["tV"], 1)
                inc("tV")
                mark(f"{tag}_cv{fo}", "tV")

            @blk.scalar
            def _(e, fo=fo):
                wt(e, f"{tag}_cv{fo}")
                dtile = cwork[:, (2 + fo % 2) * width:(2 + fo % 2) * width + width]
                e.activation(dtile[:, HALO:width], dtile[:, HALO:width], AF.Silu)
                e.drain()
                gate(e, "qA")
                if dst_spill is gg_sp:
                    e.dma_start(out=dst_spill[fo * 128:(fo + 1) * 128, :], in_=dtile[:, HALO:width]).then_inc(sems["qA"], 16)
                else:
                    e.dma_start(out=dst_spill[fo * 128:(fo + 1) * 128, HALO:width], in_=dtile[:, HALO:width]).then_inc(sems["qA"], 16)
                inc("qA", 16)
                mark(f"{tag}_sp{fo}", "qA")
                e.sem_inc(sems["tS"], 1)
                inc("tS")
                mark(f"{tag}_pe{fo}", "tS")

            if dbg and name in ("k", "v", "q"):
                @blk.sync
                def _(e, fo=fo):
                    wt(e, f"{tag}_pe{fo}")
                    gate(e, "qS")
                    dd = {"k": "d_k", "v": "d_v", "q": "d_q"}[name]
                    dtile = cwork[:, (2 + fo % 2) * width:(2 + fo % 2) * width + width]
                    e.dma_start(out=DBG[dd][fo * 128:(fo + 1) * 128, HALO:width], in_=dtile[:, HALO:width]).then_inc(sems["qS"], 16)
                    inc("qS", 16)
                    mark(f"dbg_{name}{fo}", "qS")
        return f"{tag}_sp{n_fo - 1}", f"{tag}_pe{n_fo - 1}"

    barrier("ln1")
    kp_sp, kp_pe = emit_proj("k", wk_t, NT, 0, k_sp, True, TW, "pk")
    barrier("pk")
    vp_sp, vp_pe = emit_proj("v", wv_t, NT, NT, v_sp, True, TW, "pv", prev_sp=kp_sp)
    barrier("pv")
    qp_sp, qp_pe = emit_proj("q", wq_t, NQ, 2 * NT, q_sp, True, TW, "pq", prev_sp=vp_sp)
    barrier("pq")
    gp_sp, gp_pe = emit_proj("gg", wg_t, NQ, 0, gg_sp, False, TW, "pg", prev_sp=qp_sp)

    NSPB = len(spans(TW, 512))
    for si, (o, w) in enumerate(spans(TW, 512)):
        @blk.tensor
        def _(e, si=si, o=o, w=w):
            wt(e, gp_pe)
            wt(e, ln1_done)
            wt(e, "bgcv_loaded")
            if si > 0:
                wt(e, f"bg_v{si - 1}")
            for f in range(NT):
                mm = e.matmul(pFF[0:2 * H, 0:w], bgw[:, f * 24:f * 24 + 16],
                              big1[:, f * TW + o:f * TW + o + w], start=(f == 0), stop=(f == NT - 1))
            for f in range(NT):
                mm2 = e.matmul(pFF2[0:H, 0:w], bgw[:, f * 24 + 16:f * 24 + 24],
                               big1[:, f * TW + o:f * TW + o + w], start=(f == 0), stop=(f == NT - 1))
            mm2.then_inc(sems["tT"], 1)
            inc("tT")
            mark(f"bg_mm{si}", "tT")

        @blk.scalar
        def _(e, si=si, o=o, w=w):
            wt(e, f"bg_mm{si}")
            e.activation(bgb[:, o:o + w], pFF[0:2 * H, 0:w], AF.Sigmoid)
            e.activation(scr[0:H, 0:w], pFF2[0:H, 0:w], AF.Abs, bias=DTC)
            e.drain()
            e.activation(scr[0:H, 0:w], scr[0:H, 0:w], AF.Exp, scale=-1.0)
            e.drain()
            e.activation(scr[0:H, 0:w], scr[0:H, 0:w], AF.Ln, bias=1.0)
            e.drain()
            e.activation(gb[:, o:o + w], pFF2[0:H, 0:w], AF.Relu, bias=DTC)
            e.drain()
            e.sem_inc(sems["tS"], 1)
            inc("tS")
            mark(f"bg_s{si}", "tS")

        @blk.vector
        def _(e, si=si, o=o, w=w):
            wt(e, f"bg_s{si}")
            e.tensor_scalar(bgb[:, o:o + w], bgb[:, o:o + w], 2.0, None, ALU.mult)
            e.tensor_add(gb[:, o:o + w], gb[:, o:o + w], scr[0:H, 0:w])
            e.drain()
            e.tensor_scalar(gb[:, o:o + w], gb[:, o:o + w], NEXPA, None, ALU.mult)
            e.drain()
            e.sem_inc(sems["tV"], 1)
            inc("tV")
            mark(f"bg_v{si}", "tV")

    @blk.vector
    def _(e):
        wt(e, f"bg_v{NSPB - 1}")
        for c in range(NCH):
            o = HALO + c * CT
            e.tensor_tensor_scan(cumb[:, o:o + CT], ONES_T, gb[:, o:o + CT], 0.0, ALU.mult, ALU.add)
        e.drain()
        e.sem_inc(sems["tV"], 1)
        inc("tV")
        mark("cum_done", "tV")

    if dbg:
        @blk.sync
        def _(e):
            wt(e, "cum_done")
            gate(e, "qS")
            e.dma_start(out=DBG["d_b"], in_=bgb[:, :]).then_inc(sems["qS"], 16)
            e.dma_start(out=DBG["d_g"], in_=gb[:, :]).then_inc(sems["qS"], 16)
            e.dma_start(out=DBG["d_cum"], in_=cumb[:, :]).then_inc(sems["qS"], 16)
            inc("qS", 48)
            mark("dbg_bg", "qS")

    for c in range(NCH):
        o = HALO + c * CT

        @blk.tensor
        def _(e, c=c, o=o):
            wt(e, "cum_done")
            if c >= 2:
                wt(e, f"stp_fin{c - 2}")
            pbk = pBC if c % 2 == 0 else pLN
            e.transpose(pbk[0:CT, 0:H], cumb[:, o:o + CT], I128[0:H, 0:H])
            e.transpose(pbk[0:CT, 8:24], bgb[:, o:o + CT], I128[0:16, 0:16]).then_inc(sems["tT"], 1)
            inc("tT")
            mark(f"stp_t1{c}", "tT")

        @blk.vector
        def _(e, c=c, o=o):
            wt(e, f"stp_t1{c}")
            par = c % 2
            pbk = pBC if c % 2 == 0 else pLN
            e.tensor_copy(scr[0:CT, par * 64:par * 64 + 24], pbk[0:CT, 0:24])
            e.drain()
            e.sem_inc(sems["tV"], 1)
            inc("tV")
            mark(f"stp_e1{c}", "tV")

        @blk.tensor
        def _(e, c=c, o=o):
            wt(e, f"stp_e1{c}")
            par = c % 2
            pmk = pMP if c % 2 == 0 else pAP
            e.matmul(pmk[:, 0:H], E_boT, scr[0:CT, par * 64:par * 64 + H], start=True, stop=True)
            e.matmul(pmk[:, 8:16], E_evT, scr[0:CT, par * 64 + 8:par * 64 + 16], start=True, stop=False)
            e.matmul(pmk[:, 8:16], E_odT, scr[0:CT, par * 64 + 16:par * 64 + 24],
                     start=False, stop=True).then_inc(sems["tT"], 1)
            inc("tT")
            mark(f"stp_t2{c}", "tT")

        @blk.vector
        def _(e, c=c, o=o):
            wt(e, f"stp_t2{c}")
            pmk = pMP if c % 2 == 0 else pAP
            e.tensor_copy(stepcs[:, c * 40:c * 40 + 8], pmk[:, 0:H])
            e.tensor_copy(stepcs[:, c * 40 + 16:c * 40 + 24], pmk[:, 8:16])
            e.drain()
            e.sem_inc(sems["tV"], 1)
            inc("tV")
            mark(f"stp_e2{c}", "tV")

        @blk.scalar
        def _(e, c=c, o=o):
            wt(e, f"stp_e2{c}")
            e.activation(stepcs[:, c * 40 + 8:c * 40 + 16], stepcs[:, c * 40:c * 40 + 8], AF.Exp)
            e.drain()
            e.sem_inc(sems["tS"], 1)
            inc("tS")
            mark(f"stp_a{c}", "tS")

        @blk.tensor
        def _(e, c=c, o=o):
            wt(e, f"stp_a{c}")
            pmk = pMP if c % 2 == 0 else pAP
            pbk = pBC if c % 2 == 0 else pLN
            e.matmul(pmk[:, 16:24], sel127[:, :], stepcs[:, c * 40 + 8:c * 40 + 16], start=True, stop=True)
            e.matmul(pmk[:, 24:32], sel127[:, :], stepcs[:, c * 40:c * 40 + 8], start=True, stop=True)
            e.transpose(pbk[0:H, 32:160], stepcs[:, c * 40:c * 40 + 8], I128).then_inc(sems["tT"], 1)
            inc("tT")
            mark(f"stp_t3{c}", "tT")

        @blk.vector
        def _(e, c=c, o=o):
            wt(e, f"stp_t3{c}")
            pmk = pMP if c % 2 == 0 else pAP
            pbk = pBC if c % 2 == 0 else pLN
            e.tensor_copy(accb[:, c * 8:c * 8 + 8], pmk[:, 16:24])
            e.tensor_copy(rowcs[:, c * 128:(c + 1) * 128], pbk[0:H, 32:160])
            e.drain()
            e.tensor_sub(stepcs[:, c * 40 + 32:c * 40 + 40], pmk[:, 24:32],
                         stepcs[:, c * 40:c * 40 + 8])
            e.drain()
            e.tensor_tensor(stepcs[:, c * 40 + 24:c * 40 + 32], stepcs[:, c * 40 + 16:c * 40 + 24],
                            stepcs[:, c * 40 + 8:c * 40 + 16], ALU.mult)
            e.drain()
            e.sem_inc(sems["tV"], 1)
            inc("tV")
            mark(f"stp_v0{c}", "tV")

        @blk.scalar
        def _(e, c=c, o=o):
            wt(e, f"stp_v0{c}")
            e.activation(stepcs[:, c * 40 + 32:c * 40 + 40], stepcs[:, c * 40 + 32:c * 40 + 40], AF.Exp)
            e.drain()
            e.sem_inc(sems["tS"], 1)
            inc("tS")
            mark(f"stp_fin{c}", "tS")

    STP_ALL = f"stp_fin{NCH - 1}"

    barrier("stp")
    prev_ch = [None]

    def A_PART(h):
        @blk.sync
        def _(e, h=h):
            wt(e, kp_sp)
            wt(e, qp_sp)
            if h < 2:
                wt(e, vp_sp)
                wt(e, gp_sp)
                wt(e, f"bg_v{NSPB - 1}")
            if h >= 2:
                wt(e, f"A_done{h - 2}")
            gate(e, "qS")
            e.dma_start(out=khv(h, 0)[:, HALO:TW], in_=k_sp[h * 128:(h + 1) * 128, HALO:TW]).then_inc(sems["qS"], 16)
            e.dma_start(out=khv(h, 1)[:, HALO:TW], in_=k_sp[KEY + h * 128:KEY + (h + 1) * 128, HALO:TW]).then_inc(sems["qS"], 16)
            e.dma_start(out=qhv(h)[:, HALO:TW], in_=q_sp[h * 128:(h + 1) * 128, HALO:TW]).then_inc(sems["qS"], 16)
            inc("qS", 48)
            mark(f"kq_ld{h}", "qS")

        for c in range(NCH):
            o = HALO + c * CT
            idx = h * NCH + c
            par = idx % 2
            pb = pRW if par == 0 else pSQ
            PRV = prev_ch[0]

            @blk.scalar
            def _(e, h=h, c=c, o=o, idx=idx):
                wt(e, vp_sp)
                if idx < 2:
                    wt(e, gp_sp)
                    wt(e, f"bg_v{NSPB - 1}")
                if idx >= 2:
                    wt(e, f"vused{idx - 2}")
                gate(e, "qA")
                e.dma_start(out=vchv(c, 0), in_=v_sp[h * 128:(h + 1) * 128, o:o + CT]).then_inc(sems["qA"], 16)
                e.dma_start(out=vchv(c, 1), in_=v_sp[KEY + h * 128:KEY + (h + 1) * 128, o:o + CT]).then_inc(sems["qA"], 16)
                inc("qA", 32)
                mark(f"v_ld{idx}", "qA")

            @blk.vector
            def _(e, h=h, c=c, o=o, idx=idx, par=par):
                wt(e, f"kq_ld{h}")
                if idx >= 2:
                    wt(e, f"intu{idx - 2}")
                if dbg and 1 <= idx <= 2 and "dbg_ch0" in ev:
                    wt(e, "dbg_ch0")
                if idx < 2:
                    wt(e, STP_ALL)
                kb = kint[:, par * C:par * C + C]
                qb = qint[:, par * C:par * C + C]
                e.memset(qb[:, 0:C:2], 0.0)
                e.tensor_copy(kb[:, 0:C:2], khv(h, 0)[:, o:o + CT])
                e.tensor_copy(kb[:, 1:C:2], khv(h, 1)[:, o:o + CT])
                e.tensor_copy(qb[:, 1:C:2], qhv(h)[:, o:o + CT])
                e.drain()
                e.sem_inc(sems["tV"], 1)
                inc("tV")
                mark(f"int{idx}", "tV")

            @blk.tensor
            def _(e, idx=idx, par=par, pb=pb):
                wt(e, f"int{idx}")
                if idx >= 2:
                    wt(e, f"pt{idx - 2}")
                e.transpose(pb[:, 0:128], kint[:, par * C:par * C + C], I128)
                e.transpose(pb[:, 128:256], qint[:, par * C:par * C + C], I128).then_inc(sems["tT"], 1)
                inc("tT")
                mark(f"tr1{idx}", "tT")

            @blk.scalar
            def _(e, idx=idx, pb=pb):
                wt(e, f"tr1{idx}")
                e.activation(scr[:, 4 * C:4 * C + 128], pb[:, 0:128], AF.Square, accum_out=nrmc[:, 0:1])
                e.drain()
                e.activation(scr[:, 4 * C:4 * C + 128], pb[:, 128:256], AF.Square, accum_out=nrmc[:, 2:3])
                e.drain()
                e.sem_inc(sems["tS"], 1)
                inc("tS")
                mark(f"ss{idx}", "tS")

            @blk.vector
            def _(e, idx=idx):
                wt(e, f"ss{idx}")
                e.tensor_scalar(nrmc[:, 0:1], nrmc[:, 0:1], 1e-24, None, ALU.max)
                e.tensor_scalar(nrmc[:, 2:3], nrmc[:, 2:3], 1e-24, None, ALU.max)
                e.drain()
                e.sem_inc(sems["tV"], 1)
                inc("tV")
                mark(f"ssm{idx}", "tV")

            @blk.scalar
            def _(e, idx=idx):
                wt(e, f"ssm{idx}")
                e.activation(nrmc[:, 1:2], nrmc[:, 0:1], AF.Sqrt)
                e.activation(nrmc[:, 3:4], nrmc[:, 2:3], AF.Sqrt)
                e.drain()
                e.sem_inc(sems["tS"], 1)
                inc("tS")
                mark(f"inv{idx}", "tS")

            @blk.vector
            def _(e, h=h, c=c, idx=idx, par=par, pb=pb):
                wt(e, f"inv{idx}")
                wt(e, STP_ALL)
                e.reciprocal(nrmc[:, 1:2], nrmc[:, 1:2])
                e.reciprocal(nrmc[:, 3:4], nrmc[:, 3:4])
                e.drain()
                e.tensor_scalar(nrmc[:, 3:4], nrmc[:, 3:4], INV_SQRT_DH, None, ALU.mult)
                e.drain()
                e.tensor_scalar(ktm[:, par * DH:par * DH + DH], pb[:, 0:128], nrmc[:, 1:2], None, ALU.mult)
                e.tensor_scalar(scr[:, 0:128], pb[:, 128:256], nrmc[:, 3:4], None, ALU.mult)
                e.drain()
                bcol = stepcs[:, c * 40 + 16 + h:c * 40 + 16 + h + 1]
                acol = stepcs[:, c * 40 + 8 + h:c * 40 + 8 + h + 1]
                e.tensor_scalar(kbtm[:, 0:DH], ktm[:, par * DH:par * DH + DH], bcol, None, ALU.mult)
                e.tensor_scalar(qatm[:, :], scr[:, 0:128], acol, None, ALU.mult)
                e.drain()
                e.sem_inc(sems["tV"], 1)
                inc("tV")
                mark(f"ktmn{idx}", "tV")
                mark(f"intu{idx}", "tV")

            @blk.tensor
            def _(e, idx=idx, par=par, pb=pb):
                wt(e, f"ktmn{idx}")
                e.transpose(pb[:, 256:384], ktm[:, par * DH:par * DH + DH], I128)
                e.transpose(pb[:, 384:512], scr[:, 0:128], I128)
                e.transpose(pb[:, 0:128], kbtm[:, 0:DH], I128)
                e.transpose(pb[:, 128:256], qatm[:, :], I128).then_inc(sems["tT"], 1)
                inc("tT")
                mark(f"tr2{idx}", "tT")

            @blk.vector
            def _(e, idx=idx, par=par, pb=pb):
                wt(e, f"tr2{idx}")
                e.tensor_copy(khf[:, par * C:par * C + C], pb[:, 256:384])
                e.tensor_copy(qhf[:, par * C:par * C + C], pb[:, 384:512])
                e.tensor_copy(kbfm[:, 0:C], pb[:, 0:128])
                e.tensor_copy(qtl[:, :], pb[:, 128:256])
                e.drain()
                e.sem_inc(sems["tV"], 1)
                inc("tV")
                mark(f"hf{idx}", "tV")

            @blk.vector
            def _(e, c=c, idx=idx):
                wt(e, f"v_ld{idx}")
                wt(e, f"hf{idx}")
                vb = scr[:, C:2 * C]
                e.tensor_copy(vb[:, 0:C:2], vchv(c, 0))
                e.tensor_copy(vb[:, 1:C:2], vchv(c, 1))
                e.drain()
                e.sem_inc(sems["tV"], 1)
                inc("tV")
                mark(f"vint{idx}", "tV")
                mark(f"vused{idx}", "tV")

            @blk.tensor
            def _(e, idx=idx, pb=pb):
                wt(e, f"vint{idx}")
                e.transpose(pb[:, 0:128], scr[:, C:2 * C], I128).then_inc(sems["tT"], 1)
                inc("tT")
                mark(f"vtr{idx}", "tT")

            @blk.vector
            def _(e, h=h, c=c, idx=idx, par=par, pb=pb):
                wt(e, f"vtr{idx}")
                wt(e, STP_ALL)
                if dbg and 1 <= idx <= 2 and "dbg_ch0" in ev:
                    wt(e, "dbg_ch0")
                bcol = stepcs[:, c * 40 + 16 + h:c * 40 + 16 + h + 1]
                bacol = stepcs[:, c * 40 + 24 + h:c * 40 + 24 + h + 1]
                rb = rhsb[:, par * 256:par * 256 + 256]
                e.tensor_scalar(rb[:, 0:DH], pb[:, 0:128], bcol, None, ALU.mult)
                e.tensor_scalar(rb[:, DH:2 * DH], ktm[:, par * DH:par * DH + DH], bacol, None, ALU.mult)
                e.drain()
                e.sem_inc(sems["tV"], 1)
                inc("tV")
                mark(f"rhs{idx}", "tV")

            @blk.tensor
            def _(e, h=h, c=c, idx=idx, par=par, pb=pb):
                wt(e, f"rhs{idx}")
                kf = khf[:, par * C:par * C + C]
                qf = qhf[:, par * C:par * C + C]
                e.matmul(pb[:, 0:128], kf, kf, start=True, stop=True)
                e.matmul(pb[:, 128:256], kf, kbfm[:, 0:C], start=True, stop=True)
                e.matmul(pb[:, 256:384], kf, qf, start=True, stop=True)
                e.matmul(pb[:, 384:512], selb[0:H, h * 128:h * 128 + 128], rowcs[0:H, c * 128:(c + 1) * 128],
                         start=True, stop=True).then_inc(sems["tT"], 1)
                inc("tT")
                mark(f"raw{idx}", "tT")

            @blk.vector
            def _(e, h=h, c=c, idx=idx, pb=pb):
                wt(e, f"raw{idx}")
                ccol = stepcs[:, c * 40 + h:c * 40 + h + 1]
                e.tensor_scalar(dxp[:, :], pb[:, 384:512], ccol, 0.0, ALU.subtract, ALU.min)
                e.drain()
                e.sem_inc(sems["tV"], 1)
                inc("tV")
                mark(f"darg{idx}", "tV")

            @blk.scalar
            def _(e, idx=idx):
                wt(e, f"darg{idx}")
                e.activation(dxp[:, :], dxp[:, :], AF.Exp)
                e.drain()
                e.sem_inc(sems["tS"], 1)
                inc("tS")
                mark(f"dexp{idx}", "tS")

            @blk.vector
            def _(e, h=h, c=c, idx=idx, pb=pb, PRV=PRV):
                wt(e, f"dexp{idx}")
                if dbg and 1 <= idx <= 2 and "dbg_ch0" in ev:
                    wt(e, "dbg_ch0")
                if PRV is not None:
                    wt(e, f"gap{idx - 1}")
                ccol = stepcs[:, c * 40 + h:c * 40 + h + 1]
                bcol = stepcs[:, c * 40 + 16 + h:c * 40 + 16 + h + 1]
                e.tensor_tensor(bpw[:, 0:C], pb[:, 128:256], dxp[:, :], ALU.mult)
                e.drain()
                e.tensor_tensor(bpw[:, 0:C], bpw[:, 0:C], MSU, ALU.mult)
                e.drain()
                e.tensor_scalar(scr[:, 2 * C:3 * C], pb[:, 384:512], ccol, -1.0, ALU.subtract, ALU.mult)
                e.drain()
                e.tensor_scalar_min(scr[:, 2 * C:3 * C], scr[:, 2 * C:3 * C], 0.0)
                e.drain()
                e.sem_inc(sems["tV"], 1)
                inc("tV")
                mark(f"bb{idx}", "tV")

            @blk.scalar
            def _(e, idx=idx):
                wt(e, f"bb{idx}")
                e.activation(scr[:, 2 * C:3 * C], scr[:, 2 * C:3 * C], AF.Exp)
                e.drain()
                e.sem_inc(sems["tS"], 1)
                inc("tS")
                mark(f"dexpL{idx}", "tS")

            @blk.vector
            def _(e, h=h, c=c, idx=idx, pb=pb):
                wt(e, f"dexpL{idx}")
                bcol = stepcs[:, c * 40 + 16 + h:c * 40 + 16 + h + 1]
                e.tensor_tensor(apw[:, 0:C], pb[:, 0:128], scr[:, 2 * C:3 * C], ALU.mult)
                e.drain()
                e.tensor_scalar(apw[:, 0:C], apw[:, 0:C], bcol, None, ALU.mult)
                e.drain()
                e.tensor_tensor(apw[:, 0:C], apw[:, 0:C], MSL, ALU.mult)
                e.tensor_tensor(gst[:, :], pb[:, 256:384], dxp[:, :], ALU.mult)
                e.drain()
                e.tensor_tensor(gst[:, :], gst[:, :], MGU, ALU.mult)
                e.drain()
                e.sem_inc(sems["tV"], 1)
                inc("tV")
                mark(f"ab{idx}", "tV")

            for lv in range(1, 7):
                @blk.tensor
                def _(e, idx=idx, lv=lv):
                    wt(e, f"ab{idx}" if lv == 1 else f"sqr{lv - 1}_{idx}")
                    src_a = apw[:, ((lv - 1) % 2) * C:((lv - 1) % 2) * C + C]
                    src_b = bpw[:, (lv - 1) * C:lv * C]
                    e.matmul(pAP[:, 0:128], src_b, src_a, start=True, stop=True)
                    e.matmul(pAP[:, 128:256], src_a, src_b, start=True, stop=True).then_inc(sems["tT"], 1)
                    inc("tT")
                    mark(f"sqt{lv}_{idx}", "tT")

                @blk.vector
                def _(e, idx=idx, lv=lv):
                    wt(e, f"sqt{lv}_{idx}")
                    if dbg and idx == 0 and lv == 1 and "dbg_bb" in ev:
                        wt(e, "dbg_bb")
                    e.tensor_copy(apw[:, (lv % 2) * C:(lv % 2) * C + C], pAP[:, 0:128])
                    e.tensor_copy(bpw[:, lv * C:(lv + 1) * C], pAP[:, 128:256])
                    e.drain()
                    e.sem_inc(sems["tV"], 1)
                    inc("tV")
                    mark(f"sqr{lv}_{idx}", "tV")

            for ai, lv in enumerate([6, 5, 4, 3, 2, 1, 0]):
                @blk.tensor
                def _(e, idx=idx, par=par, ai=ai, lv=lv, PRV=PRV):
                    wt(e, f"sqr6_{idx}" if ai == 0 else f"app{ai - 1}_{idx}")
                    if ai == 0 and PRV is not None:
                        wt(e, f"app6_{idx - 1}")
                    rb = rhsb[:, par * 256:par * 256 + 256]
                    e.matmul(pAP[:, 256:512], bpw[:, lv * C:(lv + 1) * C], rb, start=True, stop=True).then_inc(sems["tT"], 1)
                    inc("tT")
                    mark(f"apt{ai}_{idx}", "tT")

                @blk.vector
                def _(e, idx=idx, par=par, ai=ai, lv=lv):
                    wt(e, f"apt{ai}_{idx}")
                    rb = rhsb[:, par * 256:par * 256 + 256]
                    if lv == 0:
                        e.tensor_sub(rb, rb, pAP[:, 256:512])
                    else:
                        e.tensor_add(rb, rb, pAP[:, 256:512])
                    e.drain()
                    e.sem_inc(sems["tV"], 1)
                    inc("tV")
                    mark(f"app{ai}_{idx}", "tV")

            @blk.tensor
            def _(e, idx=idx, par=par, pb=pb):
                wt(e, f"app6_{idx}")
                rb = rhsb[:, par * 256:par * 256 + 256]
                e.matmul(pb[:, 0:256], gst[:, :], rb, start=True, stop=True)
                e.matmul(pb[:, 256:384], rb[:, DH:2 * DH], gst[:, :], start=True, stop=True).then_inc(sems["tT"], 1)
                inc("tT")
                mark(f"gap{idx}", "tT")

            @blk.vector
            def _(e, h=h, c=c, idx=idx, par=par, pb=pb):
                wt(e, f"gap{idx}")
                e.tensor_copy(art(h, c, 1), pb[:, 0:DH])
                e.tensor_sub(art(h, c, 0), qtl[:, :], pb[:, 256:384])
                e.drain()
                crc = stepcs[:, c * 40 + 32 + h:c * 40 + 32 + h + 1]
                e.tensor_scalar(kbar[:, :], ktm[:, par * DH:par * DH + DH], crc, None, ALU.mult)
                e.drain()
                e.sem_inc(sems["tV"], 1)
                inc("tV")
                mark(f"pt{idx}", "tV")

            @blk.tensor
            def _(e, idx=idx, par=par, PRV=PRV):
                wt(e, f"pt{idx}")
                if PRV is not None:
                    wt(e, f"cmp{idx - 1}")
                rb = rhsb[:, par * 256:par * 256 + 256]
                e.matmul(pMP[:, 0:256], kbar[:, :], rb, start=True, stop=True)
                e.matmul(pMP[:, 256:384], rb[:, DH:2 * DH], kbar[:, :], start=True, stop=True).then_inc(sems["tT"], 1)
                inc("tT")
                mark(f"mp{idx}", "tT")

            @blk.vector
            def _(e, h=h, c=c, idx=idx):
                wt(e, f"mp{idx}")
                if h >= 2 and c == 0:
                    wt(e, f"B_done{h - 2}")
                acol = accb[:, c * 8 + h:c * 8 + h + 1]
                e.tensor_copy(art(h, c, 3), pMP[:, 0:DH])
                e.scalar_tensor_tensor(art(h, c, 2), I128, acol, pMP[:, 256:384], ALU.mult, ALU.subtract)
                e.drain()
                e.sem_inc(sems["tV"], 1)
                inc("tV")
                mark(f"mn{idx}", "tV")

            if c == 0:
                @blk.vector
                def _(e, h=h, c=c, idx=idx):
                    wt(e, f"mn{idx}")
                    acol = accb[:, c * 8 + h:c * 8 + h + 1]
                    e.scalar_tensor_tensor(comp[:, 0:DH], I128, acol, pMP[:, 128:256], ALU.mult, ALU.subtract)
                    e.tensor_copy(comp[:, DH:2 * DH], art(h, c, 2))
                    e.tensor_copy(comp[:, 2 * DH:3 * DH], art(h, c, 3))
                    e.drain()
                    e.sem_inc(sems["tV"], 1)
                    inc("tV")
                    mark(f"cmp{idx}", "tV")
            else:
                @blk.vector
                def _(e, h=h, c=c, idx=idx):
                    wt(e, f"mn{idx}")
                    acol = accb[:, c * 8 + h:c * 8 + h + 1]
                    e.scalar_tensor_tensor(scr[:, 3 * C:3 * C + DH], I128, acol, pMP[:, 128:256], ALU.mult, ALU.subtract)
                    e.drain()
                    e.sem_inc(sems["tV"], 1)
                    inc("tV")
                    mark(f"mcu{idx}", "tV")

                @blk.tensor
                def _(e, h=h, c=c, idx=idx):
                    wt(e, f"mcu{idx}")
                    e.matmul(pMP[:, 0:128], art(h, c, 2), comp[:, 0:DH], start=True, stop=True)
                    e.matmul(pMP[:, 128:256], comp[:, 0:DH], scr[:, 3 * C:3 * C + DH], start=True, stop=True)
                    e.matmul(pMP[:, 256:384], art(h, c, 2), comp[:, 2 * DH:3 * DH],
                             start=True, stop=True).then_inc(sems["tT"], 1)
                    inc("tT")
                    mark(f"cmt{idx}", "tT")

                @blk.vector
                def _(e, h=h, c=c, idx=idx):
                    wt(e, f"cmt{idx}")
                    e.tensor_copy(comp[:, 0:DH], pMP[:, 0:128])
                    e.tensor_copy(comp[:, DH:2 * DH], pMP[:, 128:256])
                    e.tensor_add(comp[:, 2 * DH:3 * DH], pMP[:, 256:384], art(h, c, 3))
                    e.drain()
                    e.sem_inc(sems["tV"], 1)
                    inc("tV")
                    mark(f"cmp{idx}", "tV")

            if dbg and h == 0 and c == 0:
                @blk.sync
                def _(e, h=h, c=c, idx=idx, par=par):
                    wt(e, f"ab{idx}")
                    gate(e, "qS")
                    e.dma_start(out=DBG["d_bup"], in_=bpw[:, 0:C]).then_inc(sems["qS"], 16)
                    inc("qS", 16)
                    mark("dbg_bb", "qS")

                @blk.sync
                def _(e, h=h, c=c, idx=idx, par=par):
                    wt(e, f"mn{idx}")
                    gate(e, "qS")
                    e.dma_start(out=DBG["d_kint"], in_=kint[:, par * C:par * C + C]).then_inc(sems["qS"], 16)
                    e.dma_start(out=DBG["d_ktm"], in_=ktm[:, par * DH:par * DH + DH]).then_inc(sems["qS"], 16)
                    e.dma_start(out=DBG["d_rhs"], in_=rhsb[:, par * 256:par * 256 + 256]).then_inc(sems["qS"], 16)
                    e.dma_start(out=DBG["d_gst"], in_=gst[:, :]).then_inc(sems["qS"], 16)
                    e.dma_start(out=DBG["d_kbf"], in_=kbfm[:, 0:C]).then_inc(sems["qS"], 16)
                    e.dma_start(out=DBG["d_kbt"], in_=kbtm[:, 0:DH]).then_inc(sems["qS"], 16)
                    e.dma_start(out=DBG["d_art"], in_=big0[:, 0 + c * 4 * DH:0 + (c + 1) * 4 * DH]).then_inc(sems["qS"], 16)
                    inc("qS", 112)
                    mark("dbg_ch0", "qS")

            prev_ch[0] = (h, c)

        @blk.vector
        def _(e, h=h):
            wt(e, f"cmp{h * NCH + NCH - 1}")
            if h >= 1:
                wt(e, f"cci{h - 1}")
            e.tensor_copy(ccsb[:, 0:DH], comp[:, DH:2 * DH])
            e.tensor_copy(ccsb[:, DH:2 * DH], comp[:, 2 * DH:3 * DH])
            e.drain()
            e.sem_inc(sems["tV"], 1)
            inc("tV")
            mark(f"ccs{h}", "tV")
            mark(f"A_done{h}", "tV")

        @blk.scalar
        def _(e, h=h):
            wt(e, f"ccs{h}")
            gate(e, "qA")
            e.dma_start(out=cc_in[h], in_=ccsb[:, :]).then_inc(sems["qA"], 16)
            inc("qA", 16)
            mark(f"cci{h}", "qA")

        @blk.gpsimd
        def _(e, h=h):
            wt(e, f"cci{h}")
            e.collective_compute("AllGather", ALU.bypass, ins=[cc_in[h]], outs=[cc_out[h]],
                                 replica_groups=[[i for i in range(n_cores)]]).then_inc(sems["cc"], 1)
            inc("cc", 1)
            mark(f"ccd{h}", "cc")

    def B_PART(h):
        @blk.vector
        def _(e, h=h):
            wt(e, f"ccd{h}")
            if h >= 1:
                wt(e, f"B_done{h - 1}")
            e.memset(prefx[:, :], 0.0)
            e.drain()
            e.sem_inc(sems["tV"], 1)
            inc("tV")
            mark(f"pfx0{h}", "tV")

        for j in range(n_cores - 1):
            @blk.scalar
            def _(e, h=h, j=j):
                wt(e, f"ccd{h}")
                if j >= 2:
                    wt(e, f"pfa{h}_{j - 2}")
                gate(e, "qA")
                e.dma_start(out=gsb[:, (j % 2) * 256:(j % 2) * 256 + 256], in_=cc_out[h][j]).then_inc(sems["qA"], 16)
                inc("qA", 16)
                mark(f"gld{h}_{j}", "qA")

            @blk.vector
            def _(e, h=h, j=j):
                wt(e, f"gld{h}_{j}")
                wt(e, f"pfx0{h}")
                if j > 0:
                    wt(e, f"pfa{h}_{j - 1}")
                g = gsb[:, (j % 2) * 256:(j % 2) * 256 + 256]
                sc = selbc[:, j:j + 1]
                e.tensor_scalar(bscr[:, 0:DH], g[:, 0:DH], sc, None, ALU.mult)
                e.tensor_scalar(bscr[:, 2 * DH:3 * DH], g[:, DH:2 * DH], sc, None, ALU.mult)
                e.tensor_scalar(bscr[:, DH:2 * DH], I128, sc, -1.0, ALU.mult, ALU.mult)
                e.drain()
                e.tensor_add(bscr[:, DH:2 * DH], bscr[:, DH:2 * DH], I128)
                e.drain()
                e.tensor_add(bscr[:, 0:DH], bscr[:, 0:DH], bscr[:, DH:2 * DH])
                e.drain()
                e.sem_inc(sems["tV"], 1)
                inc("tV")
                mark(f"pfm{h}_{j}", "tV")

            @blk.tensor
            def _(e, h=h, j=j):
                wt(e, f"pfm{h}_{j}")
                e.matmul(pBC[:, 256:384], bscr[:, 0:DH], prefx[:, :], start=True, stop=True).then_inc(sems["tT"], 1)
                inc("tT")
                mark(f"pft{h}_{j}", "tT")

            @blk.vector
            def _(e, h=h, j=j):
                wt(e, f"pft{h}_{j}")
                e.tensor_add(prefx[:, :], pBC[:, 256:384], bscr[:, 2 * DH:3 * DH])
                e.drain()
                e.sem_inc(sems["tV"], 1)
                inc("tV")
                mark(f"pfa{h}_{j}", "tV")

        PFX = f"pfa{h}_{n_cores - 2}" if n_cores > 1 else f"pfx0{h}"

        if dbg and h == 0:
            @blk.sync
            def _(e, h=h):
                wt(e, PFX)
                gate(e, "qS")
                e.dma_start(out=DBG["d_sin"], in_=prefx[:, :]).then_inc(sems["qS"], 16)
                inc("qS", 16)
                mark("dbg_sin", "qS")

        @blk.vector
        def _(e, h=h):
            wt(e, PFX)
            e.tensor_copy(stt[:, :], prefx[:, :])
            e.drain()
            e.sem_inc(sems["tV"], 1)
            inc("tV")
            mark(f"stinit{h}", "tV")

        for c in range(NCH):
            bp = c % 2
            SP = f"stinit{h}" if c == 0 else f"stch{h}_{c - 1}"

            @blk.scalar
            def _(e, h=h, c=c, bp=bp):
                wt(e, gp_sp)
                if c >= 2:
                    wt(e, f"oev{h}_{c - 2}")
                gate(e, "qA")
                e.dma_start(out=ggch[:, bp * CT:bp * CT + CT],
                            in_=gg_sp[h * 128:(h + 1) * 128, c * CT:(c + 1) * CT]).then_inc(sems["qA"], 16)
                inc("qA", 16)
                mark(f"gg_ld{h}_{c}", "qA")

            @blk.tensor
            def _(e, h=h, c=c, bp=bp, SP=SP):
                wt(e, SP)
                wt(e, f"mn{h * NCH + c}")
                if c >= 2:
                    wt(e, f"oev{h}_{c - 2}")
                e.matmul(pBC[:, 0:128], art(h, c, 0), stt[:, :], start=True, stop=True)
                e.matmul(pBC[:, 128:256], art(h, c, 2), stt[:, :],
                         start=True, stop=True).then_inc(sems["tT"], 1)
                inc("tT")
                mark(f"bst{h}_{c}", "tT")

            @blk.vector
            def _(e, h=h, c=c, bp=bp):
                wt(e, f"bst{h}_{c}")
                if dbg and h == 0 and c == 2 and "dbg_och" in ev:
                    wt(e, "dbg_och")
                e.tensor_add(och[:, bp * DH:bp * DH + DH], pBC[:, 0:128], art(h, c, 1))
                e.tensor_add(stt[:, :], pBC[:, 128:256], art(h, c, 3))
                e.drain()
                e.sem_inc(sems["tV"], 1)
                inc("tV")
                mark(f"stch{h}_{c}", "tV")

            @blk.scalar
            def _(e, h=h, c=c, bp=bp):
                wt(e, f"stch{h}_{c}")
                wt(e, f"gg_ld{h}_{c}")
                ob = och[:, bp * DH:bp * DH + DH]
                e.activation(scr[:, 5 * C:5 * C + DH], ob, AF.Square, accum_out=nrmc[:, 4:5])
                e.drain()
                e.activation(nrmc[:, 5:6], nrmc[:, 4:5], AF.Sqrt, bias=EPSC, scale=INVDHC)
                e.drain()
                e.sem_inc(sems["tS"], 1)
                inc("tS")
                mark(f"sgg{h}_{c}", "tS")

            @blk.vector
            def _(e, h=h, c=c, bp=bp):
                wt(e, f"sgg{h}_{c}")
                ob = och[:, bp * DH:bp * DH + DH]
                e.reciprocal(nrmc[:, 5:6], nrmc[:, 5:6])
                e.drain()
                e.tensor_scalar(ob, ob, nrmc[:, 5:6], None, ALU.mult)
                e.drain()
                e.sem_inc(sems["tV"], 1)
                inc("tV")
                mark(f"orms{h}_{c}", "tV")

            @blk.tensor
            def _(e, h=h, c=c, bp=bp):
                wt(e, f"orms{h}_{c}")
                e.transpose(pLN[:, 0:128], och[:, bp * DH:bp * DH + DH], I128).then_inc(sems["tT"], 1)
                inc("tT")
                mark(f"otr{h}_{c}", "tT")

            @blk.vector
            def _(e, h=h, c=c, bp=bp):
                wt(e, f"otr{h}_{c}")
                o = c * CT
                e.tensor_tensor(scr[:, 5 * C:5 * C + CT], pLN[:, 1:128:2],
                                ggch[:, bp * CT:bp * CT + CT], ALU.mult)
                e.drain()
                e.tensor_scalar(ofmv(h)[:, o:o + CT], scr[:, 5 * C:5 * C + CT], onrm[:, 0:1], None, ALU.mult)
                e.drain()
                e.sem_inc(sems["tV"], 1)
                inc("tV")
                mark(f"oev{h}_{c}", "tV")

            if dbg and h == 0 and c == 0:
                @blk.sync
                def _(e, h=h, c=c, bp=bp):
                    wt(e, f"orms{h}_{c}")
                    gate(e, "qS")
                    e.dma_start(out=DBG["d_och"], in_=och[:, bp * DH:bp * DH + DH]).then_inc(sems["qS"], 16)
                    inc("qS", 16)
                    mark("dbg_och", "qS")

        @blk.vector
        def _(e, h=h):
            wt(e, f"oev{h}_{NCH - 1}")
            e.tensor_copy(bscr[0:1, 0:1], bscr[0:1, 0:1])
            e.drain()
            e.sem_inc(sems["tV"], 1)
            inc("tV")
            mark(f"B_done{h}", "tV")

    for h in range(H):
        A_PART(h)
        barrier(f"A{h}")
        if h >= 1:
            B_PART(h - 1)
    B_PART(H - 1)

    if dbg:
        @blk.sync
        def _(e):
            wt(e, f"B_done{H - 1}")
            gate(e, "qS")
            for f in range(NQ):
                e.dma_start(out=DBG["d_ofm"][f * 128:(f + 1) * 128, :],
                            in_=big1[:, OFM_OFF + f * tpc:OFM_OFF + (f + 1) * tpc]).then_inc(sems["qS"], 16)
            inc("qS", 16 * NQ)
            mark("dbg_ofm", "qS")

    X2v = big0
    barrier("scan")

    @blk.sync
    def _(e):
        wt(e, f"B_done{H - 1}")
        wt(e, "x1_spilled")
        gate(e, "qS")
        for f in range(NT):
            e.dma_start(out=X2v[:, f * tpc:(f + 1) * tpc], in_=x1_sp[f * 128:(f + 1) * 128, HALO:TW]).then_inc(sems["qS"], 16)
        inc("qS", 16 * NT)
        mark("x1_reld", "qS")

    nspo = len(spans(tpc, 512))
    for fo in range(NT):
        @blk.sync
        def _(e, fo=fo):
            wt(e, "x1_reld")
            if fo >= 2:
                wt(e, f"wo_m{fo - 2}_{nspo - 1}")
            qn = "qS" if fo % 2 == 0 else "qG"
            gate(e, qn)
            for f in range(NQ):
                e.dma_start(out=wst[:, (fo % 2) * 1280 + f * 128:(fo % 2) * 1280 + (f + 1) * 128],
                            in_=wo_t[f * 128:(f + 1) * 128, fo * 128:(fo + 1) * 128]).then_inc(sems[qn], 16)
            inc(qn, 16 * NQ)
            mark(f"wo_ld{fo}", qn)

        for si, (o, w) in enumerate(spans(tpc, 512)):
            @blk.tensor
            def _(e, fo=fo, si=si, o=o, w=w):
                wt(e, f"wo_ld{fo}")
                if fo >= 2 or si > 0:
                    pf, ps_ = (fo, si - 1) if si > 0 else (fo - 2, nspo - 1)
                    wt(e, f"wo_ev{pf}_{ps_}")
                pb = pFF if fo % 2 == 0 else pFF2
                for f in range(NQ):
                    mm = e.matmul(pb[:, 0:w], wst[:, (fo % 2) * 1280 + f * 128:(fo % 2) * 1280 + (f + 1) * 128],
                                  big1[:, OFM_OFF + f * tpc + o:OFM_OFF + f * tpc + o + w],
                                  start=(f == 0), stop=(f == NQ - 1))
                mm.then_inc(sems["tT"], 1)
                inc("tT")
                mark(f"wo_m{fo}_{si}", "tT")

            @blk.vector
            def _(e, fo=fo, si=si, o=o, w=w):
                wt(e, f"wo_m{fo}_{si}")
                pb = pFF if fo % 2 == 0 else pFF2
                e.tensor_add(X2v[:, fo * tpc + o:fo * tpc + o + w], X2v[:, fo * tpc + o:fo * tpc + o + w], pb[:, 0:w])
                e.drain()
                e.sem_inc(sems["tV"], 1)
                inc("tV")
                mark(f"wo_ev{fo}_{si}", "tV")

    WO_ALL = f"wo_ev{NT - 1}_{nspo - 1}"

    if dbg:
        @blk.sync
        def _(e):
            wt(e, WO_ALL)
            gate(e, "qS")
            for f in range(NT):
                e.dma_start(out=DBG["d_x2"][f * 128:(f + 1) * 128, :], in_=X2v[:, f * tpc:(f + 1) * tpc]).then_inc(sems["qS"], 16)
            inc("qS", 16 * NT)
            mark("dbg_x2", "qS")

    @blk.scalar
    def _(e):
        wt(e, WO_ALL)
        gate(e, "qA")
        for f in range(NT):
            e.dma_start(out=smw[:, f * SM:(f + 1) * SM], in_=w_small_t[f * 128:(f + 1) * 128, :]).then_inc(sems["qA"], 16)
        inc("qA", 16 * NT)
        mark("smw2_loaded", "qA")

    barrier("wo")
    ln2_done = emit_ln(X2v, big1, tpc, [WO_ALL])
    ffn2_done = emit_ffn(big1, X2v, tpc, [ln2_done, "smw2_loaded"], "fB")

    @blk.sync
    def _(e):
        wt(e, ffn2_done)
        gate(e, "qS")
        for f in range(NT):
            e.dma_start(out=out_fm[f * 128:(f + 1) * 128, :], in_=X2v[:, f * tpc:(f + 1) * tpc]).then_inc(sems["qS"], 16)
        inc("qS", 16 * NT)
        mark("out_st", "qS")
        wt(e, "out_st")
        if dbg:
            for nm in list(ev.keys()):
                if nm.startswith("dbg_"):
                    wt(e, nm)

    for g in reversed(guards):
        try:
            g.__exit__(None, None, None)
        except Exception:
            pass
    return nc, DBG



def _layout(inputs, n_cores, tpc):
    TW = tpc + HALO
    i = inputs
    x = np.asarray(i["L0_attn_x"], dtype=np.float32)[0]
    xT = np.ascontiguousarray(x.T)
    shared = {}
    shared["wq_t"] = np.ascontiguousarray(np.asarray(i["L0_attn_Wq"], np.float32).T)
    shared["wk_t"] = np.ascontiguousarray(np.asarray(i["L0_attn_Wk"], np.float32).T)
    shared["wv_t"] = np.ascontiguousarray(np.asarray(i["L0_attn_Wv"], np.float32).T)
    shared["wbwa_t"] = np.ascontiguousarray(
        np.concatenate([np.asarray(i["L0_attn_Wb"], np.float32).T,
                        np.asarray(i["L0_attn_Wa"], np.float32).T], axis=1))
    shared["wg_t"] = np.ascontiguousarray(np.asarray(i["L0_attn_Wg"], np.float32).T)
    shared["wo_t"] = np.ascontiguousarray(np.asarray(i["L0_attn_Wo"], np.float32).T)
    shared["a_log"] = np.asarray(i["L0_attn_A_log"], np.float32).reshape(1, H)
    shared["dt_b"] = np.asarray(i["L0_attn_dt_bias"], np.float32).reshape(1, H)
    shared["conv_q"] = np.ascontiguousarray(np.asarray(i["L0_attn_conv_q"], np.float32))
    shared["conv_k"] = np.ascontiguousarray(np.asarray(i["L0_attn_conv_k"], np.float32))
    shared["conv_v"] = np.ascontiguousarray(np.asarray(i["L0_attn_conv_v"], np.float32))
    shared["onw"] = np.asarray(i["L0_attn_o_norm_w"], np.float32).reshape(1, DH)
    shared["w_base_t"] = np.ascontiguousarray(np.asarray(i["L0_ffn_W_base"], np.float32))
    shared["b_base_c"] = np.ascontiguousarray(np.asarray(i["L0_ffn_b_base"], np.float32).reshape(NT, 128).T)
    shared["w_small_t"] = np.ascontiguousarray(
        np.concatenate([np.asarray(i["L0_ffn_A_s"], np.float32),
                        np.asarray(i["L0_ffn_W_router"], np.float32),
                        np.asarray(i["L0_ffn_A_r"], np.float32).transpose(1, 0, 2).reshape(D, NE * R)], axis=1))
    shared["b_rout"] = np.asarray(i["L0_ffn_b_router"], np.float32).reshape(1, NE)
    shared["b_s_w"] = np.ascontiguousarray(np.asarray(i["L0_ffn_B_s"], np.float32))
    shared["b_r_w"] = np.ascontiguousarray(np.asarray(i["L0_ffn_B_r"], np.float32).reshape(NE * R, D))
    ins = []
    for c in range(n_cores):
        lo = c * tpc - HALO
        sl = np.zeros((D, TW), dtype=np.float32)
        src_lo = max(0, lo)
        sl[:, (src_lo - lo):] = xT[:, src_lo:c * tpc + tpc]
        m = dict(shared)
        m["x_fm"] = np.ascontiguousarray(sl)
        m["csel"] = (np.arange(8) < c).astype(np.float32).reshape(1, 8)
        m["hmask"] = np.array([[0.0 if c == 0 else 1.0]], dtype=np.float32)
        ins.append(m)
    return ins


def kernel(**inputs):
    n_cores, tpc = 8, 1024
    nc, _ = build_program(n_cores, tpc, dbg=0)
    in_maps = _layout(inputs, n_cores, tpc)
    res = run_bass_kernel_spmd(nc, in_maps, core_ids=list(range(n_cores)))
    parts = [np.asarray(r["out_fm"]).T for r in res.results]
    out = np.concatenate(parts, axis=0)[None, :, :]
    return np.ascontiguousarray(out.astype(np.float32))


# version 04
# speedup vs baseline: 1.0296x; 1.0296x over previous
import numpy as np
import concourse.bass as bass
import concourse.mybir as mybir
from concourse.bass_utils import run_bass_kernel_spmd

F32 = mybir.dt.float32
AF = mybir.ActivationFunctionType
ALU = mybir.AluOpType
AX = mybir.AxisListType.X

D = 2048
H = 8
DH = 128
KEY = 1024
NE = 6
R = 16
HALO = 3
C = 128
CT = C // 2
NT = D // 128
NQ = KEY // 128
SM = R + NE + NE * R
EPS_LN = float(np.float32(1e-5))
EPS_RMS = float(np.float32(1e-5))
INV_D = float(np.float32(1.0) / np.float32(D))
INV_DH = float(np.float32(1.0) / np.float32(DH))
INV_SQRT_DH = float(np.float32(1.0) / np.float32(np.sqrt(np.float32(128.0))))


def spans(width, maxw=512):
    out, o = [], 0
    while o < width:
        w = min(maxw, width - o)
        out.append((o, w))
        o += w
    return out


def build_program(n_cores, tpc, dbg=0):
    TW = tpc + HALO
    NCH = tpc // CT
    NTK = (TW + 127) // 128
    nc = bass.Bass("TRN2", target_bir_lowering=False)

    x_fm = nc.dram_tensor("x_fm", [D, TW], F32, kind="ExternalInput").ap()
    wq_t = nc.dram_tensor("wq_t", [D, KEY], F32, kind="ExternalInput").ap()
    wk_t = nc.dram_tensor("wk_t", [D, D], F32, kind="ExternalInput").ap()
    wv_t = nc.dram_tensor("wv_t", [D, D], F32, kind="ExternalInput").ap()
    wbwa_t = nc.dram_tensor("wbwa_t", [D, 3 * H], F32, kind="ExternalInput").ap()
    wg_t = nc.dram_tensor("wg_t", [D, KEY], F32, kind="ExternalInput").ap()
    wo_t = nc.dram_tensor("wo_t", [KEY, D], F32, kind="ExternalInput").ap()
    a_log = nc.dram_tensor("a_log", [1, H], F32, kind="ExternalInput").ap()
    dt_b = nc.dram_tensor("dt_b", [1, H], F32, kind="ExternalInput").ap()
    conv_q = nc.dram_tensor("conv_q", [KEY, 4], F32, kind="ExternalInput").ap()
    conv_k = nc.dram_tensor("conv_k", [D, 4], F32, kind="ExternalInput").ap()
    conv_v = nc.dram_tensor("conv_v", [D, 4], F32, kind="ExternalInput").ap()
    onw = nc.dram_tensor("onw", [1, DH], F32, kind="ExternalInput").ap()
    w_base_t = nc.dram_tensor("w_base_t", [D, D], F32, kind="ExternalInput").ap()
    b_base_c = nc.dram_tensor("b_base_c", [128, NT], F32, kind="ExternalInput").ap()
    w_small_t = nc.dram_tensor("w_small_t", [D, SM], F32, kind="ExternalInput").ap()
    b_rout = nc.dram_tensor("b_rout", [1, NE], F32, kind="ExternalInput").ap()
    b_s_w = nc.dram_tensor("b_s_w", [R, D], F32, kind="ExternalInput").ap()
    b_r_w = nc.dram_tensor("b_r_w", [NE * R, D], F32, kind="ExternalInput").ap()
    csel = nc.dram_tensor("csel", [1, 8], F32, kind="ExternalInput").ap()
    hmask = nc.dram_tensor("hmask", [1, 1], F32, kind="ExternalInput").ap()
    out_fm = nc.dram_tensor("out_fm", [D, tpc], F32, kind="ExternalOutput").ap()

    x1_sp = nc.dram_tensor("x1_sp", [D, TW], F32, kind="Internal").ap()
    k_sp = nc.dram_tensor("k_sp", [D, TW], F32, kind="Internal").ap()
    v_sp = nc.dram_tensor("v_sp", [D, TW], F32, kind="Internal").ap()
    q_sp = nc.dram_tensor("q_sp", [KEY, TW], F32, kind="Internal").ap()
    gg_sp = nc.dram_tensor("gg_sp", [KEY, tpc], F32, kind="Internal").ap()
    cc_in = [nc.dram_tensor(f"cc_in{h}", [128, 2 * DH], F32, kind="Internal").ap() for h in range(H)]
    cc_out = [nc.dram_tensor(f"cc_out{h}", [n_cores, 128, 2 * DH], F32, kind="Internal",
                             addr_space="Local").ap() for h in range(H)]

    DBG = {}
    if dbg:
        def dout(name, shape):
            DBG[name] = nc.dram_tensor(name, shape, F32, kind="ExternalOutput").ap()
        dout("d_x1", [D, TW]); dout("d_xln", [D, TW])
        dout("d_k", [D, TW]); dout("d_v", [D, TW]); dout("d_q", [KEY, TW])
        dout("d_b", [2 * H, TW]); dout("d_g", [H, TW]); dout("d_cum", [H, TW])
        dout("d_kint", [DH, C]); dout("d_ktm", [C, DH]); dout("d_rhs", [C, 2 * DH])
        dout("d_gst", [C, C]); dout("d_art", [128, 4 * DH]); dout("d_bup", [C, C]); dout("d_alo", [C, C]); dout("d_kbf", [DH, C]); dout("d_kbt", [C, DH])
        dout("d_sin", [DH, DH]); dout("d_och", [C, DH]); dout("d_ofm", [KEY, tpc])
        dout("d_x2", [D, tpc])

    guards = []

    def sb(name, shape, dtype=F32):
        g = nc.sbuf_tensor(name, shape, dtype); t = g.__enter__(); guards.append(g); return t

    def psu(name, shape, dtype=F32):
        g = nc.psum_tensor(name, shape, dtype); t = g.__enter__(); guards.append(g); return t

    big0 = sb("big0", [128, NT * TW])
    big1 = sb("big1", [128, NT * TW])
    work = sb("work", [128, 2 * 128])
    wst = sb("wst", [128, 2 * 1280])
    bbase = sb("bbase", [128, NT])
    consts = sb("consts", [128, 1280])
    crow1 = sb("crow1", [1, 280])
    SELN = max(NTK, H)
    selb = sb("selb", [SELN, SELN * 128])
    sel127 = sb("sel127", [128, 128])
    stepcs = sb("stepcs", [128, NCH * 40])
    rowcs = sb("rowcs", [H, NCH * 128])
    accb = sb("accb", [128, NCH * 8])
    nrmc = sb("nrmc", [128, 8])
    selbc = sb("selbc", [128, 8])
    onrm = sb("onrm", [128, DH])
    poolh = sb("poolh", [128, 7400])
    pool = poolh[:, :]
    smw = pool[:, 0:NT * SM]
    sfm = pool[0:R, 1888:1888 + TW]
    rfm = pool[0:NE, 2915:2915 + TW]
    afm = pool[0:NE * R, 3942:3942 + TW]
    hwb = pool[0:NE * R, 4969:4969 + TW]
    wfm = pool[0:NE, 5996:5996 + TW]
    statc = pool[:, 7023:7023 + 5 * NTK]
    statf = pool[0:NTK, 7078:7078 + 256]
    gtm = pool[:, 7338:7338 + 32]
    bgw = pool[:, 0:NT * 24]
    cvw = pool[:, 384:384 + (2 * NT + NQ) * 4]
    bgb = pool[0:2 * H, 544:544 + TW]
    gb = pool[0:H, 1571:1571 + TW]
    cumb = pool[0:H, 2598:2598 + TW]
    kint = pool[:, 0:2 * C]
    qint = pool[:, 256:256 + 2 * C]
    khf = pool[:, 512:512 + 2 * C]
    qhf = pool[:, 768:768 + 2 * C]
    ktm = pool[:, 1024:1024 + 2 * DH]
    kbtm = pool[:, 1280:1280 + 2 * DH]
    rhsb = pool[:, 1536:1536 + 2 * 256]
    apw = pool[:, 2048:2048 + 2 * C]
    bpw = pool[:, 2304:2304 + 7 * C]
    dxp = pool[:, 3200:3200 + C]
    gst = pool[:, 3328:3328 + C]
    kbar = pool[:, 3456:3456 + DH]
    qtl = pool[:, 3584:3584 + C]
    scr = pool[:, 3712:3712 + 6 * C]
    comp = pool[:, 4480:4480 + 3 * DH]
    ccsb = pool[:, 4864:4864 + 2 * DH]
    gsb = pool[:, 5120:5120 + 2 * 256]
    bscr = pool[:, 5632:5632 + 4 * DH]
    prefx = pool[:, 6144:6144 + DH]
    stt = pool[:, 6272:6272 + DH]
    ggch = pool[:, 6400:6400 + 2 * CT]
    och = pool[:, 6528:6528 + 2 * DH]
    qatm = pool[:, 6784:6784 + DH]
    kbfm = pool[:, 6912:6912 + C]
    cwork = big0

    ARTS = big0

    def art(h, c, slot):
        return ARTS[:, (h % 2) * (NCH * 4 * DH) + c * 4 * DH + slot * DH:
                    (h % 2) * (NCH * 4 * DH) + c * 4 * DH + (slot + 1) * DH]

    def khv(h, part):
        b = ((h % 2) * 2 + part) * TW
        return big1[:, b:b + TW]

    def qhv(h):
        b = 4 * TW + (h % 2) * TW
        return big1[:, b:b + TW]

    def vchv(c, part):
        b = 6 * TW + ((c % 2) * 2 + part) * CT
        return big1[:, b:b + CT]

    OFM_OFF = 8 * TW

    GAP0 = 6 * TW + 4 * CT
    SOLVE_DUAL = False
    SDEPTH = 2 if SOLVE_DUAL else 1

    def bpwv(idx):
        return bpw if (idx % 2 == 0 or not SOLVE_DUAL) else big1[:, GAP0:GAP0 + 7 * C]

    def apwv(idx):
        return apw if (idx % 2 == 0 or not SOLVE_DUAL) else big1[:, GAP0 + 7 * C:GAP0 + 9 * C]

    def gstv(idx):
        return gst if idx % 2 == 0 else pool[:, 7168:7168 + C]

    def ofmv(h):
        return big1[:, OFM_OFF + h * tpc:OFM_OFF + (h + 1) * tpc]

    pLN = psu("pLN", [128, 512])
    pBC = psu("pBC", [128, 512])
    pFF = psu("pFF", [128, 512])
    pFF2 = psu("pFF2", [128, 512])
    pRW = psu("pRW", [128, 512])
    pSQ = psu("pSQ", [128, 512])
    pAP = psu("pAP", [128, 512])
    pMP = psu("pMP", [128, 512])

    sems, cnt, ev = {}, {}, {}
    for s in ["qS", "qA", "qG", "tT", "tV", "tS", "tG", "cc"]:
        g = nc.semaphore(s); sems[s] = g.__enter__(); guards.append(g); cnt[s] = 0

    def inc(s, by=1):
        cnt[s] += by
        return cnt[s]

    def mark(name, s):
        ev[name] = (s, cnt[s])

    def wt(eng, name):
        s, v = ev[name]
        if v > 0:
            eng.wait_ge(sems[s], v)

    def barrier(tagb):
        vals = [(s, cnt[s]) for s in ["tT", "tV", "tS", "tG", "qS", "qA", "qG", "cc"] if cnt[s] > 0]

        @blk.tensor
        def _(e):
            for s, v in vals:
                e.wait_ge(sems[s], v)
            e.matmul(pLN[0:1, 40:41], ONESC[0:1, 0:1], ONESC[0:1, 0:1], start=True, stop=True).then_inc(sems["tT"], 1)
        inc("tT")
        mark(f"bar_{tagb}_t", "tT")

        @blk.vector
        def _(e):
            for s, v in vals:
                e.wait_ge(sems[s], v)
            e.sem_inc(sems["tV"], 1)
        inc("tV")

        @blk.scalar
        def _(e):
            for s, v in vals:
                e.wait_ge(sems[s], v)
            e.sem_inc(sems["tS"], 1)
        inc("tS")

        @blk.sync
        def _(e):
            for s, v in vals:
                e.wait_ge(sems[s], v)

    def gate(eng, qn):
        if cnt[qn] > 0:
            eng.wait_ge(sems[qn], cnt[qn])

    blk = nc.Block().__enter__()
    guards.append(blk)

    I128 = consts[:, 0:128]
    E_evT = consts[0:64, 128:256]
    E_odT = consts[0:64, 720:848]
    E_boT = consts[0:64, 256:384]
    E6 = consts[0:NE, 448:544]
    NEXPA = consts[0:H, 560:561]
    DTC = consts[0:H, 561:562]
    BRT = consts[0:128, 576:582]
    ONESC = consts[:, 640:641]
    EPSC = consts[:, 648:649]
    INVDHC = consts[:, 649:650]
    HMC = consts[:, 652:653]
    EPSN = consts[:, 653:654]
    ONES_T = consts[0:H, 656:656 + CT]
    MSU = consts[:, 896:1024]
    MSL = consts[:, 1024:1152]
    MGU = consts[:, 1152:1280]
    ONESR = crow1[0:1, 0:128]

    @blk.sync
    def _(e):
        gate(e, "qS")
        for f in range(NT):
            e.dma_start(out=big0[:, f * TW:(f + 1) * TW], in_=x_fm[f * 128:(f + 1) * 128, :]).then_inc(sems["qS"], 16)
        inc("qS", 16 * NT)
        mark("x_loaded", "qS")

    @blk.scalar
    def _(e):
        n = 0
        e.wait_ge(sems["tG"], 1)
        gate(e, "qA")
        for f in range(NT):
            e.dma_start(out=smw[:, f * SM:(f + 1) * SM], in_=w_small_t[f * 128:(f + 1) * 128, :]).then_inc(sems["qA"], 16); n += 1

        e.dma_start(out=bbase[:, :], in_=b_base_c).then_inc(sems["qA"], 16); n += 1
        e.dma_start(out=crow1[0:1, 256:256 + NE], in_=b_rout).then_inc(sems["qA"], 16); n += 1
        e.dma_start(out=consts[0:1, 544:544 + H], in_=a_log).then_inc(sems["qA"], 16); n += 1
        e.dma_start(out=consts[0:1, 552:552 + H], in_=dt_b).then_inc(sems["qA"], 16); n += 1
        e.dma_start(out=crow1[0:1, 128:128 + DH], in_=onw).then_inc(sems["qA"], 16); n += 1
        e.dma_start(out=crow1[0:1, 262:270], in_=csel).then_inc(sems["qA"], 16); n += 1
        e.dma_start(out=crow1[0:1, 270:271], in_=hmask).then_inc(sems["qA"], 16); n += 1

        inc("qA", 16 * n)
        mark("smalls_loaded", "qA")

    @blk.gpsimd
    def _(e):
        e.memset(consts[:, 0:544], 1.0)
        e.memset(consts[:, 544:640], 0.0)
        e.memset(consts[:, 640:1280], 1.0)
        e.memset(crow1[0:1, 0:128], 1.0)
        e.memset(selb[:, :], 1.0)
        e.memset(sel127[:, :], 1.0)
        e.memset(poolh[:, :], 0.0)
        e.memset(work[:, :], 0.0)
        e.drain()
        e.memset(EPSC, EPS_LN)
        e.memset(INVDHC, INV_DH)
        e.memset(EPSN, 1e-24)
        e.drain()
        e.affine_select(I128, I128, pattern=[[1, 128]], compare_op=ALU.is_equal, fill=0.0, base=0, channel_multiplier=-1)
        e.memset(E_odT, 1.0)
        e.drain()
        e.affine_select(E_evT, E_evT, pattern=[[1, 128]], compare_op=ALU.is_equal, fill=0.0, base=0, channel_multiplier=-2)
        e.affine_select(E_odT, E_odT, pattern=[[1, 128]], compare_op=ALU.is_equal, fill=0.0, base=-1, channel_multiplier=-2)
        e.affine_select(E_boT, E_boT, pattern=[[1, 128]], compare_op=ALU.is_ge, fill=0.0, base=0, channel_multiplier=-2)
        e.affine_select(E6, E6, pattern=[[1, 96]], compare_op=ALU.is_ge, fill=0.0, base=0, channel_multiplier=-16)
        e.affine_select(MSU, MSU, pattern=[[1, 128]], compare_op=ALU.is_ge, fill=0.0, base=-1, channel_multiplier=-1)
        e.affine_select(MSL, MSL, pattern=[[-1, 128]], compare_op=ALU.is_ge, fill=0.0, base=-1, channel_multiplier=1)
        e.affine_select(MGU, MGU, pattern=[[1, 128]], compare_op=ALU.is_ge, fill=0.0, base=0, channel_multiplier=-1)
        e.affine_select(selb[:, :], selb[:, :], pattern=[[1, SELN * 128]], compare_op=ALU.is_ge, fill=0.0, base=0, channel_multiplier=-128)
        e.drain()
        e.affine_select(selb[:, :], selb[:, :], pattern=[[-1, SELN * 128]], compare_op=ALU.is_ge, fill=0.0, base=127, channel_multiplier=128)
        e.affine_select(sel127[:, :], sel127[:, :], pattern=[[0, 128]], compare_op=ALU.is_equal, fill=0.0, base=-127, channel_multiplier=1)
        e.drain()
        e.affine_select(E_boT, E_boT, pattern=[[-1, 128]], compare_op=ALU.is_ge, fill=0.0, base=1, channel_multiplier=2)
        e.affine_select(E6, E6, pattern=[[-1, 96]], compare_op=ALU.is_ge, fill=0.0, base=15, channel_multiplier=16)
        e.drain()
        e.sem_inc(sems["tG"], 1)
        inc("tG")
        mark("consts_ready", "tG")

    @blk.tensor
    def _(e):
        wt(e, "smalls_loaded")
        wt(e, "consts_ready")
        e.matmul(pBC[:, 0:NE], ONESR, crow1[0:1, 256:256 + NE], start=True, stop=True)
        e.matmul(pBC[:, 384:392], ONESR, crow1[0:1, 262:270], start=True, stop=True)
        e.matmul(pBC[:, 392:393], ONESR, crow1[0:1, 270:271], start=True, stop=True)
        e.transpose(pBC[0:DH, 264:265], crow1[0:1, 128:128 + DH], I128[0:1, 0:1])
        e.transpose(pBC[0:H, 256:257], consts[0:1, 544:544 + H], I128[0:1, 0:1])
        e.transpose(pBC[0:H, 260:261], consts[0:1, 552:552 + H], I128[0:1, 0:1]).then_inc(sems["tT"], 1)
        inc("tT")
        mark("tiny_tr", "tT")

    @blk.vector
    def _(e):
        wt(e, "tiny_tr")
        e.tensor_copy(BRT, pBC[:, 0:NE])
        e.tensor_copy(onrm[:, 0:1], pBC[0:DH, 264:265])
        e.tensor_copy(selbc[:, :], pBC[:, 384:392])
        e.tensor_copy(HMC, pBC[:, 392:393])
        e.tensor_copy(DTC, pBC[0:H, 260:261])
        e.drain()
        e.sem_inc(sems["tV"], 1)
        inc("tV")
        mark("tiny_v", "tV")

    @blk.scalar
    def _(e):
        wt(e, "tiny_v")
        e.activation(NEXPA, pBC[0:H, 256:257], AF.Exp)
        e.drain()
        e.sem_inc(sems["tS"], 1)
        inc("tS")
        mark("tiny_s", "tS")

    @blk.vector
    def _(e):
        wt(e, "tiny_s")
        e.tensor_scalar(NEXPA, NEXPA, -1.0, None, ALU.mult)
        e.drain()
        e.sem_inc(sems["tV"], 1)
        inc("tV")
        mark("consts2", "tV")

    SEQ = {"ln": 0}

    def emit_ln(src, dst, width, ready_evs):
        tch = spans(width, 128)
        tag = f"ln{SEQ['ln']}"; SEQ["ln"] += 1
        ncol = len(tch)

        for ci, (o, w) in enumerate(tch):
            for f in range(NT):
                @blk.scalar
                def _(e, ci=ci, o=o, w=w, f=f):
                    for evn in ready_evs:
                        wt(e, evn)
                    k = ci * NT + f
                    if k >= 2:
                        pci, pf = divmod(k - 2, NT)
                        wt(e, f"{tag}_sq{pci}_{pf}")
                    e.activation(work[:, (f % 2) * 128:(f % 2) * 128 + w],
                                 src[:, f * width + o:f * width + o + w], AF.Square)
                    e.drain()
                    e.sem_inc(sems["tS"], 1)
                    inc("tS")
                    mark(f"{tag}_sqs{ci}_{f}", "tS")

                @blk.tensor
                def _(e, ci=ci, o=o, w=w, f=f):
                    wt(e, f"{tag}_sqs{ci}_{f}")
                    if ci == 0 and f == 0:
                        wt(e, "consts2")
                    e.matmul(pLN[0:w, ci:ci + 1], src[:, f * width + o:f * width + o + w], ONESC,
                             start=(f == 0), stop=(f == NT - 1))
                    e.matmul(pBC[0:w, ci:ci + 1], work[:, (f % 2) * 128:(f % 2) * 128 + w], ONESC,
                             start=(f == 0), stop=(f == NT - 1)).then_inc(sems["tT"], 1)
                    inc("tT")
                    mark(f"{tag}_sq{ci}_{f}", "tT")

        @blk.vector
        def _(e):
            wt(e, f"{tag}_sq{len(tch) - 1}_{NT - 1}")
            for ci, (o, w) in enumerate(tch):
                e.tensor_scalar(statc[0:w, ci:ci + 1], pLN[0:w, ci:ci + 1], INV_D, None, ALU.mult)
                e.tensor_scalar(statc[0:w, NTK + ci:NTK + ci + 1], pBC[0:w, ci:ci + 1], INV_D, None, ALU.mult)
            e.drain()
            e.tensor_tensor(statc[:, 2 * NTK:2 * NTK + ncol], statc[:, 0:ncol], statc[:, 0:ncol], ALU.mult)
            e.drain()
            e.tensor_sub(statc[:, NTK:NTK + ncol], statc[:, NTK:NTK + ncol], statc[:, 2 * NTK:2 * NTK + ncol])
            e.drain()
            e.sem_inc(sems["tV"], 1)
            inc("tV")
            mark(f"{tag}_var", "tV")

        @blk.scalar
        def _(e):
            wt(e, f"{tag}_var")
            e.activation(statc[:, 2 * NTK:2 * NTK + ncol], statc[:, NTK:NTK + ncol], AF.Sqrt, bias=EPSC)
            e.drain()
            e.sem_inc(sems["tS"], 1)
            inc("tS")
            mark(f"{tag}_inv", "tS")

        @blk.vector
        def _(e):
            wt(e, f"{tag}_inv")
            e.reciprocal(statc[:, 2 * NTK:2 * NTK + ncol], statc[:, 2 * NTK:2 * NTK + ncol])
            e.drain()
            e.tensor_tensor(statc[:, 3 * NTK:3 * NTK + ncol], statc[:, 0:ncol],
                            statc[:, 2 * NTK:2 * NTK + ncol], ALU.mult)
            e.drain()
            e.sem_inc(sems["tV"], 1)
            inc("tV")
            mark(f"{tag}_minv", "tV")

        @blk.tensor
        def _(e):
            wt(e, f"{tag}_minv")
            e.transpose(pLN[0:ncol, 128:256], statc[:, 2 * NTK:2 * NTK + ncol], I128)
            e.transpose(pLN[0:ncol, 256:384], statc[:, 3 * NTK:3 * NTK + ncol], I128).then_inc(sems["tT"], 1)
            inc("tT")
            mark(f"{tag}_strows", "tT")

        @blk.vector
        def _(e):
            wt(e, f"{tag}_strows")
            e.tensor_copy(statf[0:ncol, 0:128], pLN[0:ncol, 128:256])
            e.tensor_copy(statf[0:ncol, 128:256], pLN[0:ncol, 256:384])
            e.drain()
            e.sem_inc(sems["tV"], 1)
            inc("tV")
            mark(f"{tag}_stf", "tV")

        for ci, (o, w) in enumerate(tch):
            @blk.tensor
            def _(e, ci=ci, o=o, w=w):
                wt(e, f"{tag}_stf")
                if ci >= 2:
                    wt(e, f"{tag}_ap{ci - 2}")
                pbk = pBC if ci % 2 == 0 else pLN
                e.matmul(pbk[:, 0:w], selb[0:NTK, ci * 128:ci * 128 + 128], statf[0:NTK, 0:w], start=True, stop=True)
                e.matmul(pbk[:, 128:128 + w], selb[0:NTK, ci * 128:ci * 128 + 128], statf[0:NTK, 128:128 + w],
                         start=True, stop=True).then_inc(sems["tT"], 1)
                inc("tT")
                mark(f"{tag}_bc{ci}", "tT")

            @blk.vector
            def _(e, ci=ci, o=o, w=w):
                wt(e, f"{tag}_bc{ci}")
                pbk = pBC if ci % 2 == 0 else pLN
                for f in range(NT):
                    e.tensor_tensor(dst[:, f * width + o:f * width + o + w],
                                    src[:, f * width + o:f * width + o + w],
                                    pbk[:, 0:w], ALU.mult)
                e.drain()
                for f in range(NT):
                    e.tensor_sub(dst[:, f * width + o:f * width + o + w],
                                 dst[:, f * width + o:f * width + o + w],
                                 pbk[:, 128:128 + w])
                e.drain()
                e.sem_inc(sems["tV"], 1)
                inc("tV")
                mark(f"{tag}_ap{ci}", "tV")

        return f"{tag}_ap{len(tch) - 1}"

    def emit_ffn(xarena, starena, width, ready_evs, tag):
        tch = spans(width, 128)
        sp512 = spans(width, 512)

        for si, (o, w) in enumerate(sp512):
            @blk.tensor
            def _(e, si=si, o=o, w=w):
                for evn in ready_evs:
                    wt(e, evn)
                wt(e, "consts2")
                if si > 0:
                    wt(e, f"{tag}_spe{si - 1}")
                for f in range(NT):
                    e.matmul(pFF[0:R, 0:w], smw[:, f * SM:f * SM + R],
                             xarena[:, f * width + o:f * width + o + w],
                             start=(f == 0), stop=(f == NT - 1))
                for f in range(NT):
                    e.matmul(pLN[0:NE, 0:w], smw[:, f * SM + R:f * SM + R + NE],
                             xarena[:, f * width + o:f * width + o + w],
                             start=(f == 0), stop=(f == NT - 1))
                for f in range(NT):
                    mm = e.matmul(pFF2[0:NE * R, 0:w], smw[:, f * SM + R + NE:(f + 1) * SM],
                                  xarena[:, f * width + o:f * width + o + w],
                                  start=(f == 0), stop=(f == NT - 1))
                mm.then_inc(sems["tT"], 1)
                inc("tT")
                mark(f"{tag}_sp{si}", "tT")

            @blk.vector
            def _(e, si=si, o=o, w=w):
                wt(e, f"{tag}_sp{si}")
                e.tensor_copy(sfm[:, o:o + w], pFF[0:R, 0:w])
                e.tensor_copy(rfm[:, o:o + w], pLN[0:NE, 0:w])
                e.tensor_copy(afm[:, o:o + w], pFF2[0:NE * R, 0:w])
                e.drain()
                e.sem_inc(sems["tV"], 1)
                inc("tV")
                mark(f"{tag}_spe{si}", "tV")

        for ci, (o, w) in enumerate(tch):
            @blk.tensor
            def _(e, ci=ci, o=o, w=w):
                wt(e, f"{tag}_spe{len(sp512) - 1}")
                if ci > 0:
                    wt(e, f"{tag}_wf{ci - 1}")
                e.transpose(pBC[0:w, 0:NE], rfm[0:NE, o:o + w], I128[0:NE, 0:NE]).then_inc(sems["tT"], 1)
                inc("tT")
                mark(f"{tag}_rt{ci}", "tT")

            @blk.vector
            def _(e, ci=ci, o=o, w=w):
                wt(e, f"{tag}_rt{ci}")
                e.tensor_tensor(gtm[0:w, 0:NE], pBC[0:w, 0:NE], BRT[0:w, :], ALU.add)
                e.drain()
                e.tensor_reduce(gtm[0:w, 8:9], gtm[0:w, 0:NE], AX, ALU.max, negate=True)
                e.drain()
                e.sem_inc(sems["tV"], 1)
                inc("tV")
                mark(f"{tag}_z{ci}", "tV")

            @blk.scalar
            def _(e, ci=ci, o=o, w=w):
                wt(e, f"{tag}_z{ci}")
                e.activation(gtm[0:w, 16:16 + NE], gtm[0:w, 0:NE], AF.Exp, bias=gtm[0:w, 8:9])
                e.drain()
                e.sem_inc(sems["tS"], 1)
                inc("tS")
                mark(f"{tag}_e{ci}", "tS")

            @blk.vector
            def _(e, ci=ci, o=o, w=w):
                wt(e, f"{tag}_e{ci}")
                eT = gtm[0:w, 16:16 + NE]
                e.tensor_reduce(gtm[0:w, 9:10], eT, AX, ALU.add)
                e.tensor_reduce(gtm[0:w, 10:11], eT, AX, ALU.max)
                e.drain()
                e.reciprocal(gtm[0:w, 11:12], gtm[0:w, 9:10])
                e.tensor_scalar(gtm[0:w, 24:24 + NE], eT, gtm[0:w, 10:11], None, ALU.is_ge)
                e.drain()
                e.scalar_tensor_tensor(gtm[0:w, 24:24 + NE], gtm[0:w, 24:24 + NE], -2.0, eT, ALU.mult, ALU.add)
                e.drain()
                e.tensor_reduce(gtm[0:w, 12:13], gtm[0:w, 24:24 + NE], AX, ALU.max)
                e.drain()
                e.tensor_scalar(gtm[0:w, 24:24 + NE], eT, gtm[0:w, 12:13], None, ALU.is_ge)
                e.drain()
                e.scalar_tensor_tensor(gtm[0:w, 0:NE], eT, gtm[0:w, 11:12], gtm[0:w, 24:24 + NE], ALU.mult, ALU.mult)
                e.drain()
                e.sem_inc(sems["tV"], 1)
                inc("tV")
                mark(f"{tag}_w{ci}", "tV")

            @blk.tensor
            def _(e, ci=ci, o=o, w=w):
                wt(e, f"{tag}_w{ci}")
                e.transpose(pBC[0:NE, 128:128 + w], gtm[0:w, 0:NE], I128[0:w, 0:w]).then_inc(sems["tT"], 1)
                inc("tT")
                mark(f"{tag}_wt{ci}", "tT")

            @blk.vector
            def _(e, ci=ci, o=o, w=w):
                wt(e, f"{tag}_wt{ci}")
                e.tensor_copy(wfm[:, o:o + w], pBC[0:NE, 128:128 + w])
                e.drain()
                e.sem_inc(sems["tV"], 1)
                inc("tV")
                mark(f"{tag}_wf{ci}", "tV")

        for si, (o, w) in enumerate(sp512):
            @blk.tensor
            def _(e, si=si, o=o, w=w):
                wt(e, f"{tag}_wf{len(tch) - 1}")
                if si > 0:
                    wt(e, f"{tag}_hwe{si - 1}")
                e.matmul(pFF[0:NE * R, 0:w], E6, wfm[:, o:o + w], start=True, stop=True).then_inc(sems["tT"], 1)
                inc("tT")
                mark(f"{tag}_hw{si}", "tT")

            @blk.vector
            def _(e, si=si, o=o, w=w):
                wt(e, f"{tag}_hw{si}")
                e.tensor_tensor(hwb[:, o:o + w], afm[:, o:o + w], pFF[0:NE * R, 0:w], ALU.mult)
                e.drain()
                e.sem_inc(sems["tV"], 1)
                inc("tV")
                mark(f"{tag}_hwe{si}", "tV")

        def wld_batch(e, fo, hv):
            qn = "qS" if (fo * 2 + hv) % 2 == 0 else "qG"
            gate(e, qn)
            slot = (fo * 2 + hv) % 2
            nb = 0
            for f8 in range(8):
                f = hv * 8 + f8
                e.dma_start(out=wst[:, slot * 1280 + f8 * 128:slot * 1280 + (f8 + 1) * 128],
                            in_=w_base_t[f * 128:(f + 1) * 128, fo * 128:(fo + 1) * 128]).then_inc(sems[qn], 16)
                nb += 1
            if hv == 1:
                e.dma_start(out=wst[0:R, slot * 1280 + 1024:slot * 1280 + 1152],
                            in_=b_s_w[:, fo * 128:(fo + 1) * 128]).then_inc(sems[qn], 16)
                e.dma_start(out=wst[0:NE * R, slot * 1280 + 1152:slot * 1280 + 1280],
                            in_=b_r_w[:, fo * 128:(fo + 1) * 128]).then_inc(sems[qn], 16)
                nb += 2
            inc(qn, 16 * nb)
            mark(f"{tag}_wld{fo}_{hv}", qn)

        @blk.sync
        def _(e):
            for evn in ready_evs:
                wt(e, evn)
            wld_batch(e, 0, 0)
            wld_batch(e, 0, 1)

        nsp = len(sp512)
        BANKS = [pFF, pFF2, pLN]
        BOFF = [0, 0, 64]

        for fo in range(NT):
            for hv in range(2):
                @blk.tensor
                def _(e, fo=fo, hv=hv):
                    wt(e, f"{tag}_wld{fo}_{hv}")
                    wt(e, f"{tag}_hwe{nsp - 1}")
                    if hv == 0 and fo >= 1:
                        for psi in range(nsp):
                            wt(e, f"{tag}_be{fo - 1}_{psi}")
                    slot = (fo * 2 + hv) % 2
                    last = None
                    for si, (o, w) in enumerate(sp512):
                        pb = BANKS[si]
                        po = BOFF[si]
                        for f8 in range(8):
                            last = e.matmul(pb[:, po:po + w], wst[:, slot * 1280 + f8 * 128:slot * 1280 + (f8 + 1) * 128],
                                            xarena[:, (hv * 8 + f8) * width + o:(hv * 8 + f8) * width + o + w],
                                            start=(hv == 0 and f8 == 0), stop=False)
                        if hv == 1:
                            e.matmul(pb[:, po:po + w], wst[0:R, slot * 1280 + 1024:slot * 1280 + 1152],
                                     sfm[:, o:o + w], start=False, stop=False)
                            last = e.matmul(pb[:, po:po + w], wst[0:NE * R, slot * 1280 + 1152:slot * 1280 + 1280],
                                            hwb[:, o:o + w], start=False, stop=True)
                    last.then_inc(sems["tT"], 1)
                    inc("tT")
                    mark(f"{tag}_b{fo}_{hv}", "tT")

                nxt = fo * 2 + hv + 2
                if nxt < NT * 2:
                    @blk.sync
                    def _(e, fo=fo, hv=hv, nxt=nxt):
                        wt(e, f"{tag}_b{fo}_{hv}")
                        wld_batch(e, nxt // 2, nxt % 2)

            for si, (o, w) in enumerate(sp512):
                @blk.vector
                def _(e, fo=fo, si=si, o=o, w=w):
                    wt(e, f"{tag}_b{fo}_{1}")
                    pb = BANKS[si]
                    po = BOFF[si]
                    e.tensor_scalar(scr[:, 0:w], pb[:, po:po + w], bbase[:, fo:fo + 1], 0.5, ALU.add, ALU.mult)
                    e.drain()
                    e.tensor_tensor(starena[:, fo * width + o:fo * width + o + w],
                                    starena[:, fo * width + o:fo * width + o + w], scr[:, 0:w], ALU.add)
                    e.drain()
                    e.sem_inc(sems["tV"], 1)
                    inc("tV")
                    mark(f"{tag}_be{fo}_{si}", "tV")

        return f"{tag}_be{NT - 1}_{nsp - 1}"

    ln0_done = emit_ln(big0, big1, TW, ["x_loaded"])
    ffn1_done = emit_ffn(big1, big0, TW, [ln0_done], "fA")

    if dbg:
        @blk.sync
        def _(e):
            wt(e, ffn1_done)
            gate(e, "qS")
            for f in range(NT):
                e.dma_start(out=DBG["d_x1"][f * 128:(f + 1) * 128, :], in_=big0[:, f * TW:(f + 1) * TW]).then_inc(sems["qS"], 16)
            inc("qS", 16 * NT)
            mark("dbg_x1", "qS")

    barrier("ffn1")
    ln1_done = emit_ln(big0, big1, TW, [ffn1_done])

    @blk.scalar
    def _(e):
        wt(e, ln1_done)
        gate(e, "qA")
        for f in range(NT):
            e.dma_start(out=x1_sp[f * 128:(f + 1) * 128, :], in_=big0[:, f * TW:(f + 1) * TW]).then_inc(sems["qA"], 16)
        inc("qA", 16 * NT)
        mark("x1_spilled", "qA")

    @blk.scalar
    def _(e):
        wt(e, ln1_done)
        wt(e, "x1_spilled")
        gate(e, "qA")
        nbg = 0
        for f in range(NT):
            e.dma_start(out=bgw[:, f * 24:(f + 1) * 24], in_=wbwa_t[f * 128:(f + 1) * 128, :]).then_inc(sems["qA"], 16); nbg += 1
        for f in range(NT):
            e.dma_start(out=cvw[:, f * 4:(f + 1) * 4], in_=conv_k[f * 128:(f + 1) * 128, :]).then_inc(sems["qA"], 16); nbg += 1
        for f in range(NT):
            e.dma_start(out=cvw[:, (NT + f) * 4:(NT + f + 1) * 4], in_=conv_v[f * 128:(f + 1) * 128, :]).then_inc(sems["qA"], 16); nbg += 1
        for f in range(NQ):
            e.dma_start(out=cvw[:, (2 * NT + f) * 4:(2 * NT + f + 1) * 4], in_=conv_q[f * 128:(f + 1) * 128, :]).then_inc(sems["qA"], 16); nbg += 1
        inc("qA", 16 * nbg)
        mark("bgcv_loaded", "qA")

    if dbg:
        @blk.sync
        def _(e):
            wt(e, ln1_done)
            gate(e, "qS")
            for f in range(NT):
                e.dma_start(out=DBG["d_xln"][f * 128:(f + 1) * 128, :], in_=big1[:, f * TW:(f + 1) * TW]).then_inc(sems["qS"], 16)
            inc("qS", 16 * NT)
            mark("dbg_xln", "qS")

    def emit_proj(name, w_dram, n_fo, conv_base, dst_spill, has_conv, width, tag, prev_sp=None):
        sp512 = spans(width, 512)
        nsp = len(sp512)
        for fo in range(n_fo):
            def pwld_batch(e, fo, hv):
                qn = "qS" if (fo * 2 + hv) % 2 == 0 else "qG"
                gate(e, qn)
                slot = (fo * 2 + hv) % 2
                for f8 in range(8):
                    f = hv * 8 + f8
                    e.dma_start(out=wst[:, slot * 1280 + f8 * 128:slot * 1280 + (f8 + 1) * 128],
                                in_=w_dram[f * 128:(f + 1) * 128, fo * 128:(fo + 1) * 128]).then_inc(sems[qn], 16)
                inc(qn, 16 * 8)
                mark(f"{tag}_wld{fo}_{hv}", qn)

            if fo == 0:
                @blk.sync
                def _(e, fo=fo):
                    wt(e, "x1_spilled")
                    pwld_batch(e, 0, 0)
                    pwld_batch(e, 0, 1)

            for hv in range(2):
                @blk.tensor
                def _(e, fo=fo, hv=hv):
                    wt(e, f"{tag}_wld{fo}_{hv}")
                    wt(e, ln1_done)
                    if hv == 0 and fo >= 1:
                        for psi in range(nsp):
                            wt(e, f"{tag}_pes{fo - 1}_{psi}")
                    slot = (fo * 2 + hv) % 2
                    last = None
                    for si, (o, w) in enumerate(sp512):
                        pb = [pFF, pFF2, pLN][si]
                        po = [0, 0, 64][si]
                        for f8 in range(8):
                            last = e.matmul(pb[:, po:po + w], wst[:, slot * 1280 + f8 * 128:slot * 1280 + (f8 + 1) * 128],
                                            big1[:, (hv * 8 + f8) * width + o:(hv * 8 + f8) * width + o + w],
                                            start=(hv == 0 and f8 == 0), stop=(hv == 1 and f8 == 7))
                    last.then_inc(sems["tT"], 1)
                    inc("tT")
                    mark(f"{tag}_b{fo}_{hv}", "tT")
                    if hv == 1:
                        mark(f"{tag}_mm{fo}", "tT")

                nxt = fo * 2 + hv + 2
                if nxt < n_fo * 2:
                    @blk.sync
                    def _(e, fo=fo, hv=hv, nxt=nxt):
                        wt(e, f"{tag}_b{fo}_{hv}")
                        pwld_batch(e, nxt // 2, nxt % 2)

            for si, (o, w) in enumerate(sp512):
                @blk.vector
                def _(e, fo=fo, si=si, o=o, w=w):
                    wt(e, f"{tag}_b{fo}_{1}")
                    if fo >= 2 and si == 0:
                        wt(e, f"{tag}_cv{fo - 2}")
                    pb = [pFF, pFF2, pLN][si]
                    po = [0, 0, 64][si]
                    e.tensor_copy(cwork[:, (fo % 2) * width + o:(fo % 2) * width + o + w], pb[:, po:po + w])
                    e.drain()
                    e.sem_inc(sems["tV"], 1)
                    inc("tV")
                    mark(f"{tag}_pes{fo}_{si}", "tV")

            @blk.vector
            def _(e, fo=fo):
                wt(e, f"{tag}_pes{fo}_{nsp - 1}")
                if fo >= 2:
                    wt(e, f"{tag}_sp{fo - 2}")
                elif prev_sp is not None:
                    wt(e, prev_sp)
                wt(e, "bgcv_loaded")
                wtile = cwork[:, (fo % 2) * width:(fo % 2) * width + width]
                dtile = cwork[:, (2 + fo % 2) * width:(2 + fo % 2) * width + width]
                if has_conv:
                    cw = cvw[:, (conv_base + fo) * 4:(conv_base + fo) * 4 + 4]
                    e.tensor_scalar(wtile[:, 0:HALO], wtile[:, 0:HALO], HMC[:, 0:1], None, ALU.mult)
                    e.drain()
                    e.tensor_scalar(dtile[:, HALO:width], wtile[:, 0:width - HALO], cw[:, 0:1], None, ALU.mult)
                    e.drain()
                    for j in range(1, 4):
                        e.scalar_tensor_tensor(dtile[:, HALO:width], wtile[:, j:j + width - HALO], cw[:, j:j + 1],
                                               dtile[:, HALO:width], ALU.mult, ALU.add)
                        e.drain()
                else:
                    e.tensor_copy(dtile[:, HALO:width], wtile[:, HALO:width])
                    e.drain()
                e.sem_inc(sems["tV"], 1)
                inc("tV")
                mark(f"{tag}_cv{fo}", "tV")

            @blk.scalar
            def _(e, fo=fo):
                wt(e, f"{tag}_cv{fo}")
                dtile = cwork[:, (2 + fo % 2) * width:(2 + fo % 2) * width + width]
                e.activation(dtile[:, HALO:width], dtile[:, HALO:width], AF.Silu)
                e.drain()
                gate(e, "qA")
                if dst_spill is gg_sp:
                    e.dma_start(out=dst_spill[fo * 128:(fo + 1) * 128, :], in_=dtile[:, HALO:width]).then_inc(sems["qA"], 16)
                else:
                    e.dma_start(out=dst_spill[fo * 128:(fo + 1) * 128, HALO:width], in_=dtile[:, HALO:width]).then_inc(sems["qA"], 16)
                inc("qA", 16)
                mark(f"{tag}_sp{fo}", "qA")
                e.sem_inc(sems["tS"], 1)
                inc("tS")
                mark(f"{tag}_pe{fo}", "tS")

            if dbg and name in ("k", "v", "q"):
                @blk.sync
                def _(e, fo=fo):
                    wt(e, f"{tag}_pe{fo}")
                    gate(e, "qS")
                    dd = {"k": "d_k", "v": "d_v", "q": "d_q"}[name]
                    dtile = cwork[:, (2 + fo % 2) * width:(2 + fo % 2) * width + width]
                    e.dma_start(out=DBG[dd][fo * 128:(fo + 1) * 128, HALO:width], in_=dtile[:, HALO:width]).then_inc(sems["qS"], 16)
                    inc("qS", 16)
                    mark(f"dbg_{name}{fo}", "qS")
        return f"{tag}_sp{n_fo - 1}", f"{tag}_pe{n_fo - 1}"

    barrier("ln1")
    kp_sp, kp_pe = emit_proj("k", wk_t, NT, 0, k_sp, True, TW, "pk")
    barrier("pk")
    vp_sp, vp_pe = emit_proj("v", wv_t, NT, NT, v_sp, True, TW, "pv", prev_sp=kp_sp)
    barrier("pv")
    qp_sp, qp_pe = emit_proj("q", wq_t, NQ, 2 * NT, q_sp, True, TW, "pq", prev_sp=vp_sp)
    barrier("pq")
    gp_sp, gp_pe = emit_proj("gg", wg_t, NQ, 0, gg_sp, False, TW, "pg", prev_sp=qp_sp)

    NSPB = len(spans(TW, 512))
    for si, (o, w) in enumerate(spans(TW, 512)):
        @blk.tensor
        def _(e, si=si, o=o, w=w):
            wt(e, gp_pe)
            wt(e, ln1_done)
            wt(e, "bgcv_loaded")
            if si > 0:
                wt(e, f"bg_v{si - 1}")
            for f in range(NT):
                mm = e.matmul(pFF[0:2 * H, 0:w], bgw[:, f * 24:f * 24 + 16],
                              big1[:, f * TW + o:f * TW + o + w], start=(f == 0), stop=(f == NT - 1))
            for f in range(NT):
                mm2 = e.matmul(pFF2[0:H, 0:w], bgw[:, f * 24 + 16:f * 24 + 24],
                               big1[:, f * TW + o:f * TW + o + w], start=(f == 0), stop=(f == NT - 1))
            mm2.then_inc(sems["tT"], 1)
            inc("tT")
            mark(f"bg_mm{si}", "tT")

        @blk.scalar
        def _(e, si=si, o=o, w=w):
            wt(e, f"bg_mm{si}")
            e.activation(bgb[:, o:o + w], pFF[0:2 * H, 0:w], AF.Sigmoid)
            e.activation(scr[0:H, 0:w], pFF2[0:H, 0:w], AF.Abs, bias=DTC)
            e.drain()
            e.activation(scr[0:H, 0:w], scr[0:H, 0:w], AF.Exp, scale=-1.0)
            e.drain()
            e.activation(scr[0:H, 0:w], scr[0:H, 0:w], AF.Ln, bias=1.0)
            e.drain()
            e.activation(gb[:, o:o + w], pFF2[0:H, 0:w], AF.Relu, bias=DTC)
            e.drain()
            e.sem_inc(sems["tS"], 1)
            inc("tS")
            mark(f"bg_s{si}", "tS")

        @blk.vector
        def _(e, si=si, o=o, w=w):
            wt(e, f"bg_s{si}")
            e.tensor_scalar(bgb[:, o:o + w], bgb[:, o:o + w], 2.0, None, ALU.mult)
            e.tensor_add(gb[:, o:o + w], gb[:, o:o + w], scr[0:H, 0:w])
            e.drain()
            e.tensor_scalar(gb[:, o:o + w], gb[:, o:o + w], NEXPA, None, ALU.mult)
            e.drain()
            e.sem_inc(sems["tV"], 1)
            inc("tV")
            mark(f"bg_v{si}", "tV")

    @blk.vector
    def _(e):
        wt(e, f"bg_v{NSPB - 1}")
        for c in range(NCH):
            o = HALO + c * CT
            e.tensor_tensor_scan(cumb[:, o:o + CT], ONES_T, gb[:, o:o + CT], 0.0, ALU.mult, ALU.add)
        e.drain()
        e.sem_inc(sems["tV"], 1)
        inc("tV")
        mark("cum_done", "tV")

    if dbg:
        @blk.sync
        def _(e):
            wt(e, "cum_done")
            gate(e, "qS")
            e.dma_start(out=DBG["d_b"], in_=bgb[:, :]).then_inc(sems["qS"], 16)
            e.dma_start(out=DBG["d_g"], in_=gb[:, :]).then_inc(sems["qS"], 16)
            e.dma_start(out=DBG["d_cum"], in_=cumb[:, :]).then_inc(sems["qS"], 16)
            inc("qS", 48)
            mark("dbg_bg", "qS")

    for c in range(NCH):
        o = HALO + c * CT

        @blk.tensor
        def _(e, c=c, o=o):
            wt(e, "cum_done")
            if c >= 2:
                wt(e, f"stp_fin{c - 2}")
            pbk = pBC if c % 2 == 0 else pLN
            e.transpose(pbk[0:CT, 0:H], cumb[:, o:o + CT], I128[0:H, 0:H])
            e.transpose(pbk[0:CT, 8:24], bgb[:, o:o + CT], I128[0:16, 0:16]).then_inc(sems["tT"], 1)
            inc("tT")
            mark(f"stp_t1{c}", "tT")

        @blk.vector
        def _(e, c=c, o=o):
            wt(e, f"stp_t1{c}")
            par = c % 2
            pbk = pBC if c % 2 == 0 else pLN
            e.tensor_copy(scr[0:CT, par * 64:par * 64 + 24], pbk[0:CT, 0:24])
            e.drain()
            e.sem_inc(sems["tV"], 1)
            inc("tV")
            mark(f"stp_e1{c}", "tV")

        @blk.tensor
        def _(e, c=c, o=o):
            wt(e, f"stp_e1{c}")
            par = c % 2
            pmk = pMP if c % 2 == 0 else pAP
            e.matmul(pmk[:, 0:H], E_boT, scr[0:CT, par * 64:par * 64 + H], start=True, stop=True)
            e.matmul(pmk[:, 8:16], E_evT, scr[0:CT, par * 64 + 8:par * 64 + 16], start=True, stop=False)
            e.matmul(pmk[:, 8:16], E_odT, scr[0:CT, par * 64 + 16:par * 64 + 24],
                     start=False, stop=True).then_inc(sems["tT"], 1)
            inc("tT")
            mark(f"stp_t2{c}", "tT")

        @blk.vector
        def _(e, c=c, o=o):
            wt(e, f"stp_t2{c}")
            pmk = pMP if c % 2 == 0 else pAP
            e.tensor_copy(stepcs[:, c * 40:c * 40 + 8], pmk[:, 0:H])
            e.tensor_copy(stepcs[:, c * 40 + 16:c * 40 + 24], pmk[:, 8:16])
            e.drain()
            e.sem_inc(sems["tV"], 1)
            inc("tV")
            mark(f"stp_e2{c}", "tV")

        @blk.scalar
        def _(e, c=c, o=o):
            wt(e, f"stp_e2{c}")
            e.activation(stepcs[:, c * 40 + 8:c * 40 + 16], stepcs[:, c * 40:c * 40 + 8], AF.Exp)
            e.drain()
            e.sem_inc(sems["tS"], 1)
            inc("tS")
            mark(f"stp_a{c}", "tS")

        @blk.tensor
        def _(e, c=c, o=o):
            wt(e, f"stp_a{c}")
            pmk = pMP if c % 2 == 0 else pAP
            pbk = pBC if c % 2 == 0 else pLN
            e.matmul(pmk[:, 16:24], sel127[:, :], stepcs[:, c * 40 + 8:c * 40 + 16], start=True, stop=True)
            e.matmul(pmk[:, 24:32], sel127[:, :], stepcs[:, c * 40:c * 40 + 8], start=True, stop=True)
            e.transpose(pbk[0:H, 32:160], stepcs[:, c * 40:c * 40 + 8], I128).then_inc(sems["tT"], 1)
            inc("tT")
            mark(f"stp_t3{c}", "tT")

        @blk.vector
        def _(e, c=c, o=o):
            wt(e, f"stp_t3{c}")
            pmk = pMP if c % 2 == 0 else pAP
            pbk = pBC if c % 2 == 0 else pLN
            e.tensor_copy(accb[:, c * 8:c * 8 + 8], pmk[:, 16:24])
            e.tensor_copy(rowcs[:, c * 128:(c + 1) * 128], pbk[0:H, 32:160])
            e.drain()
            e.tensor_sub(stepcs[:, c * 40 + 32:c * 40 + 40], pmk[:, 24:32],
                         stepcs[:, c * 40:c * 40 + 8])
            e.drain()
            e.tensor_tensor(stepcs[:, c * 40 + 24:c * 40 + 32], stepcs[:, c * 40 + 16:c * 40 + 24],
                            stepcs[:, c * 40 + 8:c * 40 + 16], ALU.mult)
            e.drain()
            e.sem_inc(sems["tV"], 1)
            inc("tV")
            mark(f"stp_v0{c}", "tV")

        @blk.scalar
        def _(e, c=c, o=o):
            wt(e, f"stp_v0{c}")
            e.activation(stepcs[:, c * 40 + 32:c * 40 + 40], stepcs[:, c * 40 + 32:c * 40 + 40], AF.Exp)
            e.drain()
            e.sem_inc(sems["tS"], 1)
            inc("tS")
            mark(f"stp_fin{c}", "tS")

    STP_ALL = f"stp_fin{NCH - 1}"

    barrier("stp")
    prev_ch = [None]

    def A_PART(h):
        @blk.sync
        def _(e, h=h):
            wt(e, kp_sp)
            wt(e, qp_sp)
            if h < 2:
                wt(e, vp_sp)
                wt(e, gp_sp)
                wt(e, f"bg_v{NSPB - 1}")
            if h >= 2:
                wt(e, f"A_done{h - 2}")
            gate(e, "qS")
            e.dma_start(out=khv(h, 0)[:, HALO:TW], in_=k_sp[h * 128:(h + 1) * 128, HALO:TW]).then_inc(sems["qS"], 16)
            e.dma_start(out=khv(h, 1)[:, HALO:TW], in_=k_sp[KEY + h * 128:KEY + (h + 1) * 128, HALO:TW]).then_inc(sems["qS"], 16)
            e.dma_start(out=qhv(h)[:, HALO:TW], in_=q_sp[h * 128:(h + 1) * 128, HALO:TW]).then_inc(sems["qS"], 16)
            inc("qS", 48)
            mark(f"kq_ld{h}", "qS")

        for c in range(NCH):
            o = HALO + c * CT
            idx = h * NCH + c
            par = idx % 2
            pb = pRW if par == 0 else pSQ
            PRV = prev_ch[0]

            @blk.scalar
            def _(e, h=h, c=c, o=o, idx=idx):
                wt(e, vp_sp)
                if idx < 2:
                    wt(e, gp_sp)
                    wt(e, f"bg_v{NSPB - 1}")
                if idx >= 2:
                    wt(e, f"vused{idx - 2}")
                gate(e, "qA")
                e.dma_start(out=vchv(c, 0), in_=v_sp[h * 128:(h + 1) * 128, o:o + CT]).then_inc(sems["qA"], 16)
                e.dma_start(out=vchv(c, 1), in_=v_sp[KEY + h * 128:KEY + (h + 1) * 128, o:o + CT]).then_inc(sems["qA"], 16)
                inc("qA", 32)
                mark(f"v_ld{idx}", "qA")

            @blk.vector
            def _(e, h=h, c=c, o=o, idx=idx, par=par):
                wt(e, f"kq_ld{h}")
                if idx >= 2:
                    wt(e, f"intu{idx - 2}")
                if dbg and 1 <= idx <= 2 and "dbg_ch0" in ev:
                    wt(e, "dbg_ch0")
                if idx < 2:
                    wt(e, STP_ALL)
                kb = kint[:, par * C:par * C + C]
                qb = qint[:, par * C:par * C + C]
                e.memset(qb[:, 0:C:2], 0.0)
                e.tensor_copy(kb[:, 0:C:2], khv(h, 0)[:, o:o + CT])
                e.tensor_copy(kb[:, 1:C:2], khv(h, 1)[:, o:o + CT])
                e.tensor_copy(qb[:, 1:C:2], qhv(h)[:, o:o + CT])
                e.drain()
                e.sem_inc(sems["tV"], 1)
                inc("tV")
                mark(f"int{idx}", "tV")

            @blk.tensor
            def _(e, idx=idx, par=par, pb=pb):
                wt(e, f"int{idx}")
                if idx >= 2:
                    wt(e, f"pt{idx - 2}")
                e.transpose(pb[:, 0:128], kint[:, par * C:par * C + C], I128)
                e.transpose(pb[:, 128:256], qint[:, par * C:par * C + C], I128).then_inc(sems["tT"], 1)
                inc("tT")
                mark(f"tr1{idx}", "tT")

            @blk.scalar
            def _(e, idx=idx, pb=pb):
                wt(e, f"tr1{idx}")
                e.activation(scr[:, 4 * C:4 * C + 128], pb[:, 0:128], AF.Square, accum_out=nrmc[:, 0:1])
                e.drain()
                e.activation(scr[:, 4 * C:4 * C + 128], pb[:, 128:256], AF.Square, accum_out=nrmc[:, 2:3])
                e.drain()
                e.sem_inc(sems["tS"], 1)
                inc("tS")
                mark(f"ss{idx}", "tS")

            @blk.scalar
            def _(e, idx=idx):
                wt(e, f"ss{idx}")
                e.activation(nrmc[:, 1:2], nrmc[:, 0:1], AF.Sqrt, bias=EPSN)
                e.activation(nrmc[:, 3:4], nrmc[:, 2:3], AF.Sqrt, bias=EPSN)
                e.drain()
                e.sem_inc(sems["tS"], 1)
                inc("tS")
                mark(f"inv{idx}", "tS")

            @blk.vector
            def _(e, h=h, c=c, idx=idx, par=par, pb=pb):
                wt(e, f"inv{idx}")
                wt(e, STP_ALL)
                e.reciprocal(nrmc[:, 1:2], nrmc[:, 1:2])
                e.reciprocal(nrmc[:, 3:4], nrmc[:, 3:4])
                e.drain()
                e.tensor_scalar(nrmc[:, 3:4], nrmc[:, 3:4], INV_SQRT_DH, None, ALU.mult)
                e.drain()
                e.tensor_scalar(ktm[:, par * DH:par * DH + DH], pb[:, 0:128], nrmc[:, 1:2], None, ALU.mult)
                e.tensor_scalar(scr[:, 0:128], pb[:, 128:256], nrmc[:, 3:4], None, ALU.mult)
                e.drain()
                bcol = stepcs[:, c * 40 + 16 + h:c * 40 + 16 + h + 1]
                acol = stepcs[:, c * 40 + 8 + h:c * 40 + 8 + h + 1]
                e.tensor_scalar(kbtm[:, 0:DH], ktm[:, par * DH:par * DH + DH], bcol, None, ALU.mult)
                e.tensor_scalar(qatm[:, :], scr[:, 0:128], acol, None, ALU.mult)
                e.drain()
                e.sem_inc(sems["tV"], 1)
                inc("tV")
                mark(f"ktmn{idx}", "tV")
                mark(f"intu{idx}", "tV")

            @blk.tensor
            def _(e, idx=idx, par=par, pb=pb):
                wt(e, f"ktmn{idx}")
                e.transpose(pb[:, 256:384], ktm[:, par * DH:par * DH + DH], I128)
                e.transpose(pb[:, 384:512], scr[:, 0:128], I128)
                e.transpose(pb[:, 0:128], kbtm[:, 0:DH], I128)
                e.transpose(pb[:, 128:256], qatm[:, :], I128).then_inc(sems["tT"], 1)
                inc("tT")
                mark(f"tr2{idx}", "tT")

            @blk.vector
            def _(e, idx=idx, par=par, pb=pb):
                wt(e, f"tr2{idx}")
                e.tensor_copy(khf[:, par * C:par * C + C], pb[:, 256:384])
                e.tensor_copy(qhf[:, par * C:par * C + C], pb[:, 384:512])
                e.tensor_copy(kbfm[:, 0:C], pb[:, 0:128])
                e.tensor_copy(qtl[:, :], pb[:, 128:256])
                e.drain()
                e.sem_inc(sems["tV"], 1)
                inc("tV")
                mark(f"hf{idx}", "tV")

            @blk.vector
            def _(e, c=c, idx=idx):
                wt(e, f"v_ld{idx}")
                wt(e, f"hf{idx}")
                vb = scr[:, C:2 * C]
                e.tensor_copy(vb[:, 0:C:2], vchv(c, 0))
                e.tensor_copy(vb[:, 1:C:2], vchv(c, 1))
                e.drain()
                e.sem_inc(sems["tV"], 1)
                inc("tV")
                mark(f"vint{idx}", "tV")
                mark(f"vused{idx}", "tV")

            @blk.tensor
            def _(e, idx=idx, pb=pb):
                wt(e, f"vint{idx}")
                e.transpose(pb[:, 0:128], scr[:, C:2 * C], I128).then_inc(sems["tT"], 1)
                inc("tT")
                mark(f"vtr{idx}", "tT")

            @blk.vector
            def _(e, h=h, c=c, idx=idx, par=par, pb=pb):
                wt(e, f"vtr{idx}")
                wt(e, STP_ALL)
                if idx >= 2:
                    wt(e, f"mp{idx - 2}")
                if dbg and 1 <= idx <= 2 and "dbg_ch0" in ev:
                    wt(e, "dbg_ch0")
                bcol = stepcs[:, c * 40 + 16 + h:c * 40 + 16 + h + 1]
                bacol = stepcs[:, c * 40 + 24 + h:c * 40 + 24 + h + 1]
                rb = rhsb[:, par * 256:par * 256 + 256]
                e.tensor_scalar(rb[:, 0:DH], pb[:, 0:128], bcol, None, ALU.mult)
                e.tensor_scalar(rb[:, DH:2 * DH], ktm[:, par * DH:par * DH + DH], bacol, None, ALU.mult)
                e.drain()
                e.sem_inc(sems["tV"], 1)
                inc("tV")
                mark(f"rhs{idx}", "tV")

            @blk.tensor
            def _(e, h=h, c=c, idx=idx, par=par, pb=pb):
                wt(e, f"rhs{idx}")
                kf = khf[:, par * C:par * C + C]
                qf = qhf[:, par * C:par * C + C]
                e.matmul(pb[:, 128:256], kf, kbfm[:, 0:C], start=True, stop=True)
                e.matmul(pb[:, 256:384], kf, qf, start=True, stop=True)
                e.matmul(pb[:, 384:512], selb[0:H, h * 128:h * 128 + 128], rowcs[0:H, c * 128:(c + 1) * 128],
                         start=True, stop=True).then_inc(sems["tT"], 1)
                inc("tT")
                mark(f"raw{idx}", "tT")

            @blk.vector
            def _(e, h=h, c=c, idx=idx, pb=pb):
                wt(e, f"raw{idx}")
                ccol = stepcs[:, c * 40 + h:c * 40 + h + 1]
                e.tensor_scalar(dxp[:, :], pb[:, 384:512], ccol, 0.0, ALU.subtract, ALU.min)
                e.drain()
                e.sem_inc(sems["tV"], 1)
                inc("tV")
                mark(f"darg{idx}", "tV")

            @blk.scalar
            def _(e, idx=idx):
                wt(e, f"darg{idx}")
                e.activation(dxp[:, :], dxp[:, :], AF.Exp)
                e.drain()
                e.sem_inc(sems["tS"], 1)
                inc("tS")
                mark(f"dexp{idx}", "tS")

            @blk.vector
            def _(e, h=h, c=c, idx=idx, pb=pb, PRV=PRV):
                wt(e, f"dexp{idx}")
                if dbg and 1 <= idx <= 2 and "dbg_ch0" in ev:
                    wt(e, "dbg_ch0")
                if PRV is not None:
                    wt(e, f"gap{idx - 1}")
                ccol = stepcs[:, c * 40 + h:c * 40 + h + 1]
                bcol = stepcs[:, c * 40 + 16 + h:c * 40 + 16 + h + 1]
                e.tensor_tensor(bpw[:, 0:C], pb[:, 128:256], dxp[:, :], ALU.mult)
                e.drain()
                e.tensor_tensor(bpw[:, 0:C], bpw[:, 0:C], MSU, ALU.mult)
                g_ = gstv(idx)
                e.tensor_tensor(g_[:, :], pb[:, 256:384], dxp[:, :], ALU.mult)
                e.drain()
                e.tensor_tensor(g_[:, :], g_[:, :], MGU, ALU.mult)
                e.drain()
                e.sem_inc(sems["tV"], 1)
                inc("tV")
                mark(f"bb{idx}", "tV")

            @blk.tensor
            def _(e, idx=idx, pb=pb):
                wt(e, f"bb{idx}")
                e.transpose(pb[:, 0:128], bpwv(idx)[:, 0:C], I128).then_inc(sems["tT"], 1)
                inc("tT")
                mark(f"atr{idx}", "tT")

            @blk.vector
            def _(e, h=h, c=c, idx=idx, pb=pb):
                wt(e, f"atr{idx}")
                e.tensor_copy(apwv(idx)[:, 0:C], pb[:, 0:128]).then_inc(sems["tV"], 1)
                inc("tV")
                mark(f"ab{idx}", "tV")

            pS = pAP if (idx % 2 == 0 or not SOLVE_DUAL) else pFF
            bp_ = bpwv(idx)
            ap_ = apwv(idx)
            for lv in range(1, 7):
                @blk.tensor
                def _(e, idx=idx, lv=lv, pS=pS, bp_=bp_, ap_=ap_):
                    wt(e, f"ab{idx}" if lv == 1 else f"sqr{lv - 1}_{idx}")
                    src_a = ap_[:, ((lv - 1) % 2) * C:((lv - 1) % 2) * C + C]
                    src_b = bp_[:, (lv - 1) * C:lv * C]
                    e.matmul(pS[:, 0:128], src_b, src_a, start=True, stop=True)
                    e.matmul(pS[:, 128:256], src_a, src_b, start=True, stop=True).then_inc(sems["tT"], 1)
                    inc("tT")
                    mark(f"sqt{lv}_{idx}", "tT")

                @blk.vector
                def _(e, idx=idx, lv=lv, pS=pS, bp_=bp_, ap_=ap_):
                    wt(e, f"sqt{lv}_{idx}")
                    if dbg and idx == 0 and lv == 1 and "dbg_bb" in ev:
                        wt(e, "dbg_bb")
                    e.tensor_copy(ap_[:, (lv % 2) * C:(lv % 2) * C + C], pS[:, 0:128])
                    e.tensor_copy(bp_[:, lv * C:(lv + 1) * C], pS[:, 128:256])
                    e.drain()
                    e.sem_inc(sems["tV"], 1)
                    inc("tV")
                    mark(f"sqr{lv}_{idx}", "tV")

            for ai, lv in enumerate([6, 5, 4, 3, 2, 1, 0]):
                @blk.tensor
                def _(e, idx=idx, par=par, ai=ai, lv=lv, PRV=PRV, pS=pS, bp_=bp_):
                    wt(e, f"sqr6_{idx}" if ai == 0 else f"app{ai - 1}_{idx}")
                    if ai == 0 and idx >= SDEPTH:
                        wt(e, f"app6_{idx - SDEPTH}")
                    rb = rhsb[:, par * 256:par * 256 + 256]
                    e.matmul(pS[:, 256:512], bp_[:, lv * C:(lv + 1) * C], rb, start=True, stop=True).then_inc(sems["tT"], 1)
                    inc("tT")
                    mark(f"apt{ai}_{idx}", "tT")

                @blk.vector
                def _(e, idx=idx, par=par, ai=ai, lv=lv, pS=pS):
                    wt(e, f"apt{ai}_{idx}")
                    rb = rhsb[:, par * 256:par * 256 + 256]
                    if lv == 0:
                        e.tensor_sub(rb, rb, pS[:, 256:512])
                    else:
                        e.tensor_add(rb, rb, pS[:, 256:512])
                    e.drain()
                    e.sem_inc(sems["tV"], 1)
                    inc("tV")
                    mark(f"app{ai}_{idx}", "tV")

            @blk.tensor
            def _(e, idx=idx, par=par, pb=pb):
                wt(e, f"app6_{idx}")
                rb = rhsb[:, par * 256:par * 256 + 256]
                g_ = gstv(idx)
                e.matmul(pb[:, 0:256], g_[:, :], rb, start=True, stop=True)
                e.matmul(pb[:, 256:384], rb[:, DH:2 * DH], g_[:, :], start=True, stop=True).then_inc(sems["tT"], 1)
                inc("tT")
                mark(f"gap{idx}", "tT")

            @blk.vector
            def _(e, h=h, c=c, idx=idx, par=par, pb=pb):
                wt(e, f"gap{idx}")
                e.tensor_copy(art(h, c, 1), pb[:, 0:DH])
                e.tensor_sub(art(h, c, 0), qtl[:, :], pb[:, 256:384])
                e.drain()
                crc = stepcs[:, c * 40 + 32 + h:c * 40 + 32 + h + 1]
                e.tensor_scalar(kbar[:, :], ktm[:, par * DH:par * DH + DH], crc, None, ALU.mult)
                e.drain()
                e.sem_inc(sems["tV"], 1)
                inc("tV")
                mark(f"pt{idx}", "tV")

            @blk.tensor
            def _(e, idx=idx, par=par, PRV=PRV):
                wt(e, f"pt{idx}")
                if PRV is not None:
                    wt(e, f"cmp{idx - 1}")
                rb = rhsb[:, par * 256:par * 256 + 256]
                e.matmul(pMP[:, 0:256], kbar[:, :], rb, start=True, stop=True)
                e.matmul(pMP[:, 256:384], rb[:, DH:2 * DH], kbar[:, :], start=True, stop=True).then_inc(sems["tT"], 1)
                inc("tT")
                mark(f"mp{idx}", "tT")

            @blk.vector
            def _(e, h=h, c=c, idx=idx):
                wt(e, f"mp{idx}")
                if h >= 2 and c == 0:
                    wt(e, f"B_done{h - 2}")
                acol = accb[:, c * 8 + h:c * 8 + h + 1]
                e.tensor_copy(art(h, c, 3), pMP[:, 0:DH])
                e.scalar_tensor_tensor(art(h, c, 2), I128, acol, pMP[:, 256:384], ALU.mult, ALU.subtract)
                e.drain()
                e.sem_inc(sems["tV"], 1)
                inc("tV")
                mark(f"mn{idx}", "tV")

            if c == 0:
                @blk.vector
                def _(e, h=h, c=c, idx=idx):
                    wt(e, f"mn{idx}")
                    acol = accb[:, c * 8 + h:c * 8 + h + 1]
                    e.scalar_tensor_tensor(comp[:, 0:DH], I128, acol, pMP[:, 128:256], ALU.mult, ALU.subtract)
                    e.tensor_copy(comp[:, DH:2 * DH], art(h, c, 2))
                    e.tensor_copy(comp[:, 2 * DH:3 * DH], art(h, c, 3))
                    e.drain()
                    e.sem_inc(sems["tV"], 1)
                    inc("tV")
                    mark(f"cmp{idx}", "tV")
            else:
                @blk.vector
                def _(e, h=h, c=c, idx=idx):
                    wt(e, f"mn{idx}")
                    acol = accb[:, c * 8 + h:c * 8 + h + 1]
                    e.scalar_tensor_tensor(scr[:, 3 * C:3 * C + DH], I128, acol, pMP[:, 128:256], ALU.mult, ALU.subtract)
                    e.drain()
                    e.sem_inc(sems["tV"], 1)
                    inc("tV")
                    mark(f"mcu{idx}", "tV")

                @blk.tensor
                def _(e, h=h, c=c, idx=idx):
                    wt(e, f"mcu{idx}")
                    e.matmul(pMP[:, 0:128], art(h, c, 2), comp[:, 0:DH], start=True, stop=True)
                    e.matmul(pMP[:, 128:256], comp[:, 0:DH], scr[:, 3 * C:3 * C + DH], start=True, stop=True)
                    e.matmul(pMP[:, 256:384], art(h, c, 2), comp[:, 2 * DH:3 * DH],
                             start=True, stop=True).then_inc(sems["tT"], 1)
                    inc("tT")
                    mark(f"cmt{idx}", "tT")

                @blk.vector
                def _(e, h=h, c=c, idx=idx):
                    wt(e, f"cmt{idx}")
                    e.tensor_copy(comp[:, 0:DH], pMP[:, 0:128])
                    e.tensor_copy(comp[:, DH:2 * DH], pMP[:, 128:256])
                    e.tensor_add(comp[:, 2 * DH:3 * DH], pMP[:, 256:384], art(h, c, 3))
                    e.drain()
                    e.sem_inc(sems["tV"], 1)
                    inc("tV")
                    mark(f"cmp{idx}", "tV")

            if dbg and h == 0 and c == 0:
                @blk.sync
                def _(e, h=h, c=c, idx=idx, par=par):
                    wt(e, f"ab{idx}")
                    gate(e, "qS")
                    e.dma_start(out=DBG["d_bup"], in_=bpw[:, 0:C]).then_inc(sems["qS"], 16)
                    inc("qS", 16)
                    mark("dbg_bb", "qS")

                @blk.sync
                def _(e, h=h, c=c, idx=idx, par=par):
                    wt(e, f"mn{idx}")
                    gate(e, "qS")
                    e.dma_start(out=DBG["d_kint"], in_=kint[:, par * C:par * C + C]).then_inc(sems["qS"], 16)
                    e.dma_start(out=DBG["d_ktm"], in_=ktm[:, par * DH:par * DH + DH]).then_inc(sems["qS"], 16)
                    e.dma_start(out=DBG["d_rhs"], in_=rhsb[:, par * 256:par * 256 + 256]).then_inc(sems["qS"], 16)
                    e.dma_start(out=DBG["d_gst"], in_=gstv(0)[:, :]).then_inc(sems["qS"], 16)
                    e.dma_start(out=DBG["d_kbf"], in_=kbfm[:, 0:C]).then_inc(sems["qS"], 16)
                    e.dma_start(out=DBG["d_kbt"], in_=kbtm[:, 0:DH]).then_inc(sems["qS"], 16)
                    e.dma_start(out=DBG["d_art"], in_=big0[:, 0 + c * 4 * DH:0 + (c + 1) * 4 * DH]).then_inc(sems["qS"], 16)
                    inc("qS", 112)
                    mark("dbg_ch0", "qS")

            prev_ch[0] = (h, c)

        @blk.vector
        def _(e, h=h):
            wt(e, f"cmp{h * NCH + NCH - 1}")
            if h >= 1:
                wt(e, f"cci{h - 1}")
            e.tensor_copy(ccsb[:, 0:DH], comp[:, DH:2 * DH])
            e.tensor_copy(ccsb[:, DH:2 * DH], comp[:, 2 * DH:3 * DH])
            e.drain()
            e.sem_inc(sems["tV"], 1)
            inc("tV")
            mark(f"ccs{h}", "tV")
            mark(f"A_done{h}", "tV")

        @blk.scalar
        def _(e, h=h):
            wt(e, f"ccs{h}")
            gate(e, "qA")
            e.dma_start(out=cc_in[h], in_=ccsb[:, :]).then_inc(sems["qA"], 16)
            inc("qA", 16)
            mark(f"cci{h}", "qA")

        @blk.gpsimd
        def _(e, h=h):
            wt(e, f"cci{h}")
            e.collective_compute("AllGather", ALU.bypass, ins=[cc_in[h]], outs=[cc_out[h]],
                                 replica_groups=[[i for i in range(n_cores)]]).then_inc(sems["cc"], 1)
            inc("cc", 1)
            mark(f"ccd{h}", "cc")

    def B_PART(h):
        @blk.vector
        def _(e, h=h):
            wt(e, f"ccd{h}")
            if h >= 1:
                wt(e, f"B_done{h - 1}")
            e.memset(prefx[:, :], 0.0)
            e.drain()
            e.sem_inc(sems["tV"], 1)
            inc("tV")
            mark(f"pfx0{h}", "tV")

        for j in range(n_cores - 1):
            @blk.scalar
            def _(e, h=h, j=j):
                wt(e, f"ccd{h}")
                if j >= 2:
                    wt(e, f"pfa{h}_{j - 2}")
                gate(e, "qA")
                e.dma_start(out=gsb[:, (j % 2) * 256:(j % 2) * 256 + 256], in_=cc_out[h][j]).then_inc(sems["qA"], 16)
                inc("qA", 16)
                mark(f"gld{h}_{j}", "qA")

            @blk.vector
            def _(e, h=h, j=j):
                wt(e, f"gld{h}_{j}")
                wt(e, f"pfx0{h}")
                if j > 0:
                    wt(e, f"pfa{h}_{j - 1}")
                g = gsb[:, (j % 2) * 256:(j % 2) * 256 + 256]
                sc = selbc[:, j:j + 1]
                e.tensor_scalar(bscr[:, 0:DH], g[:, 0:DH], sc, None, ALU.mult)
                e.tensor_scalar(bscr[:, 2 * DH:3 * DH], g[:, DH:2 * DH], sc, None, ALU.mult)
                e.tensor_scalar(bscr[:, DH:2 * DH], I128, sc, -1.0, ALU.mult, ALU.mult)
                e.drain()
                e.tensor_add(bscr[:, DH:2 * DH], bscr[:, DH:2 * DH], I128)
                e.drain()
                e.tensor_add(bscr[:, 0:DH], bscr[:, 0:DH], bscr[:, DH:2 * DH])
                e.drain()
                e.sem_inc(sems["tV"], 1)
                inc("tV")
                mark(f"pfm{h}_{j}", "tV")

            @blk.tensor
            def _(e, h=h, j=j):
                wt(e, f"pfm{h}_{j}")
                e.matmul(pBC[:, 256:384], bscr[:, 0:DH], prefx[:, :], start=True, stop=True).then_inc(sems["tT"], 1)
                inc("tT")
                mark(f"pft{h}_{j}", "tT")

            @blk.vector
            def _(e, h=h, j=j):
                wt(e, f"pft{h}_{j}")
                e.tensor_add(prefx[:, :], pBC[:, 256:384], bscr[:, 2 * DH:3 * DH])
                e.drain()
                e.sem_inc(sems["tV"], 1)
                inc("tV")
                mark(f"pfa{h}_{j}", "tV")

        PFX = f"pfa{h}_{n_cores - 2}" if n_cores > 1 else f"pfx0{h}"

        if dbg and h == 0:
            @blk.sync
            def _(e, h=h):
                wt(e, PFX)
                gate(e, "qS")
                e.dma_start(out=DBG["d_sin"], in_=prefx[:, :]).then_inc(sems["qS"], 16)
                inc("qS", 16)
                mark("dbg_sin", "qS")

        @blk.vector
        def _(e, h=h):
            wt(e, PFX)
            e.tensor_copy(stt[:, :], prefx[:, :])
            e.drain()
            e.sem_inc(sems["tV"], 1)
            inc("tV")
            mark(f"stinit{h}", "tV")

        for c in range(NCH):
            bp = c % 2
            SP = f"stinit{h}" if c == 0 else f"stch{h}_{c - 1}"

            @blk.scalar
            def _(e, h=h, c=c, bp=bp):
                wt(e, gp_sp)
                if c >= 2:
                    wt(e, f"oev{h}_{c - 2}")
                gate(e, "qA")
                e.dma_start(out=ggch[:, bp * CT:bp * CT + CT],
                            in_=gg_sp[h * 128:(h + 1) * 128, c * CT:(c + 1) * CT]).then_inc(sems["qA"], 16)
                inc("qA", 16)
                mark(f"gg_ld{h}_{c}", "qA")

            @blk.tensor
            def _(e, h=h, c=c, bp=bp, SP=SP):
                wt(e, SP)
                wt(e, f"mn{h * NCH + c}")
                if c >= 2:
                    wt(e, f"oev{h}_{c - 2}")
                e.matmul(pBC[:, 0:128], art(h, c, 0), stt[:, :], start=True, stop=True)
                e.matmul(pBC[:, 128:256], art(h, c, 2), stt[:, :],
                         start=True, stop=True).then_inc(sems["tT"], 1)
                inc("tT")
                mark(f"bst{h}_{c}", "tT")

            @blk.vector
            def _(e, h=h, c=c, bp=bp):
                wt(e, f"bst{h}_{c}")
                if dbg and h == 0 and c == 2 and "dbg_och" in ev:
                    wt(e, "dbg_och")
                e.tensor_add(och[:, bp * DH:bp * DH + DH], pBC[:, 0:128], art(h, c, 1))
                e.tensor_add(stt[:, :], pBC[:, 128:256], art(h, c, 3))
                e.drain()
                e.sem_inc(sems["tV"], 1)
                inc("tV")
                mark(f"stch{h}_{c}", "tV")

            @blk.scalar
            def _(e, h=h, c=c, bp=bp):
                wt(e, f"stch{h}_{c}")
                wt(e, f"gg_ld{h}_{c}")
                ob = och[:, bp * DH:bp * DH + DH]
                e.activation(scr[:, 5 * C:5 * C + DH], ob, AF.Square, accum_out=nrmc[:, 4:5])
                e.drain()
                e.activation(nrmc[:, 5:6], nrmc[:, 4:5], AF.Sqrt, bias=EPSC, scale=INVDHC)
                e.drain()
                e.sem_inc(sems["tS"], 1)
                inc("tS")
                mark(f"sgg{h}_{c}", "tS")

            @blk.vector
            def _(e, h=h, c=c, bp=bp):
                wt(e, f"sgg{h}_{c}")
                ob = och[:, bp * DH:bp * DH + DH]
                e.reciprocal(nrmc[:, 5:6], nrmc[:, 5:6])
                e.drain()
                e.tensor_scalar(ob, ob, nrmc[:, 5:6], None, ALU.mult)
                e.drain()
                e.sem_inc(sems["tV"], 1)
                inc("tV")
                mark(f"orms{h}_{c}", "tV")

            @blk.tensor
            def _(e, h=h, c=c, bp=bp):
                wt(e, f"orms{h}_{c}")
                e.transpose(pLN[:, 0:128], och[:, bp * DH:bp * DH + DH], I128).then_inc(sems["tT"], 1)
                inc("tT")
                mark(f"otr{h}_{c}", "tT")

            @blk.vector
            def _(e, h=h, c=c, bp=bp):
                wt(e, f"otr{h}_{c}")
                o = c * CT
                e.tensor_tensor(scr[:, 5 * C:5 * C + CT], pLN[:, 1:128:2],
                                ggch[:, bp * CT:bp * CT + CT], ALU.mult)
                e.drain()
                e.tensor_scalar(ofmv(h)[:, o:o + CT], scr[:, 5 * C:5 * C + CT], onrm[:, 0:1], None, ALU.mult)
                e.drain()
                e.sem_inc(sems["tV"], 1)
                inc("tV")
                mark(f"oev{h}_{c}", "tV")

            if dbg and h == 0 and c == 0:
                @blk.sync
                def _(e, h=h, c=c, bp=bp):
                    wt(e, f"orms{h}_{c}")
                    gate(e, "qS")
                    e.dma_start(out=DBG["d_och"], in_=och[:, bp * DH:bp * DH + DH]).then_inc(sems["qS"], 16)
                    inc("qS", 16)
                    mark("dbg_och", "qS")

        @blk.vector
        def _(e, h=h):
            wt(e, f"oev{h}_{NCH - 1}")
            e.tensor_copy(bscr[0:1, 0:1], bscr[0:1, 0:1])
            e.drain()
            e.sem_inc(sems["tV"], 1)
            inc("tV")
            mark(f"B_done{h}", "tV")

    for h in range(H):
        A_PART(h)
        barrier(f"A{h}")
        if h >= 1:
            B_PART(h - 1)
    B_PART(H - 1)

    if dbg:
        @blk.sync
        def _(e):
            wt(e, f"B_done{H - 1}")
            gate(e, "qS")
            for f in range(NQ):
                e.dma_start(out=DBG["d_ofm"][f * 128:(f + 1) * 128, :],
                            in_=big1[:, OFM_OFF + f * tpc:OFM_OFF + (f + 1) * tpc]).then_inc(sems["qS"], 16)
            inc("qS", 16 * NQ)
            mark("dbg_ofm", "qS")

    X2v = big0
    barrier("scan")

    @blk.sync
    def _(e):
        wt(e, f"B_done{H - 1}")
        wt(e, "x1_spilled")
        gate(e, "qS")
        for f in range(NT):
            e.dma_start(out=X2v[:, f * tpc:(f + 1) * tpc], in_=x1_sp[f * 128:(f + 1) * 128, HALO:TW]).then_inc(sems["qS"], 16)
        inc("qS", 16 * NT)
        mark("x1_reld", "qS")

    nspo = len(spans(tpc, 512))
    for fo in range(NT):
        @blk.sync
        def _(e, fo=fo):
            wt(e, "x1_reld")
            if fo >= 2:
                wt(e, f"wo_m{fo - 2}_{nspo - 1}")
            qn = "qS" if fo % 2 == 0 else "qG"
            gate(e, qn)
            for f in range(NQ):
                e.dma_start(out=wst[:, (fo % 2) * 1280 + f * 128:(fo % 2) * 1280 + (f + 1) * 128],
                            in_=wo_t[f * 128:(f + 1) * 128, fo * 128:(fo + 1) * 128]).then_inc(sems[qn], 16)
            inc(qn, 16 * NQ)
            mark(f"wo_ld{fo}", qn)

        for si, (o, w) in enumerate(spans(tpc, 512)):
            @blk.tensor
            def _(e, fo=fo, si=si, o=o, w=w):
                wt(e, f"wo_ld{fo}")
                if fo >= 2 or si > 0:
                    pf, ps_ = (fo, si - 1) if si > 0 else (fo - 2, nspo - 1)
                    wt(e, f"wo_ev{pf}_{ps_}")
                pb = pFF if fo % 2 == 0 else pFF2
                for f in range(NQ):
                    mm = e.matmul(pb[:, 0:w], wst[:, (fo % 2) * 1280 + f * 128:(fo % 2) * 1280 + (f + 1) * 128],
                                  big1[:, OFM_OFF + f * tpc + o:OFM_OFF + f * tpc + o + w],
                                  start=(f == 0), stop=(f == NQ - 1))
                mm.then_inc(sems["tT"], 1)
                inc("tT")
                mark(f"wo_m{fo}_{si}", "tT")

            @blk.vector
            def _(e, fo=fo, si=si, o=o, w=w):
                wt(e, f"wo_m{fo}_{si}")
                pb = pFF if fo % 2 == 0 else pFF2
                e.tensor_add(X2v[:, fo * tpc + o:fo * tpc + o + w], X2v[:, fo * tpc + o:fo * tpc + o + w], pb[:, 0:w])
                e.drain()
                e.sem_inc(sems["tV"], 1)
                inc("tV")
                mark(f"wo_ev{fo}_{si}", "tV")

    WO_ALL = f"wo_ev{NT - 1}_{nspo - 1}"

    if dbg:
        @blk.sync
        def _(e):
            wt(e, WO_ALL)
            gate(e, "qS")
            for f in range(NT):
                e.dma_start(out=DBG["d_x2"][f * 128:(f + 1) * 128, :], in_=X2v[:, f * tpc:(f + 1) * tpc]).then_inc(sems["qS"], 16)
            inc("qS", 16 * NT)
            mark("dbg_x2", "qS")

    @blk.scalar
    def _(e):
        wt(e, WO_ALL)
        gate(e, "qA")
        for f in range(NT):
            e.dma_start(out=smw[:, f * SM:(f + 1) * SM], in_=w_small_t[f * 128:(f + 1) * 128, :]).then_inc(sems["qA"], 16)
        inc("qA", 16 * NT)
        mark("smw2_loaded", "qA")

    barrier("wo")
    ln2_done = emit_ln(X2v, big1, tpc, [WO_ALL])
    ffn2_done = emit_ffn(big1, X2v, tpc, [ln2_done, "smw2_loaded"], "fB")

    @blk.sync
    def _(e):
        wt(e, ffn2_done)
        gate(e, "qS")
        for f in range(NT):
            e.dma_start(out=out_fm[f * 128:(f + 1) * 128, :], in_=X2v[:, f * tpc:(f + 1) * tpc]).then_inc(sems["qS"], 16)
        inc("qS", 16 * NT)
        mark("out_st", "qS")
        wt(e, "out_st")
        if dbg:
            for nm in list(ev.keys()):
                if nm.startswith("dbg_"):
                    wt(e, nm)

    for g in reversed(guards):
        try:
            g.__exit__(None, None, None)
        except Exception:
            pass
    return nc, DBG



def _layout(inputs, n_cores, tpc):
    TW = tpc + HALO
    i = inputs
    x = np.asarray(i["L0_attn_x"], dtype=np.float32)[0]
    xT = np.ascontiguousarray(x.T)
    shared = {}
    shared["wq_t"] = np.ascontiguousarray(np.asarray(i["L0_attn_Wq"], np.float32).T)
    shared["wk_t"] = np.ascontiguousarray(np.asarray(i["L0_attn_Wk"], np.float32).T)
    shared["wv_t"] = np.ascontiguousarray(np.asarray(i["L0_attn_Wv"], np.float32).T)
    shared["wbwa_t"] = np.ascontiguousarray(
        np.concatenate([np.asarray(i["L0_attn_Wb"], np.float32).T,
                        np.asarray(i["L0_attn_Wa"], np.float32).T], axis=1))
    shared["wg_t"] = np.ascontiguousarray(np.asarray(i["L0_attn_Wg"], np.float32).T)
    shared["wo_t"] = np.ascontiguousarray(np.asarray(i["L0_attn_Wo"], np.float32).T)
    shared["a_log"] = np.asarray(i["L0_attn_A_log"], np.float32).reshape(1, H)
    shared["dt_b"] = np.asarray(i["L0_attn_dt_bias"], np.float32).reshape(1, H)
    shared["conv_q"] = np.ascontiguousarray(np.asarray(i["L0_attn_conv_q"], np.float32))
    shared["conv_k"] = np.ascontiguousarray(np.asarray(i["L0_attn_conv_k"], np.float32))
    shared["conv_v"] = np.ascontiguousarray(np.asarray(i["L0_attn_conv_v"], np.float32))
    shared["onw"] = np.asarray(i["L0_attn_o_norm_w"], np.float32).reshape(1, DH)
    shared["w_base_t"] = np.ascontiguousarray(np.asarray(i["L0_ffn_W_base"], np.float32))
    shared["b_base_c"] = np.ascontiguousarray(np.asarray(i["L0_ffn_b_base"], np.float32).reshape(NT, 128).T)
    shared["w_small_t"] = np.ascontiguousarray(
        np.concatenate([np.asarray(i["L0_ffn_A_s"], np.float32),
                        np.asarray(i["L0_ffn_W_router"], np.float32),
                        np.asarray(i["L0_ffn_A_r"], np.float32).transpose(1, 0, 2).reshape(D, NE * R)], axis=1))
    shared["b_rout"] = np.asarray(i["L0_ffn_b_router"], np.float32).reshape(1, NE)
    shared["b_s_w"] = np.ascontiguousarray(np.asarray(i["L0_ffn_B_s"], np.float32))
    shared["b_r_w"] = np.ascontiguousarray(np.asarray(i["L0_ffn_B_r"], np.float32).reshape(NE * R, D))
    ins = []
    for c in range(n_cores):
        lo = c * tpc - HALO
        sl = np.zeros((D, TW), dtype=np.float32)
        src_lo = max(0, lo)
        sl[:, (src_lo - lo):] = xT[:, src_lo:c * tpc + tpc]
        m = dict(shared)
        m["x_fm"] = np.ascontiguousarray(sl)
        m["csel"] = (np.arange(8) < c).astype(np.float32).reshape(1, 8)
        m["hmask"] = np.array([[0.0 if c == 0 else 1.0]], dtype=np.float32)
        ins.append(m)
    return ins


def kernel(**inputs):
    n_cores, tpc = 8, 1024
    nc, _ = build_program(n_cores, tpc, dbg=0)
    in_maps = _layout(inputs, n_cores, tpc)
    res = run_bass_kernel_spmd(nc, in_maps, core_ids=list(range(n_cores)))
    parts = [np.asarray(r["out_fm"]).T for r in res.results]
    out = np.concatenate(parts, axis=0)[None, :, :]
    return np.ascontiguousarray(out.astype(np.float32))


# version 05 (final)
# speedup vs baseline: 1.0694x; 1.0387x over previous
import numpy as np
import concourse.bass as bass
import concourse.mybir as mybir
from concourse.bass_utils import run_bass_kernel_spmd

F32 = mybir.dt.float32
AF = mybir.ActivationFunctionType
ALU = mybir.AluOpType
AX = mybir.AxisListType.X

D = 2048
H = 8
DH = 128
KEY = 1024
NE = 6
R = 16
HALO = 3
C = 128
CT = C // 2
NT = D // 128
NQ = KEY // 128
SM = R + NE + NE * R
EPS_LN = float(np.float32(1e-5))
EPS_RMS = float(np.float32(1e-5))
INV_D = float(np.float32(1.0) / np.float32(D))
INV_DH = float(np.float32(1.0) / np.float32(DH))
INV_SQRT_DH = float(np.float32(1.0) / np.float32(np.sqrt(np.float32(128.0))))


def spans(width, maxw=512):
    out, o = [], 0
    while o < width:
        w = min(maxw, width - o)
        out.append((o, w))
        o += w
    return out


def build_program(n_cores, tpc, dbg=0):
    TW = tpc + HALO
    NCH = tpc // CT
    NTK = (TW + 127) // 128
    nc = bass.Bass("TRN2", target_bir_lowering=False)

    x_fm = nc.dram_tensor("x_fm", [D, TW], F32, kind="ExternalInput").ap()
    wq_t = nc.dram_tensor("wq_t", [D, KEY], F32, kind="ExternalInput").ap()
    wk_t = nc.dram_tensor("wk_t", [D, D], F32, kind="ExternalInput").ap()
    wv_t = nc.dram_tensor("wv_t", [D, D], F32, kind="ExternalInput").ap()
    wbwa_t = nc.dram_tensor("wbwa_t", [D, 3 * H], F32, kind="ExternalInput").ap()
    wg_t = nc.dram_tensor("wg_t", [D, KEY], F32, kind="ExternalInput").ap()
    wo_t = nc.dram_tensor("wo_t", [KEY, D], F32, kind="ExternalInput").ap()
    a_log = nc.dram_tensor("a_log", [1, H], F32, kind="ExternalInput").ap()
    dt_b = nc.dram_tensor("dt_b", [1, H], F32, kind="ExternalInput").ap()
    conv_q = nc.dram_tensor("conv_q", [KEY, 4], F32, kind="ExternalInput").ap()
    conv_k = nc.dram_tensor("conv_k", [D, 4], F32, kind="ExternalInput").ap()
    conv_v = nc.dram_tensor("conv_v", [D, 4], F32, kind="ExternalInput").ap()
    onw = nc.dram_tensor("onw", [1, DH], F32, kind="ExternalInput").ap()
    w_base_t = nc.dram_tensor("w_base_t", [D, D], F32, kind="ExternalInput").ap()
    b_base_c = nc.dram_tensor("b_base_c", [128, NT], F32, kind="ExternalInput").ap()
    w_small_t = nc.dram_tensor("w_small_t", [D, SM], F32, kind="ExternalInput").ap()
    b_rout = nc.dram_tensor("b_rout", [1, NE], F32, kind="ExternalInput").ap()
    b_s_w = nc.dram_tensor("b_s_w", [R, D], F32, kind="ExternalInput").ap()
    b_r_w = nc.dram_tensor("b_r_w", [NE * R, D], F32, kind="ExternalInput").ap()
    csel = nc.dram_tensor("csel", [1, 8], F32, kind="ExternalInput").ap()
    hmask = nc.dram_tensor("hmask", [1, 1], F32, kind="ExternalInput").ap()
    out_fm = nc.dram_tensor("out_fm", [D, tpc], F32, kind="ExternalOutput").ap()

    x1_sp = nc.dram_tensor("x1_sp", [D, TW], F32, kind="Internal").ap()
    k_sp = nc.dram_tensor("k_sp", [D, TW], F32, kind="Internal").ap()
    v_sp = nc.dram_tensor("v_sp", [D, TW], F32, kind="Internal").ap()
    q_sp = nc.dram_tensor("q_sp", [KEY, TW], F32, kind="Internal").ap()
    gg_sp = nc.dram_tensor("gg_sp", [KEY, tpc], F32, kind="Internal").ap()
    cc_in = [nc.dram_tensor(f"cc_in{h}", [128, 2 * DH], F32, kind="Internal").ap() for h in range(H)]
    cc_out = [nc.dram_tensor(f"cc_out{h}", [n_cores, 128, 2 * DH], F32, kind="Internal",
                             addr_space="Local").ap() for h in range(H)]

    DBG = {}
    if dbg:
        def dout(name, shape):
            DBG[name] = nc.dram_tensor(name, shape, F32, kind="ExternalOutput").ap()
        dout("d_x1", [D, TW]); dout("d_xln", [D, TW])
        dout("d_k", [D, TW]); dout("d_v", [D, TW]); dout("d_q", [KEY, TW])
        dout("d_b", [2 * H, TW]); dout("d_g", [H, TW]); dout("d_cum", [H, TW])
        dout("d_kint", [DH, C]); dout("d_ktm", [C, DH]); dout("d_rhs", [C, 2 * DH])
        dout("d_gst", [C, C]); dout("d_art", [128, 4 * DH]); dout("d_bup", [C, C]); dout("d_alo", [C, C]); dout("d_kbf", [DH, C]); dout("d_kbt", [C, DH])
        dout("d_sin", [DH, DH]); dout("d_och", [C, DH]); dout("d_ofm", [KEY, tpc])
        dout("d_x2", [D, tpc])

    guards = []

    def sb(name, shape, dtype=F32):
        g = nc.sbuf_tensor(name, shape, dtype); t = g.__enter__(); guards.append(g); return t

    def psu(name, shape, dtype=F32):
        g = nc.psum_tensor(name, shape, dtype); t = g.__enter__(); guards.append(g); return t

    big0 = sb("big0", [128, NT * TW])
    big1 = sb("big1", [128, NT * TW])
    work = sb("work", [128, 2 * 128])
    wst = sb("wst", [128, 2 * 1280])
    bbase = sb("bbase", [128, NT])
    consts = sb("consts", [128, 1280])
    crow1 = sb("crow1", [1, 280])
    SELN = max(NTK, H)
    selb = sb("selb", [SELN, SELN * 128])
    sel127 = sb("sel127", [128, 128])
    stepcs = sb("stepcs", [128, NCH * 40])
    rowcs = sb("rowcs", [H, NCH * 128])
    accb = sb("accb", [128, NCH * 8])
    nrmc = sb("nrmc", [128, 8])
    selbc = sb("selbc", [128, 8])
    onrm = sb("onrm", [128, DH])
    GAPFITS = (6 * TW + 4 * CT + 9 * C) <= 8 * TW
    poolh = sb("poolh", [128, 7400 if GAPFITS else 7400 + 9 * C])
    pool = poolh[:, :]
    smw = pool[:, 0:NT * SM]
    sfm = pool[0:R, 1888:1888 + TW]
    rfm = pool[0:NE, 2915:2915 + TW]
    afm = pool[0:NE * R, 3942:3942 + TW]
    hwb = pool[0:NE * R, 4969:4969 + TW]
    wfm = pool[0:NE, 5996:5996 + TW]
    statc = pool[:, 7023:7023 + 5 * NTK]
    statf = pool[0:NTK, 7078:7078 + 256]
    gtm = pool[:, 7338:7338 + 32]
    bgw = pool[:, 0:NT * 24]
    cvw = pool[:, 384:384 + (2 * NT + NQ) * 4]
    bgb = pool[0:2 * H, 544:544 + TW]
    gb = pool[0:H, 1571:1571 + TW]
    cumb = pool[0:H, 2598:2598 + TW]
    kint = pool[:, 0:2 * C]
    qint = pool[:, 256:256 + 2 * C]
    khf = pool[:, 512:512 + 2 * C]
    qhf = pool[:, 768:768 + 2 * C]
    ktm = pool[:, 1024:1024 + 2 * DH]
    kbtm = pool[:, 1280:1280 + 2 * DH]
    rhsb = pool[:, 1536:1536 + 2 * 256]
    apw = pool[:, 2048:2048 + 2 * C]
    bpw = pool[:, 2304:2304 + 7 * C]
    dxp = pool[:, 3200:3200 + C]
    gst = pool[:, 3328:3328 + C]
    kbar = pool[:, 3456:3456 + DH]
    qtl = pool[:, 3584:3584 + C]
    scr = pool[:, 3712:3712 + 6 * C]
    comp = pool[:, 4480:4480 + 3 * DH]
    ccsb = pool[:, 4864:4864 + 2 * DH]
    gsb = pool[:, 5120:5120 + 2 * 256]
    bscr = pool[:, 5632:5632 + 4 * DH]
    prefx = pool[:, 6144:6144 + DH]
    stt = pool[:, 6272:6272 + DH]
    ggch = pool[:, 6400:6400 + 2 * CT]
    och = pool[:, 6528:6528 + 2 * DH]
    qatm = pool[:, 6784:6784 + DH]
    kbfm = pool[:, 6912:6912 + C]
    cwork = big0

    ARTS = big0

    def art(h, c, slot):
        return ARTS[:, (h % 2) * (NCH * 4 * DH) + c * 4 * DH + slot * DH:
                    (h % 2) * (NCH * 4 * DH) + c * 4 * DH + (slot + 1) * DH]

    def khv(h, part):
        b = ((h % 2) * 2 + part) * TW
        return big1[:, b:b + TW]

    def qhv(h):
        b = 4 * TW + (h % 2) * TW
        return big1[:, b:b + TW]

    def vchv(c, part):
        b = 6 * TW + ((c % 2) * 2 + part) * CT
        return big1[:, b:b + CT]

    OFM_OFF = 8 * TW

    GAP0 = 6 * TW + 4 * CT
    SOLVE_DUAL = True
    SDEPTH = 2 if SOLVE_DUAL else 1
    _bp2 = big1[:, GAP0:GAP0 + 7 * C] if GAPFITS else pool[:, 7400:7400 + 7 * C]
    _ap2 = big1[:, GAP0 + 7 * C:GAP0 + 9 * C] if GAPFITS else pool[:, 7400 + 7 * C:7400 + 9 * C]

    def bpwv(idx):
        return bpw if idx % 2 == 0 else _bp2

    def apwv(idx):
        return apw if idx % 2 == 0 else _ap2

    def gstv(idx):
        return gst if idx % 2 == 0 else pool[:, 7168:7168 + C]

    def ofmv(h):
        return big1[:, OFM_OFF + h * tpc:OFM_OFF + (h + 1) * tpc]

    pLN = psu("pLN", [128, 512])
    pBC = psu("pBC", [128, 512])
    pFF = psu("pFF", [128, 512])
    pFF2 = psu("pFF2", [128, 512])
    pRW = psu("pRW", [128, 512])
    pSQ = psu("pSQ", [128, 512])
    pAP = psu("pAP", [128, 512])
    pMP = psu("pMP", [128, 512])

    sems, cnt, ev = {}, {}, {}
    for s in ["qS", "qA", "qG", "tT", "tV", "tS", "tG", "cc"]:
        g = nc.semaphore(s); sems[s] = g.__enter__(); guards.append(g); cnt[s] = 0

    def inc(s, by=1):
        cnt[s] += by
        return cnt[s]

    def mark(name, s):
        ev[name] = (s, cnt[s])

    def wt(eng, name):
        s, v = ev[name]
        if v > 0:
            eng.wait_ge(sems[s], v)

    def barrier(tagb):
        vals = [(s, cnt[s]) for s in ["tT", "tV", "tS", "tG", "qS", "qA", "qG", "cc"] if cnt[s] > 0]

        @blk.tensor
        def _(e):
            for s, v in vals:
                e.wait_ge(sems[s], v)
            e.matmul(pLN[0:1, 40:41], ONESC[0:1, 0:1], ONESC[0:1, 0:1], start=True, stop=True).then_inc(sems["tT"], 1)
        inc("tT")
        mark(f"bar_{tagb}_t", "tT")

        @blk.vector
        def _(e):
            for s, v in vals:
                e.wait_ge(sems[s], v)
            e.sem_inc(sems["tV"], 1)
        inc("tV")

        @blk.scalar
        def _(e):
            for s, v in vals:
                e.wait_ge(sems[s], v)
            e.sem_inc(sems["tS"], 1)
        inc("tS")

        @blk.sync
        def _(e):
            for s, v in vals:
                e.wait_ge(sems[s], v)

    def gate(eng, qn):
        if cnt[qn] > 0:
            eng.wait_ge(sems[qn], cnt[qn])

    blk = nc.Block().__enter__()
    guards.append(blk)

    I128 = consts[:, 0:128]
    E_evT = consts[0:64, 128:256]
    E_odT = consts[0:64, 720:848]
    E_boT = consts[0:64, 256:384]
    E6 = consts[0:NE, 448:544]
    NEXPA = consts[0:H, 560:561]
    DTC = consts[0:H, 561:562]
    BRT = consts[0:128, 576:582]
    ONESC = consts[:, 640:641]
    EPSC = consts[:, 648:649]
    INVDHC = consts[:, 649:650]
    HMC = consts[:, 652:653]
    EPSN = consts[:, 653:654]
    ONES_T = consts[0:H, 656:656 + CT]
    MSU = consts[:, 896:1024]
    MSL = consts[:, 1024:1152]
    MGU = consts[:, 1152:1280]
    ONESR = crow1[0:1, 0:128]

    @blk.sync
    def _(e):
        gate(e, "qS")
        for f in range(NT):
            e.dma_start(out=big0[:, f * TW:(f + 1) * TW], in_=x_fm[f * 128:(f + 1) * 128, :]).then_inc(sems["qS"], 16)
        inc("qS", 16 * NT)
        mark("x_loaded", "qS")

    @blk.scalar
    def _(e):
        n = 0
        e.wait_ge(sems["tG"], 1)
        gate(e, "qA")
        for f in range(NT):
            e.dma_start(out=smw[:, f * SM:(f + 1) * SM], in_=w_small_t[f * 128:(f + 1) * 128, :]).then_inc(sems["qA"], 16); n += 1

        e.dma_start(out=bbase[:, :], in_=b_base_c).then_inc(sems["qA"], 16); n += 1
        e.dma_start(out=crow1[0:1, 256:256 + NE], in_=b_rout).then_inc(sems["qA"], 16); n += 1
        e.dma_start(out=consts[0:1, 544:544 + H], in_=a_log).then_inc(sems["qA"], 16); n += 1
        e.dma_start(out=consts[0:1, 552:552 + H], in_=dt_b).then_inc(sems["qA"], 16); n += 1
        e.dma_start(out=crow1[0:1, 128:128 + DH], in_=onw).then_inc(sems["qA"], 16); n += 1
        e.dma_start(out=crow1[0:1, 262:270], in_=csel).then_inc(sems["qA"], 16); n += 1
        e.dma_start(out=crow1[0:1, 270:271], in_=hmask).then_inc(sems["qA"], 16); n += 1

        inc("qA", 16 * n)
        mark("smalls_loaded", "qA")

    @blk.gpsimd
    def _(e):
        e.memset(consts[:, 0:544], 1.0)
        e.memset(consts[:, 544:640], 0.0)
        e.memset(consts[:, 640:1280], 1.0)
        e.memset(crow1[0:1, 0:128], 1.0)
        e.memset(selb[:, :], 1.0)
        e.memset(sel127[:, :], 1.0)
        e.memset(poolh[:, :], 0.0)
        e.memset(work[:, :], 0.0)
        e.drain()
        e.memset(EPSC, EPS_LN)
        e.memset(INVDHC, INV_DH)
        e.memset(EPSN, 1e-24)
        e.drain()
        e.affine_select(I128, I128, pattern=[[1, 128]], compare_op=ALU.is_equal, fill=0.0, base=0, channel_multiplier=-1)
        e.memset(E_odT, 1.0)
        e.drain()
        e.affine_select(E_evT, E_evT, pattern=[[1, 128]], compare_op=ALU.is_equal, fill=0.0, base=0, channel_multiplier=-2)
        e.affine_select(E_odT, E_odT, pattern=[[1, 128]], compare_op=ALU.is_equal, fill=0.0, base=-1, channel_multiplier=-2)
        e.affine_select(E_boT, E_boT, pattern=[[1, 128]], compare_op=ALU.is_ge, fill=0.0, base=0, channel_multiplier=-2)
        e.affine_select(E6, E6, pattern=[[1, 96]], compare_op=ALU.is_ge, fill=0.0, base=0, channel_multiplier=-16)
        e.affine_select(MSU, MSU, pattern=[[1, 128]], compare_op=ALU.is_ge, fill=0.0, base=-1, channel_multiplier=-1)
        e.affine_select(MSL, MSL, pattern=[[-1, 128]], compare_op=ALU.is_ge, fill=0.0, base=-1, channel_multiplier=1)
        e.affine_select(MGU, MGU, pattern=[[1, 128]], compare_op=ALU.is_ge, fill=0.0, base=0, channel_multiplier=-1)
        e.affine_select(selb[:, :], selb[:, :], pattern=[[1, SELN * 128]], compare_op=ALU.is_ge, fill=0.0, base=0, channel_multiplier=-128)
        e.drain()
        e.affine_select(selb[:, :], selb[:, :], pattern=[[-1, SELN * 128]], compare_op=ALU.is_ge, fill=0.0, base=127, channel_multiplier=128)
        e.affine_select(sel127[:, :], sel127[:, :], pattern=[[0, 128]], compare_op=ALU.is_equal, fill=0.0, base=-127, channel_multiplier=1)
        e.drain()
        e.affine_select(E_boT, E_boT, pattern=[[-1, 128]], compare_op=ALU.is_ge, fill=0.0, base=1, channel_multiplier=2)
        e.affine_select(E6, E6, pattern=[[-1, 96]], compare_op=ALU.is_ge, fill=0.0, base=15, channel_multiplier=16)
        e.drain()
        e.sem_inc(sems["tG"], 1)
        inc("tG")
        mark("consts_ready", "tG")

    @blk.tensor
    def _(e):
        wt(e, "smalls_loaded")
        wt(e, "consts_ready")
        e.matmul(pBC[:, 0:NE], ONESR, crow1[0:1, 256:256 + NE], start=True, stop=True)
        e.matmul(pBC[:, 384:392], ONESR, crow1[0:1, 262:270], start=True, stop=True)
        e.matmul(pBC[:, 392:393], ONESR, crow1[0:1, 270:271], start=True, stop=True)
        e.transpose(pBC[0:DH, 264:265], crow1[0:1, 128:128 + DH], I128[0:1, 0:1])
        e.transpose(pBC[0:H, 256:257], consts[0:1, 544:544 + H], I128[0:1, 0:1])
        e.transpose(pBC[0:H, 260:261], consts[0:1, 552:552 + H], I128[0:1, 0:1]).then_inc(sems["tT"], 1)
        inc("tT")
        mark("tiny_tr", "tT")

    @blk.vector
    def _(e):
        wt(e, "tiny_tr")
        e.tensor_copy(BRT, pBC[:, 0:NE])
        e.tensor_copy(onrm[:, 0:1], pBC[0:DH, 264:265])
        e.tensor_copy(selbc[:, :], pBC[:, 384:392])
        e.tensor_copy(HMC, pBC[:, 392:393])
        e.tensor_copy(DTC, pBC[0:H, 260:261])
        e.drain()
        e.sem_inc(sems["tV"], 1)
        inc("tV")
        mark("tiny_v", "tV")

    @blk.scalar
    def _(e):
        wt(e, "tiny_v")
        e.activation(NEXPA, pBC[0:H, 256:257], AF.Exp)
        e.drain()
        e.sem_inc(sems["tS"], 1)
        inc("tS")
        mark("tiny_s", "tS")

    @blk.vector
    def _(e):
        wt(e, "tiny_s")
        e.tensor_scalar(NEXPA, NEXPA, -1.0, None, ALU.mult)
        e.drain()
        e.sem_inc(sems["tV"], 1)
        inc("tV")
        mark("consts2", "tV")

    SEQ = {"ln": 0}

    def emit_ln(src, dst, width, ready_evs):
        tch = spans(width, 128)
        tag = f"ln{SEQ['ln']}"; SEQ["ln"] += 1
        ncol = len(tch)

        for ci, (o, w) in enumerate(tch):
            for f in range(NT):
                @blk.scalar
                def _(e, ci=ci, o=o, w=w, f=f):
                    for evn in ready_evs:
                        wt(e, evn)
                    k = ci * NT + f
                    if k >= 2:
                        pci, pf = divmod(k - 2, NT)
                        wt(e, f"{tag}_sq{pci}_{pf}")
                    e.activation(work[:, (f % 2) * 128:(f % 2) * 128 + w],
                                 src[:, f * width + o:f * width + o + w], AF.Square)
                    e.drain()
                    e.sem_inc(sems["tS"], 1)
                    inc("tS")
                    mark(f"{tag}_sqs{ci}_{f}", "tS")

                @blk.tensor
                def _(e, ci=ci, o=o, w=w, f=f):
                    wt(e, f"{tag}_sqs{ci}_{f}")
                    if ci == 0 and f == 0:
                        wt(e, "consts2")
                    e.matmul(pLN[0:w, ci:ci + 1], src[:, f * width + o:f * width + o + w], ONESC,
                             start=(f == 0), stop=(f == NT - 1))
                    e.matmul(pBC[0:w, ci:ci + 1], work[:, (f % 2) * 128:(f % 2) * 128 + w], ONESC,
                             start=(f == 0), stop=(f == NT - 1)).then_inc(sems["tT"], 1)
                    inc("tT")
                    mark(f"{tag}_sq{ci}_{f}", "tT")

        @blk.vector
        def _(e):
            wt(e, f"{tag}_sq{len(tch) - 1}_{NT - 1}")
            for ci, (o, w) in enumerate(tch):
                e.tensor_scalar(statc[0:w, ci:ci + 1], pLN[0:w, ci:ci + 1], INV_D, None, ALU.mult)
                e.tensor_scalar(statc[0:w, NTK + ci:NTK + ci + 1], pBC[0:w, ci:ci + 1], INV_D, None, ALU.mult)
            e.drain()
            e.tensor_tensor(statc[:, 2 * NTK:2 * NTK + ncol], statc[:, 0:ncol], statc[:, 0:ncol], ALU.mult)
            e.drain()
            e.tensor_sub(statc[:, NTK:NTK + ncol], statc[:, NTK:NTK + ncol], statc[:, 2 * NTK:2 * NTK + ncol])
            e.drain()
            e.sem_inc(sems["tV"], 1)
            inc("tV")
            mark(f"{tag}_var", "tV")

        @blk.scalar
        def _(e):
            wt(e, f"{tag}_var")
            e.activation(statc[:, 2 * NTK:2 * NTK + ncol], statc[:, NTK:NTK + ncol], AF.Sqrt, bias=EPSC)
            e.drain()
            e.sem_inc(sems["tS"], 1)
            inc("tS")
            mark(f"{tag}_inv", "tS")

        @blk.vector
        def _(e):
            wt(e, f"{tag}_inv")
            e.reciprocal(statc[:, 2 * NTK:2 * NTK + ncol], statc[:, 2 * NTK:2 * NTK + ncol])
            e.drain()
            e.tensor_tensor(statc[:, 3 * NTK:3 * NTK + ncol], statc[:, 0:ncol],
                            statc[:, 2 * NTK:2 * NTK + ncol], ALU.mult)
            e.drain()
            e.sem_inc(sems["tV"], 1)
            inc("tV")
            mark(f"{tag}_minv", "tV")

        @blk.tensor
        def _(e):
            wt(e, f"{tag}_minv")
            e.transpose(pLN[0:ncol, 128:256], statc[:, 2 * NTK:2 * NTK + ncol], I128)
            e.transpose(pLN[0:ncol, 256:384], statc[:, 3 * NTK:3 * NTK + ncol], I128).then_inc(sems["tT"], 1)
            inc("tT")
            mark(f"{tag}_strows", "tT")

        @blk.vector
        def _(e):
            wt(e, f"{tag}_strows")
            e.tensor_copy(statf[0:ncol, 0:128], pLN[0:ncol, 128:256])
            e.tensor_copy(statf[0:ncol, 128:256], pLN[0:ncol, 256:384])
            e.drain()
            e.sem_inc(sems["tV"], 1)
            inc("tV")
            mark(f"{tag}_stf", "tV")

        for ci, (o, w) in enumerate(tch):
            @blk.tensor
            def _(e, ci=ci, o=o, w=w):
                wt(e, f"{tag}_stf")
                if ci >= 2:
                    wt(e, f"{tag}_ap{ci - 2}")
                pbk = pBC if ci % 2 == 0 else pLN
                e.matmul(pbk[:, 0:w], selb[0:NTK, ci * 128:ci * 128 + 128], statf[0:NTK, 0:w], start=True, stop=True)
                e.matmul(pbk[:, 128:128 + w], selb[0:NTK, ci * 128:ci * 128 + 128], statf[0:NTK, 128:128 + w],
                         start=True, stop=True).then_inc(sems["tT"], 1)
                inc("tT")
                mark(f"{tag}_bc{ci}", "tT")

            @blk.vector
            def _(e, ci=ci, o=o, w=w):
                wt(e, f"{tag}_bc{ci}")
                pbk = pBC if ci % 2 == 0 else pLN
                for f in range(NT):
                    e.tensor_tensor(dst[:, f * width + o:f * width + o + w],
                                    src[:, f * width + o:f * width + o + w],
                                    pbk[:, 0:w], ALU.mult)
                e.drain()
                for f in range(NT):
                    e.tensor_sub(dst[:, f * width + o:f * width + o + w],
                                 dst[:, f * width + o:f * width + o + w],
                                 pbk[:, 128:128 + w])
                e.drain()
                e.sem_inc(sems["tV"], 1)
                inc("tV")
                mark(f"{tag}_ap{ci}", "tV")

        return f"{tag}_ap{len(tch) - 1}"

    def emit_ffn(xarena, starena, width, ready_evs, tag):
        tch = spans(width, 128)
        sp512 = spans(width, 512)

        for si, (o, w) in enumerate(sp512):
            @blk.tensor
            def _(e, si=si, o=o, w=w):
                for evn in ready_evs:
                    wt(e, evn)
                wt(e, "consts2")
                if si > 0:
                    wt(e, f"{tag}_spe{si - 1}")
                for f in range(NT):
                    e.matmul(pFF[0:R, 0:w], smw[:, f * SM:f * SM + R],
                             xarena[:, f * width + o:f * width + o + w],
                             start=(f == 0), stop=(f == NT - 1))
                for f in range(NT):
                    e.matmul(pLN[0:NE, 0:w], smw[:, f * SM + R:f * SM + R + NE],
                             xarena[:, f * width + o:f * width + o + w],
                             start=(f == 0), stop=(f == NT - 1))
                for f in range(NT):
                    mm = e.matmul(pFF2[0:NE * R, 0:w], smw[:, f * SM + R + NE:(f + 1) * SM],
                                  xarena[:, f * width + o:f * width + o + w],
                                  start=(f == 0), stop=(f == NT - 1))
                mm.then_inc(sems["tT"], 1)
                inc("tT")
                mark(f"{tag}_sp{si}", "tT")

            @blk.vector
            def _(e, si=si, o=o, w=w):
                wt(e, f"{tag}_sp{si}")
                e.tensor_copy(sfm[:, o:o + w], pFF[0:R, 0:w])
                e.tensor_copy(rfm[:, o:o + w], pLN[0:NE, 0:w])
                e.tensor_copy(afm[:, o:o + w], pFF2[0:NE * R, 0:w])
                e.drain()
                e.sem_inc(sems["tV"], 1)
                inc("tV")
                mark(f"{tag}_spe{si}", "tV")

        for ci, (o, w) in enumerate(tch):
            @blk.tensor
            def _(e, ci=ci, o=o, w=w):
                wt(e, f"{tag}_spe{len(sp512) - 1}")
                if ci > 0:
                    wt(e, f"{tag}_wf{ci - 1}")
                e.transpose(pBC[0:w, 0:NE], rfm[0:NE, o:o + w], I128[0:NE, 0:NE]).then_inc(sems["tT"], 1)
                inc("tT")
                mark(f"{tag}_rt{ci}", "tT")

            @blk.vector
            def _(e, ci=ci, o=o, w=w):
                wt(e, f"{tag}_rt{ci}")
                e.tensor_tensor(gtm[0:w, 0:NE], pBC[0:w, 0:NE], BRT[0:w, :], ALU.add)
                e.drain()
                e.tensor_reduce(gtm[0:w, 8:9], gtm[0:w, 0:NE], AX, ALU.max, negate=True)
                e.drain()
                e.sem_inc(sems["tV"], 1)
                inc("tV")
                mark(f"{tag}_z{ci}", "tV")

            @blk.scalar
            def _(e, ci=ci, o=o, w=w):
                wt(e, f"{tag}_z{ci}")
                e.activation(gtm[0:w, 16:16 + NE], gtm[0:w, 0:NE], AF.Exp, bias=gtm[0:w, 8:9])
                e.drain()
                e.sem_inc(sems["tS"], 1)
                inc("tS")
                mark(f"{tag}_e{ci}", "tS")

            @blk.vector
            def _(e, ci=ci, o=o, w=w):
                wt(e, f"{tag}_e{ci}")
                eT = gtm[0:w, 16:16 + NE]
                e.tensor_reduce(gtm[0:w, 9:10], eT, AX, ALU.add)
                e.tensor_reduce(gtm[0:w, 10:11], eT, AX, ALU.max)
                e.drain()
                e.reciprocal(gtm[0:w, 11:12], gtm[0:w, 9:10])
                e.tensor_scalar(gtm[0:w, 24:24 + NE], eT, gtm[0:w, 10:11], None, ALU.is_ge)
                e.drain()
                e.scalar_tensor_tensor(gtm[0:w, 24:24 + NE], gtm[0:w, 24:24 + NE], -2.0, eT, ALU.mult, ALU.add)
                e.drain()
                e.tensor_reduce(gtm[0:w, 12:13], gtm[0:w, 24:24 + NE], AX, ALU.max)
                e.drain()
                e.tensor_scalar(gtm[0:w, 24:24 + NE], eT, gtm[0:w, 12:13], None, ALU.is_ge)
                e.drain()
                e.scalar_tensor_tensor(gtm[0:w, 0:NE], eT, gtm[0:w, 11:12], gtm[0:w, 24:24 + NE], ALU.mult, ALU.mult)
                e.drain()
                e.sem_inc(sems["tV"], 1)
                inc("tV")
                mark(f"{tag}_w{ci}", "tV")

            @blk.tensor
            def _(e, ci=ci, o=o, w=w):
                wt(e, f"{tag}_w{ci}")
                e.transpose(pBC[0:NE, 128:128 + w], gtm[0:w, 0:NE], I128[0:w, 0:w]).then_inc(sems["tT"], 1)
                inc("tT")
                mark(f"{tag}_wt{ci}", "tT")

            @blk.vector
            def _(e, ci=ci, o=o, w=w):
                wt(e, f"{tag}_wt{ci}")
                e.tensor_copy(wfm[:, o:o + w], pBC[0:NE, 128:128 + w])
                e.drain()
                e.sem_inc(sems["tV"], 1)
                inc("tV")
                mark(f"{tag}_wf{ci}", "tV")

        for si, (o, w) in enumerate(sp512):
            @blk.tensor
            def _(e, si=si, o=o, w=w):
                wt(e, f"{tag}_wf{len(tch) - 1}")
                if si > 0:
                    wt(e, f"{tag}_hwe{si - 1}")
                e.matmul(pFF[0:NE * R, 0:w], E6, wfm[:, o:o + w], start=True, stop=True).then_inc(sems["tT"], 1)
                inc("tT")
                mark(f"{tag}_hw{si}", "tT")

            @blk.vector
            def _(e, si=si, o=o, w=w):
                wt(e, f"{tag}_hw{si}")
                e.tensor_tensor(hwb[:, o:o + w], afm[:, o:o + w], pFF[0:NE * R, 0:w], ALU.mult)
                e.drain()
                e.sem_inc(sems["tV"], 1)
                inc("tV")
                mark(f"{tag}_hwe{si}", "tV")

        def wld_batch(e, fo, hv):
            qn = "qS" if (fo * 2 + hv) % 2 == 0 else "qG"
            gate(e, qn)
            slot = (fo * 2 + hv) % 2
            nb = 0
            for f8 in range(8):
                f = hv * 8 + f8
                e.dma_start(out=wst[:, slot * 1280 + f8 * 128:slot * 1280 + (f8 + 1) * 128],
                            in_=w_base_t[f * 128:(f + 1) * 128, fo * 128:(fo + 1) * 128]).then_inc(sems[qn], 16)
                nb += 1
            if hv == 1:
                e.dma_start(out=wst[0:R, slot * 1280 + 1024:slot * 1280 + 1152],
                            in_=b_s_w[:, fo * 128:(fo + 1) * 128]).then_inc(sems[qn], 16)
                e.dma_start(out=wst[0:NE * R, slot * 1280 + 1152:slot * 1280 + 1280],
                            in_=b_r_w[:, fo * 128:(fo + 1) * 128]).then_inc(sems[qn], 16)
                nb += 2
            inc(qn, 16 * nb)
            mark(f"{tag}_wld{fo}_{hv}", qn)

        @blk.sync
        def _(e):
            for evn in ready_evs:
                wt(e, evn)
            wld_batch(e, 0, 0)
            wld_batch(e, 0, 1)

        nsp = len(sp512)
        BANKS = [pFF, pFF2, pLN]
        BOFF = [0, 0, 64]

        for fo in range(NT):
            for hv in range(2):
                @blk.tensor
                def _(e, fo=fo, hv=hv):
                    wt(e, f"{tag}_wld{fo}_{hv}")
                    wt(e, f"{tag}_hwe{nsp - 1}")
                    if hv == 0 and fo >= 1:
                        for psi in range(nsp):
                            wt(e, f"{tag}_be{fo - 1}_{psi}")
                    slot = (fo * 2 + hv) % 2
                    last = None
                    for si, (o, w) in enumerate(sp512):
                        pb = BANKS[si]
                        po = BOFF[si]
                        for f8 in range(8):
                            last = e.matmul(pb[:, po:po + w], wst[:, slot * 1280 + f8 * 128:slot * 1280 + (f8 + 1) * 128],
                                            xarena[:, (hv * 8 + f8) * width + o:(hv * 8 + f8) * width + o + w],
                                            start=(hv == 0 and f8 == 0), stop=False)
                        if hv == 1:
                            e.matmul(pb[:, po:po + w], wst[0:R, slot * 1280 + 1024:slot * 1280 + 1152],
                                     sfm[:, o:o + w], start=False, stop=False)
                            last = e.matmul(pb[:, po:po + w], wst[0:NE * R, slot * 1280 + 1152:slot * 1280 + 1280],
                                            hwb[:, o:o + w], start=False, stop=True)
                    last.then_inc(sems["tT"], 1)
                    inc("tT")
                    mark(f"{tag}_b{fo}_{hv}", "tT")

                nxt = fo * 2 + hv + 2
                if nxt < NT * 2:
                    @blk.sync
                    def _(e, fo=fo, hv=hv, nxt=nxt):
                        wt(e, f"{tag}_b{fo}_{hv}")
                        wld_batch(e, nxt // 2, nxt % 2)

            for si, (o, w) in enumerate(sp512):
                @blk.vector
                def _(e, fo=fo, si=si, o=o, w=w):
                    wt(e, f"{tag}_b{fo}_{1}")
                    pb = BANKS[si]
                    po = BOFF[si]
                    e.tensor_scalar(scr[:, 0:w], pb[:, po:po + w], bbase[:, fo:fo + 1], 0.5, ALU.add, ALU.mult)
                    e.drain()
                    e.tensor_tensor(starena[:, fo * width + o:fo * width + o + w],
                                    starena[:, fo * width + o:fo * width + o + w], scr[:, 0:w], ALU.add)
                    e.drain()
                    e.sem_inc(sems["tV"], 1)
                    inc("tV")
                    mark(f"{tag}_be{fo}_{si}", "tV")

        return f"{tag}_be{NT - 1}_{nsp - 1}"

    ln0_done = emit_ln(big0, big1, TW, ["x_loaded"])
    ffn1_done = emit_ffn(big1, big0, TW, [ln0_done], "fA")

    if dbg:
        @blk.sync
        def _(e):
            wt(e, ffn1_done)
            gate(e, "qS")
            for f in range(NT):
                e.dma_start(out=DBG["d_x1"][f * 128:(f + 1) * 128, :], in_=big0[:, f * TW:(f + 1) * TW]).then_inc(sems["qS"], 16)
            inc("qS", 16 * NT)
            mark("dbg_x1", "qS")

    barrier("ffn1")
    ln1_done = emit_ln(big0, big1, TW, [ffn1_done])

    @blk.scalar
    def _(e):
        wt(e, ln1_done)
        gate(e, "qA")
        for f in range(NT):
            e.dma_start(out=x1_sp[f * 128:(f + 1) * 128, :], in_=big0[:, f * TW:(f + 1) * TW]).then_inc(sems["qA"], 16)
        inc("qA", 16 * NT)
        mark("x1_spilled", "qA")

    @blk.scalar
    def _(e):
        wt(e, ln1_done)
        wt(e, "x1_spilled")
        gate(e, "qA")
        nbg = 0
        for f in range(NT):
            e.dma_start(out=bgw[:, f * 24:(f + 1) * 24], in_=wbwa_t[f * 128:(f + 1) * 128, :]).then_inc(sems["qA"], 16); nbg += 1
        for f in range(NT):
            e.dma_start(out=cvw[:, f * 4:(f + 1) * 4], in_=conv_k[f * 128:(f + 1) * 128, :]).then_inc(sems["qA"], 16); nbg += 1
        for f in range(NT):
            e.dma_start(out=cvw[:, (NT + f) * 4:(NT + f + 1) * 4], in_=conv_v[f * 128:(f + 1) * 128, :]).then_inc(sems["qA"], 16); nbg += 1
        for f in range(NQ):
            e.dma_start(out=cvw[:, (2 * NT + f) * 4:(2 * NT + f + 1) * 4], in_=conv_q[f * 128:(f + 1) * 128, :]).then_inc(sems["qA"], 16); nbg += 1
        inc("qA", 16 * nbg)
        mark("bgcv_loaded", "qA")

    if dbg:
        @blk.sync
        def _(e):
            wt(e, ln1_done)
            gate(e, "qS")
            for f in range(NT):
                e.dma_start(out=DBG["d_xln"][f * 128:(f + 1) * 128, :], in_=big1[:, f * TW:(f + 1) * TW]).then_inc(sems["qS"], 16)
            inc("qS", 16 * NT)
            mark("dbg_xln", "qS")

    def emit_proj(name, w_dram, n_fo, conv_base, dst_spill, has_conv, width, tag, prev_sp=None):
        sp512 = spans(width, 512)
        nsp = len(sp512)
        for fo in range(n_fo):
            def pwld_batch(e, fo, hv):
                qn = "qS" if (fo * 2 + hv) % 2 == 0 else "qG"
                gate(e, qn)
                slot = (fo * 2 + hv) % 2
                for f8 in range(8):
                    f = hv * 8 + f8
                    e.dma_start(out=wst[:, slot * 1280 + f8 * 128:slot * 1280 + (f8 + 1) * 128],
                                in_=w_dram[f * 128:(f + 1) * 128, fo * 128:(fo + 1) * 128]).then_inc(sems[qn], 16)
                inc(qn, 16 * 8)
                mark(f"{tag}_wld{fo}_{hv}", qn)

            if fo == 0:
                @blk.sync
                def _(e, fo=fo):
                    wt(e, "x1_spilled")
                    pwld_batch(e, 0, 0)
                    pwld_batch(e, 0, 1)

            for hv in range(2):
                @blk.tensor
                def _(e, fo=fo, hv=hv):
                    wt(e, f"{tag}_wld{fo}_{hv}")
                    wt(e, ln1_done)
                    if hv == 0 and fo >= 1:
                        for psi in range(nsp):
                            wt(e, f"{tag}_pes{fo - 1}_{psi}")
                    slot = (fo * 2 + hv) % 2
                    last = None
                    for si, (o, w) in enumerate(sp512):
                        pb = [pFF, pFF2, pLN][si]
                        po = [0, 0, 64][si]
                        for f8 in range(8):
                            last = e.matmul(pb[:, po:po + w], wst[:, slot * 1280 + f8 * 128:slot * 1280 + (f8 + 1) * 128],
                                            big1[:, (hv * 8 + f8) * width + o:(hv * 8 + f8) * width + o + w],
                                            start=(hv == 0 and f8 == 0), stop=(hv == 1 and f8 == 7))
                    last.then_inc(sems["tT"], 1)
                    inc("tT")
                    mark(f"{tag}_b{fo}_{hv}", "tT")
                    if hv == 1:
                        mark(f"{tag}_mm{fo}", "tT")

                nxt = fo * 2 + hv + 2
                if nxt < n_fo * 2:
                    @blk.sync
                    def _(e, fo=fo, hv=hv, nxt=nxt):
                        wt(e, f"{tag}_b{fo}_{hv}")
                        pwld_batch(e, nxt // 2, nxt % 2)

            for si, (o, w) in enumerate(sp512):
                @blk.vector
                def _(e, fo=fo, si=si, o=o, w=w):
                    wt(e, f"{tag}_b{fo}_{1}")
                    if fo >= 2 and si == 0:
                        wt(e, f"{tag}_cv{fo - 2}")
                    pb = [pFF, pFF2, pLN][si]
                    po = [0, 0, 64][si]
                    e.tensor_copy(cwork[:, (fo % 2) * width + o:(fo % 2) * width + o + w], pb[:, po:po + w])
                    e.drain()
                    e.sem_inc(sems["tV"], 1)
                    inc("tV")
                    mark(f"{tag}_pes{fo}_{si}", "tV")

            @blk.vector
            def _(e, fo=fo):
                wt(e, f"{tag}_pes{fo}_{nsp - 1}")
                if fo >= 2:
                    wt(e, f"{tag}_sp{fo - 2}")
                elif prev_sp is not None:
                    wt(e, prev_sp)
                wt(e, "bgcv_loaded")
                wtile = cwork[:, (fo % 2) * width:(fo % 2) * width + width]
                dtile = cwork[:, (2 + fo % 2) * width:(2 + fo % 2) * width + width]
                if has_conv:
                    cw = cvw[:, (conv_base + fo) * 4:(conv_base + fo) * 4 + 4]
                    e.tensor_scalar(wtile[:, 0:HALO], wtile[:, 0:HALO], HMC[:, 0:1], None, ALU.mult)
                    e.drain()
                    e.tensor_scalar(dtile[:, HALO:width], wtile[:, 0:width - HALO], cw[:, 0:1], None, ALU.mult)
                    e.drain()
                    for j in range(1, 4):
                        e.scalar_tensor_tensor(dtile[:, HALO:width], wtile[:, j:j + width - HALO], cw[:, j:j + 1],
                                               dtile[:, HALO:width], ALU.mult, ALU.add)
                        e.drain()
                else:
                    e.tensor_copy(dtile[:, HALO:width], wtile[:, HALO:width])
                    e.drain()
                e.sem_inc(sems["tV"], 1)
                inc("tV")
                mark(f"{tag}_cv{fo}", "tV")

            @blk.scalar
            def _(e, fo=fo):
                wt(e, f"{tag}_cv{fo}")
                dtile = cwork[:, (2 + fo % 2) * width:(2 + fo % 2) * width + width]
                e.activation(dtile[:, HALO:width], dtile[:, HALO:width], AF.Silu)
                e.drain()
                gate(e, "qA")
                if dst_spill is gg_sp:
                    e.dma_start(out=dst_spill[fo * 128:(fo + 1) * 128, :], in_=dtile[:, HALO:width]).then_inc(sems["qA"], 16)
                else:
                    e.dma_start(out=dst_spill[fo * 128:(fo + 1) * 128, HALO:width], in_=dtile[:, HALO:width]).then_inc(sems["qA"], 16)
                inc("qA", 16)
                mark(f"{tag}_sp{fo}", "qA")
                e.sem_inc(sems["tS"], 1)
                inc("tS")
                mark(f"{tag}_pe{fo}", "tS")

            if dbg and name in ("k", "v", "q"):
                @blk.sync
                def _(e, fo=fo):
                    wt(e, f"{tag}_pe{fo}")
                    gate(e, "qS")
                    dd = {"k": "d_k", "v": "d_v", "q": "d_q"}[name]
                    dtile = cwork[:, (2 + fo % 2) * width:(2 + fo % 2) * width + width]
                    e.dma_start(out=DBG[dd][fo * 128:(fo + 1) * 128, HALO:width], in_=dtile[:, HALO:width]).then_inc(sems["qS"], 16)
                    inc("qS", 16)
                    mark(f"dbg_{name}{fo}", "qS")
        return f"{tag}_sp{n_fo - 1}", f"{tag}_pe{n_fo - 1}"

    barrier("ln1")
    kp_sp, kp_pe = emit_proj("k", wk_t, NT, 0, k_sp, True, TW, "pk")
    barrier("pk")
    vp_sp, vp_pe = emit_proj("v", wv_t, NT, NT, v_sp, True, TW, "pv", prev_sp=kp_sp)
    barrier("pv")
    qp_sp, qp_pe = emit_proj("q", wq_t, NQ, 2 * NT, q_sp, True, TW, "pq", prev_sp=vp_sp)
    barrier("pq")
    gp_sp, gp_pe = emit_proj("gg", wg_t, NQ, 0, gg_sp, False, TW, "pg", prev_sp=qp_sp)

    NSPB = len(spans(TW, 512))
    for si, (o, w) in enumerate(spans(TW, 512)):
        @blk.tensor
        def _(e, si=si, o=o, w=w):
            wt(e, gp_pe)
            wt(e, ln1_done)
            wt(e, "bgcv_loaded")
            if si > 0:
                wt(e, f"bg_v{si - 1}")
            for f in range(NT):
                mm = e.matmul(pFF[0:2 * H, 0:w], bgw[:, f * 24:f * 24 + 16],
                              big1[:, f * TW + o:f * TW + o + w], start=(f == 0), stop=(f == NT - 1))
            for f in range(NT):
                mm2 = e.matmul(pFF2[0:H, 0:w], bgw[:, f * 24 + 16:f * 24 + 24],
                               big1[:, f * TW + o:f * TW + o + w], start=(f == 0), stop=(f == NT - 1))
            mm2.then_inc(sems["tT"], 1)
            inc("tT")
            mark(f"bg_mm{si}", "tT")

        @blk.scalar
        def _(e, si=si, o=o, w=w):
            wt(e, f"bg_mm{si}")
            e.activation(bgb[:, o:o + w], pFF[0:2 * H, 0:w], AF.Sigmoid)
            e.activation(scr[0:H, 0:w], pFF2[0:H, 0:w], AF.Abs, bias=DTC)
            e.drain()
            e.activation(scr[0:H, 0:w], scr[0:H, 0:w], AF.Exp, scale=-1.0)
            e.drain()
            e.activation(scr[0:H, 0:w], scr[0:H, 0:w], AF.Ln, bias=1.0)
            e.drain()
            e.activation(gb[:, o:o + w], pFF2[0:H, 0:w], AF.Relu, bias=DTC)
            e.drain()
            e.sem_inc(sems["tS"], 1)
            inc("tS")
            mark(f"bg_s{si}", "tS")

        @blk.vector
        def _(e, si=si, o=o, w=w):
            wt(e, f"bg_s{si}")
            e.tensor_scalar(bgb[:, o:o + w], bgb[:, o:o + w], 2.0, None, ALU.mult)
            e.tensor_add(gb[:, o:o + w], gb[:, o:o + w], scr[0:H, 0:w])
            e.drain()
            e.tensor_scalar(gb[:, o:o + w], gb[:, o:o + w], NEXPA, None, ALU.mult)
            e.drain()
            e.sem_inc(sems["tV"], 1)
            inc("tV")
            mark(f"bg_v{si}", "tV")

    @blk.vector
    def _(e):
        wt(e, f"bg_v{NSPB - 1}")
        for c in range(NCH):
            o = HALO + c * CT
            e.tensor_tensor_scan(cumb[:, o:o + CT], ONES_T, gb[:, o:o + CT], 0.0, ALU.mult, ALU.add)
        e.drain()
        e.sem_inc(sems["tV"], 1)
        inc("tV")
        mark("cum_done", "tV")

    if dbg:
        @blk.sync
        def _(e):
            wt(e, "cum_done")
            gate(e, "qS")
            e.dma_start(out=DBG["d_b"], in_=bgb[:, :]).then_inc(sems["qS"], 16)
            e.dma_start(out=DBG["d_g"], in_=gb[:, :]).then_inc(sems["qS"], 16)
            e.dma_start(out=DBG["d_cum"], in_=cumb[:, :]).then_inc(sems["qS"], 16)
            inc("qS", 48)
            mark("dbg_bg", "qS")

    for c in range(NCH):
        o = HALO + c * CT

        @blk.tensor
        def _(e, c=c, o=o):
            wt(e, "cum_done")
            if c >= 2:
                wt(e, f"stp_fin{c - 2}")
            pbk = pBC if c % 2 == 0 else pLN
            e.transpose(pbk[0:CT, 0:H], cumb[:, o:o + CT], I128[0:H, 0:H])
            e.transpose(pbk[0:CT, 8:24], bgb[:, o:o + CT], I128[0:16, 0:16]).then_inc(sems["tT"], 1)
            inc("tT")
            mark(f"stp_t1{c}", "tT")

        @blk.vector
        def _(e, c=c, o=o):
            wt(e, f"stp_t1{c}")
            par = c % 2
            pbk = pBC if c % 2 == 0 else pLN
            e.tensor_copy(scr[0:CT, par * 64:par * 64 + 24], pbk[0:CT, 0:24])
            e.drain()
            e.sem_inc(sems["tV"], 1)
            inc("tV")
            mark(f"stp_e1{c}", "tV")

        @blk.tensor
        def _(e, c=c, o=o):
            wt(e, f"stp_e1{c}")
            par = c % 2
            pmk = pMP if c % 2 == 0 else pAP
            e.matmul(pmk[:, 0:H], E_boT, scr[0:CT, par * 64:par * 64 + H], start=True, stop=True)
            e.matmul(pmk[:, 8:16], E_evT, scr[0:CT, par * 64 + 8:par * 64 + 16], start=True, stop=False)
            e.matmul(pmk[:, 8:16], E_odT, scr[0:CT, par * 64 + 16:par * 64 + 24],
                     start=False, stop=True).then_inc(sems["tT"], 1)
            inc("tT")
            mark(f"stp_t2{c}", "tT")

        @blk.vector
        def _(e, c=c, o=o):
            wt(e, f"stp_t2{c}")
            pmk = pMP if c % 2 == 0 else pAP
            e.tensor_copy(stepcs[:, c * 40:c * 40 + 8], pmk[:, 0:H])
            e.tensor_copy(stepcs[:, c * 40 + 16:c * 40 + 24], pmk[:, 8:16])
            e.drain()
            e.sem_inc(sems["tV"], 1)
            inc("tV")
            mark(f"stp_e2{c}", "tV")

        @blk.scalar
        def _(e, c=c, o=o):
            wt(e, f"stp_e2{c}")
            e.activation(stepcs[:, c * 40 + 8:c * 40 + 16], stepcs[:, c * 40:c * 40 + 8], AF.Exp)
            e.drain()
            e.sem_inc(sems["tS"], 1)
            inc("tS")
            mark(f"stp_a{c}", "tS")

        @blk.tensor
        def _(e, c=c, o=o):
            wt(e, f"stp_a{c}")
            pmk = pMP if c % 2 == 0 else pAP
            pbk = pBC if c % 2 == 0 else pLN
            e.matmul(pmk[:, 16:24], sel127[:, :], stepcs[:, c * 40 + 8:c * 40 + 16], start=True, stop=True)
            e.matmul(pmk[:, 24:32], sel127[:, :], stepcs[:, c * 40:c * 40 + 8], start=True, stop=True)
            e.transpose(pbk[0:H, 32:160], stepcs[:, c * 40:c * 40 + 8], I128).then_inc(sems["tT"], 1)
            inc("tT")
            mark(f"stp_t3{c}", "tT")

        @blk.vector
        def _(e, c=c, o=o):
            wt(e, f"stp_t3{c}")
            pmk = pMP if c % 2 == 0 else pAP
            pbk = pBC if c % 2 == 0 else pLN
            e.tensor_copy(accb[:, c * 8:c * 8 + 8], pmk[:, 16:24])
            e.tensor_copy(rowcs[:, c * 128:(c + 1) * 128], pbk[0:H, 32:160])
            e.drain()
            e.tensor_sub(stepcs[:, c * 40 + 32:c * 40 + 40], pmk[:, 24:32],
                         stepcs[:, c * 40:c * 40 + 8])
            e.drain()
            e.tensor_tensor(stepcs[:, c * 40 + 24:c * 40 + 32], stepcs[:, c * 40 + 16:c * 40 + 24],
                            stepcs[:, c * 40 + 8:c * 40 + 16], ALU.mult)
            e.drain()
            e.sem_inc(sems["tV"], 1)
            inc("tV")
            mark(f"stp_v0{c}", "tV")

        @blk.scalar
        def _(e, c=c, o=o):
            wt(e, f"stp_v0{c}")
            e.activation(stepcs[:, c * 40 + 32:c * 40 + 40], stepcs[:, c * 40 + 32:c * 40 + 40], AF.Exp)
            e.drain()
            e.sem_inc(sems["tS"], 1)
            inc("tS")
            mark(f"stp_fin{c}", "tS")

    STP_ALL = f"stp_fin{NCH - 1}"

    barrier("stp")
    prev_ch = [None]

    def A_PART(h):
        @blk.sync
        def _(e, h=h):
            wt(e, kp_sp)
            wt(e, qp_sp)
            if h < 2:
                wt(e, vp_sp)
                wt(e, gp_sp)
                wt(e, f"bg_v{NSPB - 1}")
            if h >= 2:
                wt(e, f"A_done{h - 2}")
            gate(e, "qS")
            e.dma_start(out=khv(h, 0)[:, HALO:TW], in_=k_sp[h * 128:(h + 1) * 128, HALO:TW]).then_inc(sems["qS"], 16)
            e.dma_start(out=khv(h, 1)[:, HALO:TW], in_=k_sp[KEY + h * 128:KEY + (h + 1) * 128, HALO:TW]).then_inc(sems["qS"], 16)
            e.dma_start(out=qhv(h)[:, HALO:TW], in_=q_sp[h * 128:(h + 1) * 128, HALO:TW]).then_inc(sems["qS"], 16)
            inc("qS", 48)
            mark(f"kq_ld{h}", "qS")

        for c in range(NCH):
            o = HALO + c * CT
            idx = h * NCH + c
            par = idx % 2
            pb = pRW if par == 0 else pSQ
            PRV = prev_ch[0]

            @blk.scalar
            def _(e, h=h, c=c, o=o, idx=idx):
                wt(e, vp_sp)
                if idx < 2:
                    wt(e, gp_sp)
                    wt(e, f"bg_v{NSPB - 1}")
                if idx >= 2:
                    wt(e, f"vused{idx - 2}")
                gate(e, "qA")
                e.dma_start(out=vchv(c, 0), in_=v_sp[h * 128:(h + 1) * 128, o:o + CT]).then_inc(sems["qA"], 16)
                e.dma_start(out=vchv(c, 1), in_=v_sp[KEY + h * 128:KEY + (h + 1) * 128, o:o + CT]).then_inc(sems["qA"], 16)
                inc("qA", 32)
                mark(f"v_ld{idx}", "qA")

            @blk.vector
            def _(e, h=h, c=c, o=o, idx=idx, par=par):
                wt(e, f"kq_ld{h}")
                if idx >= 2:
                    wt(e, f"intu{idx - 2}")
                if dbg and 1 <= idx <= 2 and "dbg_ch0" in ev:
                    wt(e, "dbg_ch0")
                if idx < 2:
                    wt(e, STP_ALL)
                kb = kint[:, par * C:par * C + C]
                qb = qint[:, par * C:par * C + C]
                e.memset(qb[:, 0:C:2], 0.0)
                e.tensor_copy(kb[:, 0:C:2], khv(h, 0)[:, o:o + CT])
                e.tensor_copy(kb[:, 1:C:2], khv(h, 1)[:, o:o + CT])
                e.tensor_copy(qb[:, 1:C:2], qhv(h)[:, o:o + CT]).then_inc(sems["tV"], 1)
                inc("tV")
                mark(f"int{idx}", "tV")

            @blk.tensor
            def _(e, idx=idx, par=par, pb=pb):
                wt(e, f"int{idx}")
                if idx >= 2:
                    wt(e, f"pt{idx - 2}")
                e.transpose(pb[:, 0:128], kint[:, par * C:par * C + C], I128)
                e.transpose(pb[:, 128:256], qint[:, par * C:par * C + C], I128).then_inc(sems["tT"], 1)
                inc("tT")
                mark(f"tr1{idx}", "tT")

            @blk.scalar
            def _(e, idx=idx, pb=pb):
                wt(e, f"tr1{idx}")
                e.activation(scr[:, 4 * C:4 * C + 128], pb[:, 0:128], AF.Square, accum_out=nrmc[:, 0:1])
                e.drain()
                e.activation(scr[:, 4 * C:4 * C + 128], pb[:, 128:256], AF.Square, accum_out=nrmc[:, 2:3]).then_inc(sems["tS"], 1)
                inc("tS")
                mark(f"ss{idx}", "tS")

            @blk.scalar
            def _(e, idx=idx):
                wt(e, f"ss{idx}")
                e.activation(nrmc[:, 1:2], nrmc[:, 0:1], AF.Sqrt, bias=EPSN)
                e.activation(nrmc[:, 3:4], nrmc[:, 2:3], AF.Sqrt, bias=EPSN).then_inc(sems["tS"], 1)
                inc("tS")
                mark(f"inv{idx}", "tS")

            @blk.vector
            def _(e, h=h, c=c, idx=idx, par=par, pb=pb):
                wt(e, f"inv{idx}")
                wt(e, STP_ALL)
                e.reciprocal(nrmc[:, 1:2], nrmc[:, 1:2])
                e.reciprocal(nrmc[:, 3:4], nrmc[:, 3:4])
                e.drain()
                e.tensor_scalar(nrmc[:, 3:4], nrmc[:, 3:4], INV_SQRT_DH, None, ALU.mult)
                e.drain()
                e.tensor_scalar(ktm[:, par * DH:par * DH + DH], pb[:, 0:128], nrmc[:, 1:2], None, ALU.mult)
                e.tensor_scalar(scr[:, 0:128], pb[:, 128:256], nrmc[:, 3:4], None, ALU.mult)
                e.drain()
                bcol = stepcs[:, c * 40 + 16 + h:c * 40 + 16 + h + 1]
                acol = stepcs[:, c * 40 + 8 + h:c * 40 + 8 + h + 1]
                e.tensor_scalar(kbtm[:, 0:DH], ktm[:, par * DH:par * DH + DH], bcol, None, ALU.mult)
                e.tensor_scalar(qatm[:, :], scr[:, 0:128], acol, None, ALU.mult).then_inc(sems["tV"], 1)
                inc("tV")
                mark(f"ktmn{idx}", "tV")
                mark(f"intu{idx}", "tV")

            @blk.tensor
            def _(e, idx=idx, par=par, pb=pb):
                wt(e, f"ktmn{idx}")
                e.transpose(pb[:, 256:384], ktm[:, par * DH:par * DH + DH], I128)
                e.transpose(pb[:, 384:512], scr[:, 0:128], I128)
                e.transpose(pb[:, 0:128], kbtm[:, 0:DH], I128)
                e.transpose(pb[:, 128:256], qatm[:, :], I128).then_inc(sems["tT"], 1)
                inc("tT")
                mark(f"tr2{idx}", "tT")

            @blk.vector
            def _(e, idx=idx, par=par, pb=pb):
                wt(e, f"tr2{idx}")
                e.tensor_copy(khf[:, par * C:par * C + C], pb[:, 256:384])
                e.tensor_copy(qhf[:, par * C:par * C + C], pb[:, 384:512])
                e.tensor_copy(kbfm[:, 0:C], pb[:, 0:128])
                e.tensor_copy(qtl[:, :], pb[:, 128:256]).then_inc(sems["tV"], 1)
                inc("tV")
                mark(f"hf{idx}", "tV")

            @blk.vector
            def _(e, c=c, idx=idx):
                wt(e, f"v_ld{idx}")
                wt(e, f"hf{idx}")
                vb = scr[:, C:2 * C]
                e.tensor_copy(vb[:, 0:C:2], vchv(c, 0))
                e.tensor_copy(vb[:, 1:C:2], vchv(c, 1)).then_inc(sems["tV"], 1)
                inc("tV")
                mark(f"vint{idx}", "tV")
                mark(f"vused{idx}", "tV")

            @blk.tensor
            def _(e, idx=idx, pb=pb):
                wt(e, f"vint{idx}")
                e.transpose(pb[:, 0:128], scr[:, C:2 * C], I128).then_inc(sems["tT"], 1)
                inc("tT")
                mark(f"vtr{idx}", "tT")

            @blk.vector
            def _(e, h=h, c=c, idx=idx, par=par, pb=pb):
                wt(e, f"vtr{idx}")
                wt(e, STP_ALL)
                if idx >= 2:
                    wt(e, f"mp{idx - 2}")
                if dbg and 1 <= idx <= 2 and "dbg_ch0" in ev:
                    wt(e, "dbg_ch0")
                bcol = stepcs[:, c * 40 + 16 + h:c * 40 + 16 + h + 1]
                bacol = stepcs[:, c * 40 + 24 + h:c * 40 + 24 + h + 1]
                rb = rhsb[:, par * 256:par * 256 + 256]
                e.tensor_scalar(rb[:, 0:DH], pb[:, 0:128], bcol, None, ALU.mult)
                e.tensor_scalar(rb[:, DH:2 * DH], ktm[:, par * DH:par * DH + DH], bacol, None, ALU.mult).then_inc(sems["tV"], 1)
                inc("tV")
                mark(f"rhs{idx}", "tV")

            @blk.tensor
            def _(e, h=h, c=c, idx=idx, par=par, pb=pb):
                wt(e, f"rhs{idx}")
                kf = khf[:, par * C:par * C + C]
                qf = qhf[:, par * C:par * C + C]
                e.matmul(pb[:, 128:256], kf, kbfm[:, 0:C], start=True, stop=True)
                e.matmul(pb[:, 256:384], kf, qf, start=True, stop=True)
                e.matmul(pb[:, 384:512], selb[0:H, h * 128:h * 128 + 128], rowcs[0:H, c * 128:(c + 1) * 128],
                         start=True, stop=True).then_inc(sems["tT"], 1)
                inc("tT")
                mark(f"raw{idx}", "tT")

            @blk.vector
            def _(e, h=h, c=c, idx=idx, pb=pb):
                wt(e, f"raw{idx}")
                ccol = stepcs[:, c * 40 + h:c * 40 + h + 1]
                e.tensor_scalar(dxp[:, :], pb[:, 384:512], ccol, 0.0, ALU.subtract, ALU.min).then_inc(sems["tV"], 1)
                inc("tV")
                mark(f"darg{idx}", "tV")

            @blk.scalar
            def _(e, idx=idx):
                wt(e, f"darg{idx}")
                e.activation(dxp[:, :], dxp[:, :], AF.Exp).then_inc(sems["tS"], 1)
                inc("tS")
                mark(f"dexp{idx}", "tS")

            @blk.vector
            def _(e, h=h, c=c, idx=idx, pb=pb, PRV=PRV):
                wt(e, f"dexp{idx}")
                if dbg and 1 <= idx <= 2 and "dbg_ch0" in ev:
                    wt(e, "dbg_ch0")
                if PRV is not None:
                    wt(e, f"gap{idx - 1}")
                ccol = stepcs[:, c * 40 + h:c * 40 + h + 1]
                bcol = stepcs[:, c * 40 + 16 + h:c * 40 + 16 + h + 1]
                bp_ = bpwv(idx)
                e.tensor_tensor(bp_[:, 0:C], pb[:, 128:256], dxp[:, :], ALU.mult)
                e.drain()
                e.tensor_tensor(bp_[:, 0:C], bp_[:, 0:C], MSU, ALU.mult)
                g_ = gstv(idx)
                e.tensor_tensor(g_[:, :], pb[:, 256:384], dxp[:, :], ALU.mult)
                e.drain()
                e.tensor_tensor(g_[:, :], g_[:, :], MGU, ALU.mult).then_inc(sems["tV"], 1)
                inc("tV")
                mark(f"bb{idx}", "tV")

            @blk.tensor
            def _(e, idx=idx, pb=pb):
                wt(e, f"bb{idx}")
                e.transpose(pb[:, 0:128], bpwv(idx)[:, 0:C], I128).then_inc(sems["tT"], 1)
                inc("tT")
                mark(f"atr{idx}", "tT")

            @blk.vector
            def _(e, h=h, c=c, idx=idx, pb=pb):
                wt(e, f"atr{idx}")
                e.tensor_copy(apwv(idx)[:, 0:C], pb[:, 0:128]).then_inc(sems["tV"], 1)
                inc("tV")
                mark(f"ab{idx}", "tV")

            pS = pAP if (idx % 2 == 0 or not SOLVE_DUAL) else pFF
            bp_ = bpwv(idx)
            ap_ = apwv(idx)
            for lv in range(1, 7):
                @blk.tensor
                def _(e, idx=idx, lv=lv, pS=pS, bp_=bp_, ap_=ap_):
                    wt(e, f"ab{idx}" if lv == 1 else f"sqr{lv - 1}_{idx}")
                    src_a = ap_[:, ((lv - 1) % 2) * C:((lv - 1) % 2) * C + C]
                    src_b = bp_[:, (lv - 1) * C:lv * C]
                    e.matmul(pS[:, 0:128], src_b, src_a, start=True, stop=True)
                    e.matmul(pS[:, 128:256], src_a, src_b, start=True, stop=True).then_inc(sems["tT"], 1)
                    inc("tT")
                    mark(f"sqt{lv}_{idx}", "tT")

                @blk.vector
                def _(e, idx=idx, lv=lv, pS=pS, bp_=bp_, ap_=ap_):
                    wt(e, f"sqt{lv}_{idx}")
                    if dbg and idx == 0 and lv == 1 and "dbg_bb" in ev:
                        wt(e, "dbg_bb")
                    e.tensor_copy(ap_[:, (lv % 2) * C:(lv % 2) * C + C], pS[:, 0:128])
                    e.tensor_copy(bp_[:, lv * C:(lv + 1) * C], pS[:, 128:256]).then_inc(sems["tV"], 1)
                    inc("tV")
                    mark(f"sqr{lv}_{idx}", "tV")

            for ai, lv in enumerate([6, 5, 4, 3, 2, 1, 0]):
                @blk.tensor
                def _(e, idx=idx, par=par, ai=ai, lv=lv, PRV=PRV, pS=pS, bp_=bp_):
                    wt(e, f"sqr6_{idx}" if ai == 0 else f"app{ai - 1}_{idx}")
                    if ai == 0 and idx >= SDEPTH:
                        wt(e, f"app6_{idx - SDEPTH}")
                    rb = rhsb[:, par * 256:par * 256 + 256]
                    e.matmul(pS[:, 256:512], bp_[:, lv * C:(lv + 1) * C], rb, start=True, stop=True).then_inc(sems["tT"], 1)
                    inc("tT")
                    mark(f"apt{ai}_{idx}", "tT")

                @blk.vector
                def _(e, idx=idx, par=par, ai=ai, lv=lv, pS=pS):
                    wt(e, f"apt{ai}_{idx}")
                    rb = rhsb[:, par * 256:par * 256 + 256]
                    if lv == 0:
                        e.tensor_sub(rb, rb, pS[:, 256:512])
                    else:
                        e.tensor_add(rb, rb, pS[:, 256:512])
                    e.drain()
                    e.sem_inc(sems["tV"], 1)
                    inc("tV")
                    mark(f"app{ai}_{idx}", "tV")

            @blk.tensor
            def _(e, idx=idx, par=par, pb=pb):
                wt(e, f"app6_{idx}")
                rb = rhsb[:, par * 256:par * 256 + 256]
                g_ = gstv(idx)
                e.matmul(pb[:, 0:256], g_[:, :], rb, start=True, stop=True)
                e.matmul(pb[:, 256:384], rb[:, DH:2 * DH], g_[:, :], start=True, stop=True).then_inc(sems["tT"], 1)
                inc("tT")
                mark(f"gap{idx}", "tT")

            @blk.vector
            def _(e, h=h, c=c, idx=idx, par=par, pb=pb):
                wt(e, f"gap{idx}")
                e.tensor_copy(art(h, c, 1), pb[:, 0:DH])
                e.tensor_sub(art(h, c, 0), qtl[:, :], pb[:, 256:384])
                e.drain()
                crc = stepcs[:, c * 40 + 32 + h:c * 40 + 32 + h + 1]
                e.tensor_scalar(kbar[:, :], ktm[:, par * DH:par * DH + DH], crc, None, ALU.mult).then_inc(sems["tV"], 1)
                inc("tV")
                mark(f"pt{idx}", "tV")

            @blk.tensor
            def _(e, idx=idx, par=par, PRV=PRV):
                wt(e, f"pt{idx}")
                if PRV is not None:
                    wt(e, f"cmp{idx - 1}")
                rb = rhsb[:, par * 256:par * 256 + 256]
                e.matmul(pMP[:, 0:256], kbar[:, :], rb, start=True, stop=True)
                e.matmul(pMP[:, 256:384], rb[:, DH:2 * DH], kbar[:, :], start=True, stop=True).then_inc(sems["tT"], 1)
                inc("tT")
                mark(f"mp{idx}", "tT")

            @blk.vector
            def _(e, h=h, c=c, idx=idx):
                wt(e, f"mp{idx}")
                if h >= 2 and c == 0:
                    wt(e, f"B_done{h - 2}")
                acol = accb[:, c * 8 + h:c * 8 + h + 1]
                e.tensor_copy(art(h, c, 3), pMP[:, 0:DH])
                e.scalar_tensor_tensor(art(h, c, 2), I128, acol, pMP[:, 256:384], ALU.mult, ALU.subtract).then_inc(sems["tV"], 1)
                inc("tV")
                mark(f"mn{idx}", "tV")

            if c == 0:
                @blk.vector
                def _(e, h=h, c=c, idx=idx):
                    wt(e, f"mn{idx}")
                    acol = accb[:, c * 8 + h:c * 8 + h + 1]
                    e.scalar_tensor_tensor(comp[:, 0:DH], I128, acol, pMP[:, 128:256], ALU.mult, ALU.subtract)
                    e.tensor_copy(comp[:, DH:2 * DH], art(h, c, 2))
                    e.tensor_copy(comp[:, 2 * DH:3 * DH], art(h, c, 3)).then_inc(sems["tV"], 1)
                    inc("tV")
                    mark(f"cmp{idx}", "tV")
            else:
                @blk.vector
                def _(e, h=h, c=c, idx=idx):
                    wt(e, f"mn{idx}")
                    acol = accb[:, c * 8 + h:c * 8 + h + 1]
                    e.scalar_tensor_tensor(scr[:, 3 * C:3 * C + DH], I128, acol, pMP[:, 128:256], ALU.mult, ALU.subtract).then_inc(sems["tV"], 1)
                    inc("tV")
                    mark(f"mcu{idx}", "tV")

                @blk.tensor
                def _(e, h=h, c=c, idx=idx):
                    wt(e, f"mcu{idx}")
                    e.matmul(pMP[:, 0:128], art(h, c, 2), comp[:, 0:DH], start=True, stop=True)
                    e.matmul(pMP[:, 128:256], comp[:, 0:DH], scr[:, 3 * C:3 * C + DH], start=True, stop=True)
                    e.matmul(pMP[:, 256:384], art(h, c, 2), comp[:, 2 * DH:3 * DH],
                             start=True, stop=True).then_inc(sems["tT"], 1)
                    inc("tT")
                    mark(f"cmt{idx}", "tT")

                @blk.vector
                def _(e, h=h, c=c, idx=idx):
                    wt(e, f"cmt{idx}")
                    e.tensor_copy(comp[:, 0:DH], pMP[:, 0:128])
                    e.tensor_copy(comp[:, DH:2 * DH], pMP[:, 128:256])
                    e.tensor_add(comp[:, 2 * DH:3 * DH], pMP[:, 256:384], art(h, c, 3)).then_inc(sems["tV"], 1)
                    inc("tV")
                    mark(f"cmp{idx}", "tV")

            if dbg and h == 0 and c == 0:
                @blk.sync
                def _(e, h=h, c=c, idx=idx, par=par):
                    wt(e, f"ab{idx}")
                    gate(e, "qS")
                    e.dma_start(out=DBG["d_bup"], in_=bpw[:, 0:C]).then_inc(sems["qS"], 16)
                    inc("qS", 16)
                    mark("dbg_bb", "qS")

                @blk.sync
                def _(e, h=h, c=c, idx=idx, par=par):
                    wt(e, f"mn{idx}")
                    gate(e, "qS")
                    e.dma_start(out=DBG["d_kint"], in_=kint[:, par * C:par * C + C]).then_inc(sems["qS"], 16)
                    e.dma_start(out=DBG["d_ktm"], in_=ktm[:, par * DH:par * DH + DH]).then_inc(sems["qS"], 16)
                    e.dma_start(out=DBG["d_rhs"], in_=rhsb[:, par * 256:par * 256 + 256]).then_inc(sems["qS"], 16)
                    e.dma_start(out=DBG["d_gst"], in_=gstv(0)[:, :]).then_inc(sems["qS"], 16)
                    e.dma_start(out=DBG["d_kbf"], in_=kbfm[:, 0:C]).then_inc(sems["qS"], 16)
                    e.dma_start(out=DBG["d_kbt"], in_=kbtm[:, 0:DH]).then_inc(sems["qS"], 16)
                    e.dma_start(out=DBG["d_art"], in_=big0[:, 0 + c * 4 * DH:0 + (c + 1) * 4 * DH]).then_inc(sems["qS"], 16)
                    inc("qS", 112)
                    mark("dbg_ch0", "qS")

            prev_ch[0] = (h, c)

        @blk.vector
        def _(e, h=h):
            wt(e, f"cmp{h * NCH + NCH - 1}")
            if h >= 1:
                wt(e, f"cci{h - 1}")
            e.tensor_copy(ccsb[:, 0:DH], comp[:, DH:2 * DH])
            e.tensor_copy(ccsb[:, DH:2 * DH], comp[:, 2 * DH:3 * DH]).then_inc(sems["tV"], 1)
            inc("tV")
            mark(f"ccs{h}", "tV")
            mark(f"A_done{h}", "tV")

        @blk.scalar
        def _(e, h=h):
            wt(e, f"ccs{h}")
            gate(e, "qA")
            e.dma_start(out=cc_in[h], in_=ccsb[:, :]).then_inc(sems["qA"], 16)
            inc("qA", 16)
            mark(f"cci{h}", "qA")

        @blk.gpsimd
        def _(e, h=h):
            wt(e, f"cci{h}")
            e.collective_compute("AllGather", ALU.bypass, ins=[cc_in[h]], outs=[cc_out[h]],
                                 replica_groups=[[i for i in range(n_cores)]]).then_inc(sems["cc"], 1)
            inc("cc", 1)
            mark(f"ccd{h}", "cc")

    def B_PART(h):
        @blk.vector
        def _(e, h=h):
            wt(e, f"ccd{h}")
            if h >= 1:
                wt(e, f"B_done{h - 1}")
            e.memset(prefx[:, :], 0.0)
            e.drain()
            e.sem_inc(sems["tV"], 1)
            inc("tV")
            mark(f"pfx0{h}", "tV")

        for j in range(n_cores - 1):
            @blk.scalar
            def _(e, h=h, j=j):
                wt(e, f"ccd{h}")
                if j >= 2:
                    wt(e, f"pfa{h}_{j - 2}")
                gate(e, "qA")
                e.dma_start(out=gsb[:, (j % 2) * 256:(j % 2) * 256 + 256], in_=cc_out[h][j]).then_inc(sems["qA"], 16)
                inc("qA", 16)
                mark(f"gld{h}_{j}", "qA")

            @blk.vector
            def _(e, h=h, j=j):
                wt(e, f"gld{h}_{j}")
                wt(e, f"pfx0{h}")
                if j > 0:
                    wt(e, f"pfa{h}_{j - 1}")
                g = gsb[:, (j % 2) * 256:(j % 2) * 256 + 256]
                sc = selbc[:, j:j + 1]
                e.tensor_scalar(bscr[:, 0:DH], g[:, 0:DH], sc, None, ALU.mult)
                e.tensor_scalar(bscr[:, 2 * DH:3 * DH], g[:, DH:2 * DH], sc, None, ALU.mult)
                e.tensor_scalar(bscr[:, DH:2 * DH], I128, sc, -1.0, ALU.mult, ALU.mult)
                e.drain()
                e.tensor_add(bscr[:, DH:2 * DH], bscr[:, DH:2 * DH], I128)
                e.drain()
                e.tensor_add(bscr[:, 0:DH], bscr[:, 0:DH], bscr[:, DH:2 * DH]).then_inc(sems["tV"], 1)
                inc("tV")
                mark(f"pfm{h}_{j}", "tV")

            @blk.tensor
            def _(e, h=h, j=j):
                wt(e, f"pfm{h}_{j}")
                e.matmul(pBC[:, 256:384], bscr[:, 0:DH], prefx[:, :], start=True, stop=True).then_inc(sems["tT"], 1)
                inc("tT")
                mark(f"pft{h}_{j}", "tT")

            @blk.vector
            def _(e, h=h, j=j):
                wt(e, f"pft{h}_{j}")
                e.tensor_add(prefx[:, :], pBC[:, 256:384], bscr[:, 2 * DH:3 * DH]).then_inc(sems["tV"], 1)
                inc("tV")
                mark(f"pfa{h}_{j}", "tV")

        PFX = f"pfa{h}_{n_cores - 2}" if n_cores > 1 else f"pfx0{h}"

        if dbg and h == 0:
            @blk.sync
            def _(e, h=h):
                wt(e, PFX)
                gate(e, "qS")
                e.dma_start(out=DBG["d_sin"], in_=prefx[:, :]).then_inc(sems["qS"], 16)
                inc("qS", 16)
                mark("dbg_sin", "qS")

        @blk.vector
        def _(e, h=h):
            wt(e, PFX)
            e.tensor_copy(stt[:, :], prefx[:, :]).then_inc(sems["tV"], 1)
            inc("tV")
            mark(f"stinit{h}", "tV")

        for c in range(NCH):
            bp = c % 2
            SP = f"stinit{h}" if c == 0 else f"stch{h}_{c - 1}"

            @blk.scalar
            def _(e, h=h, c=c, bp=bp):
                wt(e, gp_sp)
                if c >= 2:
                    wt(e, f"oev{h}_{c - 2}")
                gate(e, "qA")
                e.dma_start(out=ggch[:, bp * CT:bp * CT + CT],
                            in_=gg_sp[h * 128:(h + 1) * 128, c * CT:(c + 1) * CT]).then_inc(sems["qA"], 16)
                inc("qA", 16)
                mark(f"gg_ld{h}_{c}", "qA")

            @blk.tensor
            def _(e, h=h, c=c, bp=bp, SP=SP):
                wt(e, SP)
                wt(e, f"mn{h * NCH + c}")
                if c >= 2:
                    wt(e, f"oev{h}_{c - 2}")
                e.matmul(pBC[:, 0:128], art(h, c, 0), stt[:, :], start=True, stop=True)
                e.matmul(pBC[:, 128:256], art(h, c, 2), stt[:, :],
                         start=True, stop=True).then_inc(sems["tT"], 1)
                inc("tT")
                mark(f"bst{h}_{c}", "tT")

            @blk.vector
            def _(e, h=h, c=c, bp=bp):
                wt(e, f"bst{h}_{c}")
                if dbg and h == 0 and c == 2 and "dbg_och" in ev:
                    wt(e, "dbg_och")
                e.tensor_add(och[:, bp * DH:bp * DH + DH], pBC[:, 0:128], art(h, c, 1))
                e.tensor_add(stt[:, :], pBC[:, 128:256], art(h, c, 3)).then_inc(sems["tV"], 1)
                inc("tV")
                mark(f"stch{h}_{c}", "tV")

            @blk.scalar
            def _(e, h=h, c=c, bp=bp):
                wt(e, f"stch{h}_{c}")
                wt(e, f"gg_ld{h}_{c}")
                ob = och[:, bp * DH:bp * DH + DH]
                e.activation(scr[:, 5 * C:5 * C + DH], ob, AF.Square, accum_out=nrmc[:, 4:5])
                e.drain()
                e.activation(nrmc[:, 5:6], nrmc[:, 4:5], AF.Sqrt, bias=EPSC, scale=INVDHC).then_inc(sems["tS"], 1)
                inc("tS")
                mark(f"sgg{h}_{c}", "tS")

            @blk.vector
            def _(e, h=h, c=c, bp=bp):
                wt(e, f"sgg{h}_{c}")
                ob = och[:, bp * DH:bp * DH + DH]
                e.reciprocal(nrmc[:, 5:6], nrmc[:, 5:6])
                e.drain()
                e.tensor_scalar(ob, ob, nrmc[:, 5:6], None, ALU.mult).then_inc(sems["tV"], 1)
                inc("tV")
                mark(f"orms{h}_{c}", "tV")

            @blk.tensor
            def _(e, h=h, c=c, bp=bp):
                wt(e, f"orms{h}_{c}")
                e.transpose(pLN[:, 0:128], och[:, bp * DH:bp * DH + DH], I128).then_inc(sems["tT"], 1)
                inc("tT")
                mark(f"otr{h}_{c}", "tT")

            @blk.vector
            def _(e, h=h, c=c, bp=bp):
                wt(e, f"otr{h}_{c}")
                o = c * CT
                e.tensor_tensor(scr[:, 5 * C:5 * C + CT], pLN[:, 1:128:2],
                                ggch[:, bp * CT:bp * CT + CT], ALU.mult)
                e.drain()
                e.tensor_scalar(ofmv(h)[:, o:o + CT], scr[:, 5 * C:5 * C + CT], onrm[:, 0:1], None, ALU.mult).then_inc(sems["tV"], 1)
                inc("tV")
                mark(f"oev{h}_{c}", "tV")

            if dbg and h == 0 and c == 0:
                @blk.sync
                def _(e, h=h, c=c, bp=bp):
                    wt(e, f"orms{h}_{c}")
                    gate(e, "qS")
                    e.dma_start(out=DBG["d_och"], in_=och[:, bp * DH:bp * DH + DH]).then_inc(sems["qS"], 16)
                    inc("qS", 16)
                    mark("dbg_och", "qS")

        @blk.vector
        def _(e, h=h):
            wt(e, f"oev{h}_{NCH - 1}")
            e.tensor_copy(bscr[0:1, 0:1], bscr[0:1, 0:1]).then_inc(sems["tV"], 1)
            inc("tV")
            mark(f"B_done{h}", "tV")

    for h in range(H):
        A_PART(h)
        barrier(f"A{h}")
        if h >= 1:
            B_PART(h - 1)
    B_PART(H - 1)

    if dbg:
        @blk.sync
        def _(e):
            wt(e, f"B_done{H - 1}")
            gate(e, "qS")
            for f in range(NQ):
                e.dma_start(out=DBG["d_ofm"][f * 128:(f + 1) * 128, :],
                            in_=big1[:, OFM_OFF + f * tpc:OFM_OFF + (f + 1) * tpc]).then_inc(sems["qS"], 16)
            inc("qS", 16 * NQ)
            mark("dbg_ofm", "qS")

    X2v = big0
    barrier("scan")

    @blk.sync
    def _(e):
        wt(e, f"B_done{H - 1}")
        wt(e, "x1_spilled")
        gate(e, "qS")
        for f in range(NT):
            e.dma_start(out=X2v[:, f * tpc:(f + 1) * tpc], in_=x1_sp[f * 128:(f + 1) * 128, HALO:TW]).then_inc(sems["qS"], 16)
        inc("qS", 16 * NT)
        mark("x1_reld", "qS")

    nspo = len(spans(tpc, 512))
    for fo in range(NT):
        @blk.sync
        def _(e, fo=fo):
            wt(e, "x1_reld")
            if fo >= 2:
                wt(e, f"wo_m{fo - 2}_{nspo - 1}")
            qn = "qS" if fo % 2 == 0 else "qG"
            gate(e, qn)
            for f in range(NQ):
                e.dma_start(out=wst[:, (fo % 2) * 1280 + f * 128:(fo % 2) * 1280 + (f + 1) * 128],
                            in_=wo_t[f * 128:(f + 1) * 128, fo * 128:(fo + 1) * 128]).then_inc(sems[qn], 16)
            inc(qn, 16 * NQ)
            mark(f"wo_ld{fo}", qn)

        for si, (o, w) in enumerate(spans(tpc, 512)):
            @blk.tensor
            def _(e, fo=fo, si=si, o=o, w=w):
                wt(e, f"wo_ld{fo}")
                if fo >= 2 or si > 0:
                    pf, ps_ = (fo, si - 1) if si > 0 else (fo - 2, nspo - 1)
                    wt(e, f"wo_ev{pf}_{ps_}")
                pb = pFF if fo % 2 == 0 else pFF2
                for f in range(NQ):
                    mm = e.matmul(pb[:, 0:w], wst[:, (fo % 2) * 1280 + f * 128:(fo % 2) * 1280 + (f + 1) * 128],
                                  big1[:, OFM_OFF + f * tpc + o:OFM_OFF + f * tpc + o + w],
                                  start=(f == 0), stop=(f == NQ - 1))
                mm.then_inc(sems["tT"], 1)
                inc("tT")
                mark(f"wo_m{fo}_{si}", "tT")

            @blk.vector
            def _(e, fo=fo, si=si, o=o, w=w):
                wt(e, f"wo_m{fo}_{si}")
                pb = pFF if fo % 2 == 0 else pFF2
                e.tensor_add(X2v[:, fo * tpc + o:fo * tpc + o + w], X2v[:, fo * tpc + o:fo * tpc + o + w], pb[:, 0:w])
                e.drain()
                e.sem_inc(sems["tV"], 1)
                inc("tV")
                mark(f"wo_ev{fo}_{si}", "tV")

    WO_ALL = f"wo_ev{NT - 1}_{nspo - 1}"

    if dbg:
        @blk.sync
        def _(e):
            wt(e, WO_ALL)
            gate(e, "qS")
            for f in range(NT):
                e.dma_start(out=DBG["d_x2"][f * 128:(f + 1) * 128, :], in_=X2v[:, f * tpc:(f + 1) * tpc]).then_inc(sems["qS"], 16)
            inc("qS", 16 * NT)
            mark("dbg_x2", "qS")

    @blk.scalar
    def _(e):
        wt(e, WO_ALL)
        gate(e, "qA")
        for f in range(NT):
            e.dma_start(out=smw[:, f * SM:(f + 1) * SM], in_=w_small_t[f * 128:(f + 1) * 128, :]).then_inc(sems["qA"], 16)
        inc("qA", 16 * NT)
        mark("smw2_loaded", "qA")

    barrier("wo")
    ln2_done = emit_ln(X2v, big1, tpc, [WO_ALL])
    ffn2_done = emit_ffn(big1, X2v, tpc, [ln2_done, "smw2_loaded"], "fB")

    @blk.sync
    def _(e):
        wt(e, ffn2_done)
        gate(e, "qS")
        for f in range(NT):
            e.dma_start(out=out_fm[f * 128:(f + 1) * 128, :], in_=X2v[:, f * tpc:(f + 1) * tpc]).then_inc(sems["qS"], 16)
        inc("qS", 16 * NT)
        mark("out_st", "qS")
        wt(e, "out_st")
        if dbg:
            for nm in list(ev.keys()):
                if nm.startswith("dbg_"):
                    wt(e, nm)

    for g in reversed(guards):
        try:
            g.__exit__(None, None, None)
        except Exception:
            pass
    return nc, DBG



def _layout(inputs, n_cores, tpc):
    TW = tpc + HALO
    i = inputs
    x = np.asarray(i["L0_attn_x"], dtype=np.float32)[0]
    xT = np.ascontiguousarray(x.T)
    shared = {}
    shared["wq_t"] = np.ascontiguousarray(np.asarray(i["L0_attn_Wq"], np.float32).T)
    shared["wk_t"] = np.ascontiguousarray(np.asarray(i["L0_attn_Wk"], np.float32).T)
    shared["wv_t"] = np.ascontiguousarray(np.asarray(i["L0_attn_Wv"], np.float32).T)
    shared["wbwa_t"] = np.ascontiguousarray(
        np.concatenate([np.asarray(i["L0_attn_Wb"], np.float32).T,
                        np.asarray(i["L0_attn_Wa"], np.float32).T], axis=1))
    shared["wg_t"] = np.ascontiguousarray(np.asarray(i["L0_attn_Wg"], np.float32).T)
    shared["wo_t"] = np.ascontiguousarray(np.asarray(i["L0_attn_Wo"], np.float32).T)
    shared["a_log"] = np.asarray(i["L0_attn_A_log"], np.float32).reshape(1, H)
    shared["dt_b"] = np.asarray(i["L0_attn_dt_bias"], np.float32).reshape(1, H)
    shared["conv_q"] = np.ascontiguousarray(np.asarray(i["L0_attn_conv_q"], np.float32))
    shared["conv_k"] = np.ascontiguousarray(np.asarray(i["L0_attn_conv_k"], np.float32))
    shared["conv_v"] = np.ascontiguousarray(np.asarray(i["L0_attn_conv_v"], np.float32))
    shared["onw"] = np.asarray(i["L0_attn_o_norm_w"], np.float32).reshape(1, DH)
    shared["w_base_t"] = np.ascontiguousarray(np.asarray(i["L0_ffn_W_base"], np.float32))
    shared["b_base_c"] = np.ascontiguousarray(np.asarray(i["L0_ffn_b_base"], np.float32).reshape(NT, 128).T)
    shared["w_small_t"] = np.ascontiguousarray(
        np.concatenate([np.asarray(i["L0_ffn_A_s"], np.float32),
                        np.asarray(i["L0_ffn_W_router"], np.float32),
                        np.asarray(i["L0_ffn_A_r"], np.float32).transpose(1, 0, 2).reshape(D, NE * R)], axis=1))
    shared["b_rout"] = np.asarray(i["L0_ffn_b_router"], np.float32).reshape(1, NE)
    shared["b_s_w"] = np.ascontiguousarray(np.asarray(i["L0_ffn_B_s"], np.float32))
    shared["b_r_w"] = np.ascontiguousarray(np.asarray(i["L0_ffn_B_r"], np.float32).reshape(NE * R, D))
    ins = []
    for c in range(n_cores):
        lo = c * tpc - HALO
        sl = np.zeros((D, TW), dtype=np.float32)
        src_lo = max(0, lo)
        sl[:, (src_lo - lo):] = xT[:, src_lo:c * tpc + tpc]
        m = dict(shared)
        m["x_fm"] = np.ascontiguousarray(sl)
        m["csel"] = (np.arange(8) < c).astype(np.float32).reshape(1, 8)
        m["hmask"] = np.array([[0.0 if c == 0 else 1.0]], dtype=np.float32)
        ins.append(m)
    return ins


def kernel(**inputs):
    n_cores, tpc = 8, 1024
    nc, _ = build_program(n_cores, tpc, dbg=0)
    in_maps = _layout(inputs, n_cores, tpc)
    res = run_bass_kernel_spmd(nc, in_maps, core_ids=list(range(n_cores)))
    parts = [np.asarray(r["out_fm"]).T for r in res.results]
    out = np.concatenate(parts, axis=0)[None, :, :]
    return np.ascontiguousarray(out.astype(np.float32))
